# version 01
# baseline (speedup 1.0000x reference)
import numpy as np
from contextlib import ExitStack
import concourse.bass as bass
import concourse.mybir as mybir
from concourse.bass_utils import run_bass_kernel_spmd

F32 = mybir.dt.float32
BF16 = mybir.dt.bfloat16
I16 = mybir.dt.int16
I32 = mybir.dt.int32
AF = mybir.ActivationFunctionType
ALU = mybir.AluOpType
AX = mybir.AxisListType

D = 1024
NCORES = 8


class Sem:
    def __init__(self, h, name):
        self.h = h
        self.name = name
        self.count = 0


class Sched:
    ENGINES = ("sync", "vector", "tensor", "scalar", "gpsimd")

    def __init__(self, nc, es):
        self.nc = nc
        self.es = es
        self.q = {e: [] for e in self.ENGINES}

    def sem(self, name):
        h = self.es.enter_context(self.nc.semaphore(name))
        return Sem(h, name)

    def emit(self, eng, fn, wait=(), inc=None, incn=1):
        tok = None
        if inc is not None:
            inc.count += incn
            tok = inc.count
        self.q[eng].append((fn, tuple(wait), inc, incn))
        return tok

    def sy(self, fn, **kw): return self.emit("sync", fn, **kw)
    def v(self, fn, **kw): return self.emit("vector", fn, **kw)
    def t(self, fn, **kw): return self.emit("tensor", fn, **kw)
    def s(self, fn, **kw): return self.emit("scalar", fn, **kw)
    def g(self, fn, **kw): return self.emit("gpsimd", fn, **kw)

    def waitonly(self, eng, wait):
        self.q[eng].append((None, tuple(wait), None, 0))

    def replay(self, eng_name, eng):
        for fn, wait, inc, incn in self.q[eng_name]:
            for (sem, val) in wait:
                eng.wait_ge(sem.h, val)
            if fn is not None:
                instr = fn(eng)
                if inc is not None:
                    instr.then_inc(inc.h, incn)

    def build_blocks(self, block):
        sched = self

        @block.sync
        def _(sync):
            sched.replay("sync", sync)

        @block.vector
        def _(vector):
            sched.replay("vector", vector)

        @block.tensor
        def _(tensor):
            sched.replay("tensor", tensor)

        @block.scalar
        def _(scalar):
            sched.replay("scalar", scalar)

        @block.gpsimd
        def _(gpsimd):
            sched.replay("gpsimd", gpsimd)


class Arena:

    def __init__(self, nc, es, nbytes):
        self.t_bf = _sb(nc, es, "arena", [128, nbytes // 2], BF16)
        self.t_f32 = self.t_bf[:].bitcast(F32)
        self.n = nbytes
        self.free_list = [(0, nbytes)]

    def alloc(self, nbytes, align=4):
        nbytes = -(-nbytes // align) * align
        for i, (off, sz) in enumerate(self.free_list):
            aoff = -(-off // align) * align
            if aoff + nbytes <= off + sz:
                pre = (off, aoff - off)
                post = (aoff + nbytes, off + sz - aoff - nbytes)
                repl = [p for p in (pre, post) if p[1] > 0]
                self.free_list[i:i + 1] = repl
                return aoff
        raise MemoryError(f"arena: cannot alloc {nbytes}B; free={self.free_list}")

    def free(self, off, nbytes, align=4):
        nbytes = -(-nbytes // align) * align
        self.free_list.append((off, nbytes))
        self.free_list.sort()
        merged = []
        for o, s in self.free_list:
            if merged and merged[-1][0] + merged[-1][1] == o:
                merged[-1] = (merged[-1][0], merged[-1][1] + s)
            else:
                merged.append((o, s))
        self.free_list = [tuple(m) for m in merged]

    def f32(self, np_, nfree):
        off = self.alloc(np_ * 0 + nfree * 4)
        return self.t_f32[:, off // 4: off // 4 + nfree], (off, nfree * 4)

    def bf16(self, np_, nfree):
        off = self.alloc(nfree * 2)
        return self.t_bf[:, off // 2: off // 2 + nfree], (off, nfree * 2)


class Scope:

    def __init__(self, arena):
        self.a = arena
        self.allocs = []

    def f32(self, nfree, rows=128):
        v, rec = self.a.f32(rows, nfree)
        self.allocs.append(rec)
        return v[0:rows, :]

    def bf16(self, nfree, rows=128):
        v, rec = self.a.bf16(rows, nfree)
        self.allocs.append(rec)
        return v[0:rows, :]

    def close(self):
        for off, sz in self.allocs:
            self.a.free(off, sz)
        self.allocs = []



def _blk(wT, ob=128):
    I, O = wT.shape
    return np.ascontiguousarray(wT.reshape(I, O // ob, ob).transpose(1, 0, 2))


def _colchunks(v):
    return np.ascontiguousarray(v.reshape(-1, 128).T)


def host_prep(inputs):
    f32 = np.float32
    a = {}
    for nm in ("Wq", "Wk", "Wv", "Wo"):
        a[f"l0_{nm}T"] = np.ascontiguousarray(inputs[f"L0_attn_{nm}"].T.astype(f32))
    a["l0_bq_cols"] = _colchunks(inputs["L0_attn_bq"].astype(f32))
    a["l0_bkv_row"] = np.concatenate(
        [inputs["L0_attn_bk"], inputs["L0_attn_bv"]]).astype(f32).reshape(1, 2 * D)
    a["l0_bo_row"] = inputs["L0_attn_bo"].astype(f32).reshape(1, D)
    a["l0_w1T"] = _blk(np.ascontiguousarray(inputs["L0_ffn_w1_w"].T.astype(f32)))
    a["l0_w2T"] = _blk(np.ascontiguousarray(inputs["L0_ffn_w2_w"].T.astype(f32)))
    a["l0_h1T"] = _blk(np.ascontiguousarray(inputs["L0_ffn_h1_w"].T.astype(f32)))
    a["l0_h2T"] = _blk(np.ascontiguousarray(inputs["L0_ffn_h2_w"].T.astype(f32)))
    a["l0_w3T"] = _blk(np.ascontiguousarray(inputs["L0_ffn_w3_w"].T.astype(f32)))
    for nm in ("w1_s", "w2_s", "h1_s", "h2_s"):
        a[f"l0_{nm}_cols"] = _colchunks(inputs[f"L0_ffn_{nm}"].astype(f32))
    a["l0_w3_s_cols"] = _colchunks(inputs["L0_ffn_w3_s"].astype(f32))
    a["l1_qT"] = np.ascontiguousarray(inputs["L1_attn_q_w"].T.astype(f32))
    a["l1_kT"] = np.ascontiguousarray(inputs["L1_attn_k_w"].T.astype(f32))
    a["l1_vT"] = np.ascontiguousarray(inputs["L1_attn_v_w"].T.astype(f32))
    a["l1_gT"] = np.ascontiguousarray(inputs["L1_attn_g_w"].T.astype(f32))
    aT32 = np.zeros((1024, 32), f32)
    aT32[:, 0:6] = inputs["L1_attn_a_w"].T.astype(f32)
    bT32 = np.zeros((1024, 32), f32)
    bT32[:, 0:6] = inputs["L1_attn_b_w"].T.astype(f32)
    a["l1_aT32"] = np.ascontiguousarray(aT32)
    a["l1_bT32"] = np.ascontiguousarray(bT32)
    a["l1_oT"] = np.ascontiguousarray(inputs["L1_attn_o_w"].T.astype(f32))
    a["l1_convq_cols"] = np.ascontiguousarray(
        inputs["L1_attn_conv_q_w"].astype(f32).reshape(6, 128, 4).transpose(1, 0, 2).reshape(128, 24))
    a["l1_convk_cols"] = np.ascontiguousarray(
        inputs["L1_attn_conv_k_w"].astype(f32).reshape(6, 128, 4).transpose(1, 0, 2).reshape(128, 24))
    a["l1_convv_cols"] = np.ascontiguousarray(
        inputs["L1_attn_conv_v_w"].astype(f32).reshape(12, 128, 4).transpose(1, 0, 2).reshape(128, 48))
    a["l1_Alog_row"] = inputs["L1_attn_A_log"].astype(f32).reshape(1, 6)
    a["l1_dtb_row"] = inputs["L1_attn_dt_bias"].astype(f32).reshape(1, 6)
    a["l1_nw_cols"] = _colchunks(inputs["L1_attn_norm_weight"].astype(f32))
    a["l1_wgate"] = np.ascontiguousarray(inputs["L1_ffn_w_gate"].astype(f32))
    a["l1_wup"] = np.ascontiguousarray(inputs["L1_ffn_w_up"].astype(f32))
    a["l1_wdown"] = np.ascontiguousarray(inputs["L1_ffn_w_down"].astype(f32))
    a["identity"] = np.eye(128, dtype=f32)
    a["ones_row"] = np.ones((1, 512), f32)
    a["tri_incl"] = np.ascontiguousarray(np.triu(np.ones((128, 128), f32)))
    a["mask_strict_low"] = np.ascontiguousarray(np.tril(np.ones((128, 128), f32), k=-1))
    bsel = np.zeros((16, 8 * 128), f32)
    for c in range(8):
        bsel[2 * c, c * 128:c * 128 + 64] = 1.0
        bsel[2 * c + 1, c * 128 + 64:c * 128 + 128] = 1.0
    a["bsel"] = np.ascontiguousarray(bsel)
    consts = [1e-5, 1e-6, 1.0 / D, -1.0, float(np.log(128.0 ** -0.5))]
    a["constbank"] = np.ascontiguousarray(
        np.tile(np.array(consts, f32)[None, :], (128, 1)))
    return a



class St:
    pass


def _sb(nc, es, name, shape, dt=F32):
    return es.enter_context(nc.sbuf_tensor(name, shape, dt))


ARENA_BYTES = 134 * 1024


def build_program(host_arrays, phases, nt=32, stream_in="x", stream_out=None,
                  final_out=False):
    nc = bass.Bass("TRN2", target_bir_lowering=False)
    T = nt * 128

    dram = {}
    if stream_in == "x":
        dram["x"] = nc.dram_tensor("x", [T, D], F32, kind="ExternalInput").ap()
    else:
        dram["stream_in"] = nc.dram_tensor("stream_in", [T, D], BF16,
                                           kind="ExternalInput").ap()
    if stream_out == "stream":
        dram["stream_out"] = nc.dram_tensor("stream_out", [T, D], BF16,
                                            kind="ExternalOutput").ap()
    if final_out:
        dram["out"] = nc.dram_tensor("out", [T, D], F32, kind="ExternalOutput").ap()
    import os as _os
    if _os.environ.get("L1A_DBG") or _os.environ.get("L1F_DBG"):
        dram["dbg"] = nc.dram_tensor("dbg", [128, 8192], F32, kind="ExternalOutput").ap()

    need = {"identity", "constbank"}
    if "l0a" in phases:
        need |= {"l0_WqT", "l0_WkT", "l0_WvT", "l0_WoT", "l0_bq_cols",
                 "l0_bkv_row", "l0_bo_row", "bsel"}
    if "l0f" in phases:
        need |= {"l0_w1T", "l0_w2T", "l0_h1T", "l0_h2T", "l0_w3T",
                 "l0_w1_s_cols", "l0_w2_s_cols", "l0_h1_s_cols", "l0_h2_s_cols",
                 "l0_w3_s_cols"}
    if "l1a" in phases:
        need |= {"l1_qT", "l1_kT", "l1_vT", "l1_gT", "l1_aT32", "l1_bT32", "l1_oT",
                 "l1_convq_cols", "l1_convk_cols", "l1_convv_cols",
                 "l1_Alog_row", "l1_dtb_row", "l1_nw_cols", "tri_incl",
                 "mask_strict_low"}
    if "l1f" in phases:
        need |= {"l1_wgate", "l1_wup", "l1_wdown"}
    for nm in sorted(need):
        arr = host_arrays[nm]
        dram[nm] = nc.dram_tensor(nm, list(arr.shape), F32, kind="ExternalInput").ap()

    with ExitStack() as es:
        S = Sched(nc, es)
        st = St()
        st.nt = nt
        st.dram = dram
        st.need = need

        st.stream = _sb(nc, es, "stream", [128, nt * D], BF16)
        st.ldbuf = _sb(nc, es, "ldbuf", [128, 2 * D], F32)
        st.ident = _sb(nc, es, "sb_ident", [128, 128], BF16)
        st.ones_row = _sb(nc, es, "sb_ones", [1, 512], BF16)
        st.cb = _sb(nc, es, "sb_cb", [128, 5], F32)
        ident_f32 = _sb(nc, es, "ident_f32", [128, 128], F32)
        st.ident_f32 = ident_f32
        st.c_eps_ln = st.cb[:, 0:1]
        st.c_eps6 = st.cb[:, 1:2]
        st.c_invD = st.cb[:, 2:3]
        st.c_neg1 = st.cb[:, 3:4]
        st.c_qln = st.cb[:, 4:5]
        st.arena = Arena(nc, es, ARENA_BYTES)

        st.banks = [es.enter_context(nc.psum_tensor(f"bank{i}", [128, 512], F32))
                    for i in range(7)]
        st.tp_bank = es.enter_context(nc.psum_tensor("tp_bank", [128, 256], BF16))

        st.LD = S.sem("LD")
        st.SV = S.sem("SV")
        st.SS = S.sem("SS")
        st.TN = S.sem("TN")
        st.GP = S.sem("GP")
        st.ST = S.sem("ST")
        st.tile_tok = {}

        S.sy(lambda e: e.dma_start(out=ident_f32[:], in_=dram["identity"]),
             inc=st.LD, incn=16)
        S.sy(lambda e: e.dma_start(out=st.cb[:], in_=dram["constbank"]),
             inc=st.LD, incn=16)
        consts_tok = st.LD.count

        if stream_in == "x":
            LX = [S.sem("LX0"), S.sem("LX1")]
            xf32 = st.ldbuf
            for t in range(nt):
                sl = xf32[:, (t % 2) * D:(t % 2) * D + D]
                if t >= 2:
                    S.waitonly("sync", [(st.SV, t - 1)])
                S.sy(lambda e, d=sl, s=dram["x"][t * 128:(t + 1) * 128, :]:
                     e.dma_start(out=d, in_=s), inc=LX[t % 2], incn=16)
                ctok = S.v(lambda e, d=st.stream[:, t * D:(t + 1) * D], s=sl:
                           e.tensor_copy(d, s),
                           wait=[(LX[t % 2], LX[t % 2].count)], inc=st.SV)
                st.tile_tok[t] = (st.SV, ctok)
        else:
            LS = S.sem("LS")
            for t in range(nt):
                S.sy(lambda e, d=st.stream[:, t * D:(t + 1) * D],
                     s=dram["stream_in"][t * 128:(t + 1) * 128, :]:
                     e.dma_start(out=d, in_=s), inc=LS, incn=16)
            S.waitonly("vector", [(LS, LS.count)])
            for t in range(nt):
                st.tile_tok[t] = (LS, LS.count)

        S.v(lambda e: e.tensor_copy(st.ident[:], ident_f32[:]),
            wait=[(st.LD, consts_tok)], inc=st.SV)
        S.v(lambda e: e.memset(st.ones_row[:], 1.0), inc=st.SV)
        st.base_ready = (st.SV, st.SV.count)

        if "l0a" in phases:
            emit_l0_attn(nc, es, S, st)
        if "l0f" in phases:
            emit_l0_ffn(nc, es, S, st)
        if "l1a" in phases:
            emit_l1_attn(nc, es, S, st)
        if "l1f" in phases:
            emit_l1_ffn(nc, es, S, st, final_out=final_out)

        if stream_out == "stream":
            for t in range(nt):
                w = [st.tile_tok[t]]
                S.sy(lambda e, d=dram["stream_out"][t * 128:(t + 1) * 128, :],
                     s=st.stream[:, t * D:(t + 1) * D]: e.dma_start(out=d, in_=s),
                     wait=w, inc=st.ST, incn=16)
        if st.ST.count:
            S.waitonly("sync", [(st.ST, st.ST.count)])

        block = es.enter_context(nc.Block())
        S.build_blocks(block)
    return nc, sorted(need)


def phase_barrier(S, st):
    toks = [(st.TN, st.TN.count), (st.SV, st.SV.count), (st.SS, st.SS.count)]
    for eng in ("tensor", "vector", "scalar", "sync", "gpsimd"):
        S.waitonly(eng, toks)


def emit_ln(nc, es, S, st, r, ln_sq, ln_st, out_bf, wait):
    SV, SS = st.SV, st.SS
    w0 = list(wait) + [(SS, SS.count)]
    v1 = S.v(lambda e: e.reduce_sum(ln_st[:, 0:1], r[:], axis=AX.X), wait=w0, inc=SV)
    s1 = S.s(lambda e: e.activation(ln_sq[:], r[:], AF.Square, accum_out=ln_st[:, 1:2]),
             wait=w0, inc=SS)
    v2 = S.v(lambda e: e.tensor_scalar_mul(ln_st[:, 2:3], ln_st[:, 0:1], 1.0 / D),
             wait=[(SV, v1)], inc=SV)
    v3 = S.v(lambda e: e.tensor_scalar_mul(ln_st[:, 3:4], ln_st[:, 2:3], ln_st[:, 2:3]),
             wait=[(SV, v2)], inc=SV)
    v4 = S.v(lambda e: e.tensor_scalar(ln_st[:, 4:5], ln_st[:, 1:2], 1.0 / D,
                                       ln_st[:, 3:4], ALU.mult, ALU.subtract),
             wait=[(SS, s1), (SV, v3)], inc=SV)
    s2 = S.s(lambda e: e.activation(ln_st[:, 5:6], ln_st[:, 4:5], AF.Sqrt,
                                    bias=st.c_eps_ln), wait=[(SV, v4)], inc=SS)
    v5 = S.v(lambda e: e.reciprocal(ln_st[:, 5:6], ln_st[:, 5:6]),
             wait=[(SS, s2)], inc=SV)
    v6 = S.v(lambda e: e.tensor_scalar(ln_st[:, 6:7], ln_st[:, 2:3], ln_st[:, 5:6],
                                       -1.0, ALU.mult, ALU.mult), wait=[(SV, v5)], inc=SV)
    tok = S.v(lambda e: e.tensor_scalar(out_bf, r[:], ln_st[:, 5:6],
                                        ln_st[:, 6:7], ALU.mult, ALU.add),
              wait=[(SV, v6)], inc=SV)
    return (SV, tok)



def emit_l0_attn(nc, es, S, st):
    import os
    dbg = os.environ.get("L0A_STOP", "")
    nt = st.nt
    dram = st.dram
    LD, SV, SS, TN = st.LD, st.SV, st.SS, st.TN
    B = st.banks
    sc = Scope(st.arena)

    WL = S.sem("l0a_wl")
    wcast = S.sem("l0a_wcast")
    wA = sc.bf16(8 * D)
    wB = sc.bf16(8 * D)
    wtmp = sc.f32(D)

    def load_wt(nm, dst, extra_wait=()):
        for j in range(8):
            src = dram[nm][128 * j:128 * (j + 1), :]
            S.sy(lambda e, d=wtmp, s=src: e.dma_start(out=d, in_=s),
                 wait=[(wcast, wcast.count)] + list(extra_wait), inc=WL, incn=16)
            S.v(lambda e, d=dst[:, j * D:(j + 1) * D], s=wtmp:
                e.tensor_copy(d, s), wait=[(WL, WL.count)], inc=wcast)

    load_wt("l0_WkT", wA)
    load_wt("l0_WvT", wB)
    wk, wv = wA, wB
    if dbg == "loadw":
        return

    WS = S.sem("l0a_ws")
    bkv = sc.bf16(2 * D, rows=1)
    bo_row = sc.bf16(D, rows=1)
    bq = sc.f32(8)
    nbq = sc.f32(8)
    bsel = sc.bf16(1024, rows=16)
    stage = st.ldbuf[0:16, 0:2048]
    S.sy(lambda e: e.dma_start(out=stage[0:1, 0:2 * D], in_=dram["l0_bkv_row"]),
         wait=[st.base_ready], inc=WS, incn=16)
    c1 = S.v(lambda e: e.tensor_copy(bkv, stage[0:1, 0:2 * D]),
             wait=[(WS, WS.count)], inc=wcast)
    S.sy(lambda e: e.dma_start(out=stage[0:1, 0:D], in_=dram["l0_bo_row"]),
         wait=[(wcast, c1)], inc=WS, incn=16)
    c2 = S.v(lambda e: e.tensor_copy(bo_row, stage[0:1, 0:D]),
             wait=[(WS, WS.count)], inc=wcast)
    S.sy(lambda e: e.dma_start(out=stage[0:16, 0:1024], in_=dram["bsel"]),
         wait=[(wcast, c2)], inc=WS, incn=16)
    S.v(lambda e: e.tensor_copy(bsel, stage[0:16, 0:1024]),
        wait=[(WS, WS.count)], inc=wcast)
    WS2 = S.sem("l0a_ws2")
    S.sy(lambda e: e.dma_start(out=bq, in_=dram["l0_bq_cols"]), inc=WS2, incn=16)
    S.v(lambda e: e.tensor_scalar_mul(nbq, bq, -1.0),
        wait=[(WS2, WS2.count)], inc=wcast)

    scr64 = sc.bf16(nt * D)
    st.l0_scr = scr64
    kv_tm = sc.bf16(D)
    vx_tm = sc.bf16(16 * 65)
    S_sh = sc.f32(8 * 130)
    S_acc = S_sh
    rres = S_sh[:, 0:D]
    S_b = sc.bf16(8 * 130)
    phi_a = st.ldbuf[:, 0:512]
    phi_b = st.ldbuf[:, 512:1024]
    ln_sq = st.ldbuf[:, 1024:2048]

    ps_tp = st.tp_bank[:, 0:128]
    ps_p = [B[1], B[2]]
    ps_s = [B[3][:, 0:130], B[4][:, 0:130], B[5][:, 0:130], B[6][:, 0:130]]

    S.v(lambda e: e.memset(vx_tm, 1.0), wait=[st.base_ready], inc=SV)
    S.v(lambda e: e.memset(S_acc, 0.0), inc=SV)
    prep_tok = SV.count
    wready_tok = wcast.count

    tfree = S.sem("l0a_tfree")
    pfree = S.sem("l0a_pfree")
    sfree = S.sem("l0a_sfree")

    for t in range(nt):
        for c in range(8):
            w = []
            if t == 0 and c == 0:
                w = [(SV, prep_tok)]
            if c == 0:
                w = w + [st.tile_tok[t]]
            if t > 0 or c > 0:
                w = w + [(tfree, tfree.count)]
            tk = S.t(lambda e, sl=st.stream[:, t * D + c * 128: t * D + (c + 1) * 128]:
                     e.transpose(ps_tp, sl, st.ident[:]), wait=w, inc=TN)
            S.v(lambda e, d=scr64[:, t * D + c * 128: t * D + (c + 1) * 128]:
                e.tensor_copy(d, ps_tp), wait=[(TN, tk)], inc=tfree)
        xfm = scr64[:, t * D:(t + 1) * D]
        tfree_tok = tfree.count
        if dbg == "tp":
            continue

        for which, (wmat, brow_off) in enumerate([(wk, 0), (wv, D)]):
            for h in range(2):
                psb = ps_p[h]
                for c in range(8):
                    w = []
                    if c == 0:
                        if which == 0 and h == 0:
                            w = [(tfree, tfree_tok), (wcast, wready_tok)]
                        w = w + [(pfree, pfree.count)]
                    S.t(lambda e, wsl=wmat[:, c * D + h * 512: c * D + h * 512 + 512],
                        lhs=xfm[:, c * 128:(c + 1) * 128], p=psb, s0=(c == 0):
                        e.matmul(p[:], lhs, wsl, start=s0, stop=False),
                        wait=w)
                import os as _os
                if _os.environ.get("NOBIAS", ""):
                    tk = S.t(lambda e, wsl=wmat[:, 7 * D + h * 512: 7 * D + h * 512 + 512],
                             lhs=xfm[:, 7 * 128:8 * 128], p=psb:
                             e.matmul(p[:], lhs, wsl, start=False, stop=True), inc=TN)
                else:
                    tk = S.t(lambda e, b=bkv[0:1, brow_off + h * 512: brow_off + h * 512 + 512],
                             p=psb: e.matmul(p[:], st.ones_row[:, 0:128], b,
                                             start=False, stop=True), inc=TN)
                import os as _os2
                if _os2.environ.get("PLAINPHI", "") and which == 0:
                    S.v(lambda e, p=psb, d=kv_tm[:, h * 512:h * 512 + 512]:
                        e.tensor_copy(d, p[:]),
                        wait=[(TN, tk), (TN, TN.count)], inc=pfree)
                elif _os2.environ.get("PLAINPHI", ""):
                    S.v(lambda e, p=psb,
                        d=vx_tm[:, h * 8 * 65:(h + 1) * 8 * 65].rearrange(
                            "p (a b) -> p a b", b=65)[:, :, 0:64]:
                        e.tensor_copy(d, p[:].rearrange("p (a b) -> p a b", b=64)),
                        wait=[(TN, tk), (TN, TN.count)], inc=pfree)
                elif which == 0:
                    va = S.v(lambda e, p=psb: e.tensor_relu(phi_a, p[:]),
                             wait=[(TN, tk), (pfree, pfree.count)], inc=SV)
                    vb = S.v(lambda e, p=psb: e.tensor_scalar_min(phi_b, p[:], 0.0),
                             wait=[(SV, va)], inc=SV)
                    s2 = S.s(lambda e: e.activation(phi_b, phi_b, AF.Exp),
                             wait=[(SV, vb), (SS, SS.count)], inc=SS)
                    S.v(lambda e, d=kv_tm[:, h * 512:h * 512 + 512]:
                        e.tensor_add(d, phi_a, phi_b),
                        wait=[(SS, s2), (SV, va), (TN, TN.count)], inc=pfree)
                else:
                    S.v(lambda e, p=psb,
                        d=vx_tm[:, h * 8 * 65:(h + 1) * 8 * 65].rearrange(
                            "p (a b) -> p a b", b=65)[:, :, 0:64]:
                        e.tensor_copy(d, p[:].rearrange("p (a b) -> p a b", b=64)),
                        wait=[(TN, tk), (TN, TN.count)], inc=pfree)
        vready = pfree.count
        if dbg == "proj":
            continue

        for pr in range(8):
            bank = ps_s[pr % 4]
            w = [(pfree, vready)]
            if t > 0 or pr >= 4:
                w.append((sfree, sfree.count))
            tk = S.t(lambda e, b=bank, lhs=kv_tm[:, pr * 128:(pr + 1) * 128],
                     r=vx_tm[:, pr * 130:(pr + 1) * 130]:
                     e.matmul(b, lhs, r, start=True, stop=True), wait=w, inc=TN)
            S.v(lambda e, b=bank, d=S_acc[:, pr * 130:(pr + 1) * 130]:
                e.tensor_add(d, d, b), wait=[(TN, tk)], inc=sfree)

    if dbg in ("tp", "proj"):
        return
    cast_tok = S.v(lambda e: e.tensor_copy(S_b, S_acc),
                   wait=[(sfree, sfree.count)], inc=SV)
    if dbg == "sweep1":
        return

    sweep1_done = (TN, TN.count)
    load_wt("l0_WqT", wA, extra_wait=[sweep1_done])
    load_wt("l0_WoT", wB, extra_wait=[sweep1_done])
    wq, wo = wA, wB
    wready2 = (wcast, wcast.count)

    zbd = sc.bf16(128)
    zm = S.v(lambda e: e.memset(zbd, 0.0), inc=SV)
    prev = zm
    for h in range(16):
        c = h // 2
        zcol = (h // 2) * 130 + 65 * (h % 2) + 64
        prev = S.v(lambda e, d=zbd[64 * (h % 2):64 * (h % 2) + 64, c * 16 + h:c * 16 + h + 1],
                   s=S_b[64 * (h % 2):64 * (h % 2) + 64, zcol:zcol + 1]:
                   e.tensor_copy(d, s), wait=[(SV, prev), (SV, cast_tok)], inc=SV)
    zbd_tok = SV.count

    phiQ = sc.bf16(8 * 512)
    ctx = phiQ
    rbuf = sc.bf16(512)
    dn = sc.bf16(512, rows=16)
    dn_f = sc.f32(512, rows=16)
    ln_st = sc.f32(8)

    ps_q = [B[1], B[2]]
    ps_dn = B[0][0:16, :]
    ps_bc = [B[3], B[4]]
    ps_n2 = [B[5], B[6]]

    qfree = S.sem("l0a_qfree")
    dnfree = S.sem("l0a_dnfree")
    bcfree = S.sem("l0a_bcfree")
    nmfree = S.sem("l0a_nmfree")
    wofree = S.sem("l0a_wofree")

    for g in range(nt // 4):
        for c in range(8):
            psb = ps_q[c % 2]
            for i in range(8):
                w = []
                if g == 0 and c == 0 and i == 0:
                    w = [(SV, cast_tok), wready2]
                if i == 0 and (g > 0 or c >= 2):
                    w = w + [(qfree, qfree.count)]
                S.t(lambda e, p=psb, wsl=wq[:, i * D + c * 128: i * D + (c + 1) * 128],
                    r=scr64[:].rearrange("p (a b) -> p a b", b=D)
                      [:, 4 * g:4 * g + 4, i * 128:(i + 1) * 128],
                    s0=(i == 0), s1=(i == 7):
                    e.matmul(p[:], wsl, r, start=s0, stop=s1),
                    wait=w, inc=(TN if i == 7 else None))
            tk = TN.count
            s1 = S.s(lambda e, p=psb, b=bq[:, c:c + 1]:
                     e.activation(phi_a, p[:], AF.Relu, bias=b),
                     wait=[(TN, tk), (qfree, qfree.count)], inc=SS)
            sr = S.s(lambda e, p=psb, b=nbq[:, c:c + 1]:
                     e.activation(phi_b, p[:], AF.Relu, bias=b, scale=-1.0),
                     wait=[(SS, s1)], inc=SS)
            s3 = S.s(lambda e: e.activation(phi_b, phi_b, AF.Exp, scale=-1.0),
                     wait=[(SS, sr)], inc=SS)
            S.v(lambda e, d=phiQ[:, c * 512:(c + 1) * 512]:
                e.tensor_add(d, phi_a, phi_b),
                wait=[(SS, s3), (TN, TN.count)], inc=qfree)
        q_tok = qfree.count

        for c in range(8):
            w = []
            if c == 0:
                w = [(SV, zbd_tok), (qfree, q_tok), (dnfree, dnfree.count)]
            S.t(lambda e, lhs=zbd[:, c * 16:(c + 1) * 16],
                r=phiQ[:, c * 512:(c + 1) * 512], s0=(c == 0), s1=(c == 7):
                e.matmul(ps_dn, lhs, r, start=s0, stop=s1),
                wait=w, inc=(TN if c == 7 else None))
        dn_tok = TN.count
        da = S.v(lambda e: e.tensor_scalar_add(dn_f, ps_dn, 1e-6),
                 wait=[(TN, dn_tok), (SV, SV.count)], inc=SV)
        db = S.v(lambda e: e.reciprocal(dn_f, dn_f), wait=[(SV, da)], inc=dnfree)
        S.v(lambda e: e.tensor_copy(dn, dn_f), wait=[(SV, da), (dnfree, db)], inc=SV)
        dn_ready = SV.count

        for pr in range(8):
            bcb = ps_bc[pr % 2]
            nmb = ps_n2[pr % 2]
            w = [(SV, dn_ready)]
            if g > 0 or pr >= 2:
                w.append((bcfree, bcfree.count))
            tb = S.t(lambda e, lhs=bsel[:, pr * 128:(pr + 1) * 128], b=bcb:
                     e.matmul(b[:], lhs, dn, start=True, stop=True),
                     wait=w, inc=TN)
            S.v(lambda e, b=bcb: e.tensor_copy(rbuf, b[:]),
                wait=[(TN, tb), (SV, SV.count), (nmfree, nmfree.count)], inc=bcfree)
            rb_tok = bcfree.count
            for hh in range(2):
                w = []
                if hh == 0 and (g > 0 or pr >= 2):
                    w = [(nmfree, nmfree.count)]
                scol = pr * 130 + 65 * hh
                S.t(lambda e, bsl2=nmb[64 * hh:64 * hh + 64, :],
                    lhs=S_b[64 * hh:64 * hh + 64, scol:scol + 64],
                    r=phiQ[64 * hh:64 * hh + 64, pr * 512:(pr + 1) * 512]:
                    e.matmul(bsl2, lhs, r, start=True, stop=True),
                    wait=w, inc=(TN if hh == 1 else None))
            tn = TN.count
            S.v(lambda e, b=nmb, d=ctx[:, pr * 512:(pr + 1) * 512]:
                e.tensor_mul(d, b[:], rbuf),
                wait=[(TN, tn), (bcfree, rb_tok)], inc=nmfree)
        ctx_tok = nmfree.count

        for stile in range(4):
            t = 4 * g + stile
            for h in range(2):
                psb = ps_bc[h]
                for i in range(8):
                    w = []
                    if i == 0:
                        w = [(nmfree, ctx_tok)]
                        if g > 0 or stile > 0:
                            w.append((wofree, wofree.count))
                    S.t(lambda e, p=psb,
                        lhs=ctx[:, i * 512 + stile * 128: i * 512 + stile * 128 + 128],
                        wsl=wo[:, i * D + h * 512: i * D + h * 512 + 512], s0=(i == 0):
                        e.matmul(p[:], lhs, wsl, start=s0, stop=False),
                        wait=w)
                S.t(lambda e, p=psb, b=bo_row[0:1, h * 512:h * 512 + 512]:
                    e.matmul(p[:], st.ones_row[:, 0:128], b, start=False, stop=True),
                    inc=TN)
            tk = TN.count
            wprev = [st.tile_tok[t - 1]] if (t - 1) in st.tile_tok else []
            S.v(lambda e, p=ps_bc[0], d=rres[:, 0:512],
                s=st.stream[:, t * D: t * D + 512]:
                e.tensor_add(d, p[:], s),
                wait=[(TN, tk), (SS, SS.count), (SV, cast_tok)] + wprev, inc=SV)
            S.v(lambda e, p=ps_bc[1], d=rres[:, 512:1024],
                s=st.stream[:, t * D + 512: t * D + 1024]:
                e.tensor_add(d, p[:], s), inc=wofree)
            res_tok = wofree.count
            st.tile_tok[t] = emit_ln(nc, es, S, st, rres, ln_sq, ln_st,
                                     st.stream[:, t * D:(t + 1) * D],
                                     wait=[(wofree, res_tok)])

    st.l0a_end = {"TN": TN.count, "SV": SV.count, "SS": SS.count}
    sc.close()


def emit_l0_ffn(nc, es, S, st):
    nt = st.nt
    dram = st.dram
    SV, SS, TN = st.SV, st.SS, st.TN
    B = st.banks
    phase_barrier(S, st)
    sc = Scope(st.arena)
    TT = 4
    NP = nt // TT
    TTK = TT * 128

    c_w1 = nc.dram_tensor("l0f_cw1", [32, 128, 8 * 128], BF16, kind="Internal").ap()
    c_w2 = nc.dram_tensor("l0f_cw2", [32, 128, 8 * 128], BF16, kind="Internal").ap()
    c_h1 = nc.dram_tensor("l0f_ch1", [32, 128, 32 * 128], BF16, kind="Internal").ap()
    c_h2 = nc.dram_tensor("l0f_ch2", [32, 128, 32 * 128], BF16, kind="Internal").ap()
    c_w3 = nc.dram_tensor("l0f_cw3", [8, 128, 32 * 128], BF16, kind="Internal").ap()

    x1fm = sc.bf16(TTK * 8)
    yfm = x1fm
    hbufA = sc.bf16(TTK * 32)
    hbufB = sc.bf16(TTK * 32)
    wld = sc.f32(1024)
    w12 = [sc.bf16(1024), sc.bf16(1024)]
    w12b = [sc.bf16(1024), sc.bf16(1024)]
    hld = sc.f32(4096)
    hbf = [sc.bf16(4096), sc.bf16(4096)]
    s1c = sc.f32(32)
    s2c = sc.f32(32)
    h1c = sc.f32(32)
    h2c = sc.f32(32)
    s3c = sc.f32(8)
    rres = sc.f32(1024)
    ln_st = sc.f32(8)
    t1sb = st.ldbuf[:, 0:512]
    t2sb = st.ldbuf[:, 512:1024]
    ln_sq = st.ldbuf[:, 1024:2048]

    SL = S.sem("l0f_sl")
    S.sy(lambda e: e.dma_start(out=s1c, in_=dram["l0_w1_s_cols"]), inc=SL, incn=16)
    S.sy(lambda e: e.dma_start(out=s2c, in_=dram["l0_w2_s_cols"]), inc=SL, incn=16)
    S.sy(lambda e: e.dma_start(out=h1c, in_=dram["l0_h1_s_cols"]), inc=SL, incn=16)
    S.sy(lambda e: e.dma_start(out=h2c, in_=dram["l0_h2_s_cols"]), inc=SL, incn=16)
    S.sy(lambda e: e.dma_start(out=s3c, in_=dram["l0_w3_s_cols"]), inc=SL, incn=16)
    sl_tok = SL.count

    WL = S.sem("l0f_wl")
    HB = [S.sem("l0f_hb0"), S.sem("l0f_hb1")]
    WB = [S.sem("l0f_wb0"), S.sem("l0f_wb1")]
    CS = S.sem("l0f_cs")
    HS = S.sem("l0f_hs")
    tpf = S.sem("l0f_tpf")
    pf = S.sem("l0f_pf")
    hf = S.sem("l0f_hf")
    yf = S.sem("l0f_yf")
    lnf = S.sem("l0f_lnf")

    ps_tp = st.tp_bank[:, 0:128]
    ps_w = [B[1], B[2]]
    ps_h = [B[3], B[4]]
    ps_y = [B[5], B[6]]

    def ldcast_pass0(cache, src_blk, dst_bf, wait_w, cast_cols):
        S.sy(lambda e, d=hld[:, 0:cast_cols] if cast_cols > 1024 else wld,
             s=src_blk: e.dma_start(out=d, in_=s),
             wait=wait_w, inc=WL, incn=16)
        ct = S.v(lambda e, d=dst_bf, s=hld[:, 0:cast_cols] if cast_cols > 1024 else wld:
                 e.tensor_copy(d, s), wait=[(WL, WL.count)], inc=CS)
        S.sy(lambda e, d=cache, s=dst_bf: e.dma_start(out=d, in_=s),
             wait=[(CS, ct)], inc=HS, incn=16)
        return ct

    prev_pass_tok = None
    for p in range(NP):
        for t in range(TT):
            gt = p * TT + t
            for c in range(8):
                w = [(tpf, tpf.count)] if (p > 0 or t > 0 or c > 0) else []
                if c == 0:
                    w = w + [st.tile_tok[gt]]
                tk = S.t(lambda e, sl=st.stream[:, gt * D + c * 128: gt * D + (c + 1) * 128]:
                         e.transpose(ps_tp, sl, st.ident[:]), wait=w, inc=TN)
                S.v(lambda e, d=x1fm[:].rearrange("p (a b) -> p a b", b=TTK)
                    [:, c, t * 128:(t + 1) * 128]:
                    e.tensor_copy(d, ps_tp), wait=[(TN, tk)], inc=tpf)
        x1_tok = tpf.count

        for ob in range(32):
            sel = ob % 2
            wcur = w12 if sel == 0 else w12b
            wwait = []
            if p == 0:
                for which, (cac, dnm) in enumerate([(c_w1, "l0_w1T"), (c_w2, "l0_w2T")]):
                    src = dram[dnm][ob].rearrange("(c p) o -> p c o", p=128)
                    wwait2 = [(CS, CS.count)]
                    S.sy(lambda e, d=wld.rearrange("p (c o) -> p c o", c=8), s=src:
                         e.dma_start(out=d, in_=s),
                         wait=wwait2 + ([(TN, TN.count)] if ob >= 2 else []),
                         inc=WL, incn=16)
                    ct = S.v(lambda e, d=wcur[which], s=wld: e.tensor_copy(d, s),
                             wait=[(WL, WL.count), (HS, HS.count)], inc=CS)
                    S.sy(lambda e, d=cac[ob], s=wcur[which]: e.dma_start(out=d, in_=s),
                         wait=[(CS, ct)], inc=HS, incn=16)
                wrdy = (CS, CS.count)
            else:
                for which, cac in enumerate([c_w1, c_w2]):
                    S.sy(lambda e, d=wcur[which], s=cac[ob]: e.dma_start(out=d, in_=s),
                         wait=[(HS, HS.count), (TN, TN.count)] if ob < 2 and p == 1 else
                              [(TN, TN.count)],
                         inc=WB[sel], incn=16)
                wrdy = (WB[sel], WB[sel].count)
            for which in range(2):
                psb = ps_w[which]
                for ic in range(8):
                    w = [wrdy] if ic == 0 else []
                    if ic == 0 and which == 0:
                        w = w + [(tpf, x1_tok), (SL, sl_tok), (pf, pf.count)]
                    elif ic == 0:
                        w = w + [(pf, pf.count)]
                    S.t(lambda e, p2=psb, lhs=x1fm[:].rearrange("p (a b) -> p a b", b=TTK)[:, ic, :],
                        wsl=wcur[which][:, ic * 128:(ic + 1) * 128], s0=(ic == 0), s1=(ic == 7):
                        e.matmul(p2[:], wsl, lhs, start=s0, stop=s1),
                        wait=w, inc=(TN if ic == 7 else None))
            tk = TN.count
            sg = S.s(lambda e, sl=s1c[:, ob:ob + 1]:
                     e.activation(t1sb, ps_w[0][:], AF.Silu, scale=sl),
                     wait=[(TN, tk), (SS, SS.count), (pf, pf.count)], inc=SS)
            vg = S.v(lambda e, sl=s2c[:, ob:ob + 1]:
                     e.tensor_scalar_mul(t2sb, ps_w[1][:], sl),
                     wait=[(TN, tk), (SV, SV.count)], inc=SV)
            S.v(lambda e, d=hbufA[:].rearrange("p (a b) -> p a b", b=TTK)[:, ob, :]:
                e.tensor_mul(d, t1sb, t2sb), wait=[(SS, sg), (SV, vg)], inc=pf)
        h_tok = pf.count

        for stage, (cac, dnm, hin, hout, scol) in enumerate(
                [(c_h1, "l0_h1T", hbufA, hbufB, h1c), (c_h2, "l0_h2T", hbufB, hbufA, h2c)]):
            for ob in range(32):
                sel = ob % 2
                if p == 0:
                    src = dram[dnm][ob].rearrange("(c p) o -> p c o", p=128)
                    S.sy(lambda e, d=hld.rearrange("p (c o) -> p c o", c=32), s=src:
                         e.dma_start(out=d, in_=s),
                         wait=[(CS, CS.count)], inc=WL, incn=16)
                    ct = S.v(lambda e, d=hbf[sel], s=hld: e.tensor_copy(d, s),
                             wait=[(WL, WL.count), (TN, TN.count), (HS, HS.count)], inc=CS)
                    S.sy(lambda e, d=cac[ob], s=hbf[sel]: e.dma_start(out=d, in_=s),
                         wait=[(CS, ct)], inc=HS, incn=16)
                    hrdy = (CS, ct)
                else:
                    S.sy(lambda e, d=hbf[sel], s=cac[ob]: e.dma_start(out=d, in_=s),
                         wait=[(TN, TN.count)], inc=HB[sel], incn=16)
                    hrdy = (HB[sel], HB[sel].count)
                psb = ps_h[sel]
                for ic in range(32):
                    w = []
                    if ic == 0:
                        w = [hrdy, (hf, hf.count)]
                        if ob == 0:
                            w.append((pf if stage == 0 else hf, h_tok if stage == 0 else h2_tok))
                    S.t(lambda e, p2=psb,
                        lhs=hbf[sel][:, ic * 128:(ic + 1) * 128],
                        r=hin[:].rearrange("p (a b) -> p a b", b=TTK)[:, ic, :],
                        s0=(ic == 0), s1=(ic == 31):
                        e.matmul(p2[:], lhs, r, start=s0, stop=s1),
                        wait=w, inc=(TN if ic == 31 else None))
                tk = TN.count
                S.s(lambda e, p2=psb, sl=scol[:, ob:ob + 1],
                    d=hout[:].rearrange("p (a b) -> p a b", b=TTK)[:, ob, :]:
                    e.activation(d, p2[:], AF.Gelu, scale=sl),
                    wait=[(TN, tk), (SS, SS.count)], inc=hf)
            h2_tok = hf.count

        for ob in range(8):
            sel = ob % 2
            if p == 0:
                src = dram["l0_w3T"][ob].rearrange("(c p) o -> p c o", p=128)
                S.sy(lambda e, d=hld.rearrange("p (c o) -> p c o", c=32), s=src:
                     e.dma_start(out=d, in_=s),
                     wait=[(CS, CS.count)], inc=WL, incn=16)
                ct = S.v(lambda e, d=hbf[sel], s=hld: e.tensor_copy(d, s),
                         wait=[(WL, WL.count), (TN, TN.count), (HS, HS.count)], inc=CS)
                S.sy(lambda e, d=c_w3[ob], s=hbf[sel]: e.dma_start(out=d, in_=s),
                     wait=[(CS, ct)], inc=HS, incn=16)
                w3rdy = (CS, ct)
            else:
                S.sy(lambda e, d=hbf[sel], s=c_w3[ob]: e.dma_start(out=d, in_=s),
                     wait=[(TN, TN.count)], inc=HB[sel], incn=16)
                w3rdy = (HB[sel], HB[sel].count)
            psb = ps_y[sel]
            for ic in range(32):
                w = []
                if ic == 0:
                    w = [w3rdy, (yf, yf.count)]
                    if ob == 0:
                        w.append((hf, h2_tok))
                S.t(lambda e, p2=psb,
                    lhs=hbf[sel][:, ic * 128:(ic + 1) * 128],
                    r=hbufA[:].rearrange("p (a b) -> p a b", b=TTK)[:, ic, :],
                    s0=(ic == 0), s1=(ic == 31):
                    e.matmul(p2[:], lhs, r, start=s0, stop=s1),
                    wait=w, inc=(TN if ic == 31 else None))
            tk = TN.count
            S.v(lambda e, p2=psb, sl=s3c[:, ob:ob + 1],
                d=yfm[:].rearrange("p (a b) -> p a b", b=TTK)[:, ob, :]:
                e.tensor_scalar_mul(d, p2[:], sl),
                wait=[(TN, tk), (SV, SV.count), (TN, TN.count)], inc=yf)
        y_tok = yf.count

        for t in range(TT):
            gt = p * TT + t
            for c in range(8):
                w = [(tpf, tpf.count)]
                if t == 0 and c == 0:
                    w.append((yf, y_tok))
                tk = S.t(lambda e, sl=yfm[:].rearrange("p (a b) -> p a b", b=TTK)
                         [:, c, t * 128:(t + 1) * 128]:
                         e.transpose(ps_tp, sl, st.ident[:]), wait=w, inc=TN)
                S.v(lambda e, d=rres[:, c * 128:(c + 1) * 128],
                    s=st.stream[:, gt * D + c * 128: gt * D + (c + 1) * 128]:
                    e.tensor_add(d, ps_tp, s),
                    wait=[(TN, tk), (lnf, lnf.count), (SS, SS.count)], inc=tpf)
            res_tok = tpf.count
            st.tile_tok[gt] = emit_ln(nc, es, S, st, rres, ln_sq, ln_st,
                                      st.stream[:, gt * D:(gt + 1) * D],
                                      wait=[(tpf, res_tok)])
            S.v(lambda e: e.engine_nop(), wait=[st.tile_tok[gt]], inc=lnf)

    st.l0f_end = {"TN": TN.count}
    sc.close()


def emit_l1_attn(nc, es, S, st):
    nt = st.nt
    dram = st.dram
    SV, SS, TN = st.SV, st.SS, st.TN
    B = st.banks
    phase_barrier(S, st)
    sc = Scope(st.arena)
    NH = 6
    NG = nt // 4

    WL = S.sem("l1_wl")
    CS = S.sem("l1_cs")
    HS = S.sem("l1_hs")
    cache = {}
    sizes = {"l1_qT": (8, 768), "l1_kT": (8, 768), "l1_vT": (8, 1536),
             "l1_gT": (8, 1536), "l1_oT": (12, 1024)}
    for nm, (blocks, ob) in sizes.items():
        cache[nm] = nc.dram_tensor("c_" + nm, [blocks * 128, ob], BF16,
                                   kind="Internal").ap()
    nw_cols = sc.f32(12)
    NW = S.sem("l1_nw")
    S.sy(lambda e: e.dma_start(out=nw_cols, in_=dram["l1_nw_cols"]), inc=NW, incn=16)
    S.waitonly("vector", [(NW, NW.count)])
    psc = Scope(st.arena)
    wstg = psc.f32(4096)
    wstgb = psc.bf16(4096)
    for nm, (blocks, ob) in sizes.items():
        rows_per = 4096 // ob
        ic = 0
        while ic < blocks:
            n = min(rows_per, blocks - ic)
            S.sy(lambda e, d=wstg[:, 0:n * ob].rearrange("p (c o) -> p c o", c=n),
                 s=dram[nm][ic * 128:(ic + n) * 128, :].rearrange("(c p) o -> p c o", p=128):
                 e.dma_start(out=d, in_=s),
                 wait=[(CS, CS.count)], inc=WL, incn=16)
            if nm == "l1_oT":
                ct = None
                for j in range(n):
                    ct = S.v(lambda e, d=wstgb[:, j * ob:(j + 1) * ob],
                             s=wstg[:, j * ob:(j + 1) * ob],
                             sl=nw_cols[:, ic + j:ic + j + 1]:
                             e.tensor_scalar_mul(d, s, sl),
                             wait=[(WL, WL.count), (HS, HS.count)], inc=CS)
            else:
                ct = S.v(lambda e, d=wstgb[:, 0:n * ob], s=wstg[:, 0:n * ob]:
                         e.tensor_copy(d, s),
                         wait=[(WL, WL.count), (HS, HS.count)], inc=CS)
            S.sy(lambda e, d=cache[nm][ic * 128:(ic + n) * 128, :].rearrange(
                 "(c p) o -> p c o", p=128),
                 s=wstgb[:, 0:n * ob].rearrange("p (c o) -> p c o", c=n):
                 e.dma_start(out=d, in_=s),
                 wait=[(CS, CS.count)], inc=HS, incn=16)
            ic += n
    cache_tok = HS.count
    psc.close()

    SM = S.sem("l1_sm")
    aT32b = sc.bf16(8 * 32)
    bT32b = sc.bf16(8 * 32)
    abstg = st.ldbuf[:, 0:256]
    convq = sc.f32(24)
    convk = sc.f32(24)
    convv = sc.f32(48)
    msl = sc.bf16(128)
    miu = sc.bf16(128)
    msu = sc.bf16(128)
    stg2 = st.ldbuf[:, 0:1024]
    S.sy(lambda e: e.dma_start(out=convq, in_=dram["l1_convq_cols"]), inc=SM, incn=16)
    S.sy(lambda e: e.dma_start(out=convk, in_=dram["l1_convk_cols"]), inc=SM, incn=16)
    S.sy(lambda e: e.dma_start(out=convv, in_=dram["l1_convv_cols"]), inc=SM, incn=16)
    m1 = S.sy(lambda e: e.dma_start(out=abstg.rearrange("p (c o) -> p c o", c=8),
                                    in_=dram["l1_aT32"].rearrange("(c p) o -> p c o", p=128)),
              wait=[st.tile_tok[nt - 1]], inc=SM, incn=16)
    c0a = S.v(lambda e: e.tensor_copy(aT32b, abstg), wait=[(SM, SM.count)], inc=CS)
    S.sy(lambda e: e.dma_start(out=abstg.rearrange("p (c o) -> p c o", c=8),
                               in_=dram["l1_bT32"].rearrange("(c p) o -> p c o", p=128)),
         wait=[(CS, c0a)], inc=SM, incn=16)
    c0 = S.v(lambda e: e.tensor_copy(bT32b, abstg), wait=[(SM, SM.count)], inc=CS)
    S.sy(lambda e: e.dma_start(out=stg2[:, 0:128], in_=dram["mask_strict_low"]),
         wait=[(CS, c0)], inc=SM, incn=16)
    c1 = S.v(lambda e: e.tensor_copy(msl, stg2[:, 0:128]), wait=[(SM, SM.count)], inc=CS)
    S.sy(lambda e: e.dma_start(out=stg2[:, 0:128], in_=dram["tri_incl"]),
         wait=[(CS, c1)], inc=SM, incn=16)
    c2 = S.v(lambda e: e.tensor_copy(miu, stg2[:, 0:128]), wait=[(SM, SM.count)], inc=CS)
    c3 = S.v(lambda e: e.tensor_sub(msu, miu, st.ident[:]), wait=[(CS, c2)], inc=CS)
    alog = sc.f32(1, rows=32)
    dtb = sc.f32(1, rows=32)
    negeA = sc.f32(1, rows=32)
    zz1 = S.v(lambda e: e.memset(alog, 0.0), inc=CS)
    zz2 = S.v(lambda e: e.memset(dtb, 0.0), wait=[(CS, zz1)], inc=CS)
    SM2 = S.sem("l1_sm2")
    S.sy(lambda e: e.dma_start(out=alog[0:6, :], in_=dram["l1_Alog_row"].rearrange("a b -> b a")),
         wait=[(CS, zz2)], inc=SM2, incn=16)
    S.sy(lambda e: e.dma_start(out=dtb[0:6, :], in_=dram["l1_dtb_row"].rearrange("a b -> b a")),
         inc=SM2, incn=16)
    e1 = S.s(lambda e: e.activation(negeA, alog, AF.Exp), wait=[(SM2, SM2.count)], inc=SS)
    S.v(lambda e: e.tensor_scalar_mul(negeA, negeA, -1.0), wait=[(SS, e1)], inc=CS)
    sm_tok = CS.count

    S_f = [sc.f32(256) for _ in range(NH)]
    S_bf = [sc.bf16(256) for _ in range(NH)]
    pv = S.sem("l1_pv")
    for h in range(NH):
        S.v(lambda e, d=S_f[h]: e.memset(d, 0.0), inc=pv)
        S.v(lambda e, d=S_bf[h]: e.memset(d, 0.0), inc=pv)
    onescol = sc.bf16(1)
    onesrow_f = sc.f32(128, rows=1)
    S.v(lambda e: e.memset(onescol, 1.0), inc=pv)
    S.v(lambda e: e.memset(onesrow_f, 1.0), inc=pv)
    qsliv = sc.bf16(6 * 4)
    ksliv = sc.bf16(6 * 4)
    vsliv = sc.bf16(12 * 4)
    for sl_ in (qsliv, ksliv, vsliv):
        S.v(lambda e, d=sl_: e.memset(d, 0.0), inc=pv)
    prep_tok = pv.count

    wbuf1 = sc.bf16(12288)
    wbuf2 = sc.bf16(12288)
    WS1 = S.sem("l1_ws1")
    WS2 = S.sem("l1_ws2")

    x2fm = sc.bf16(8 * 512)
    u_qh = sc.bf16(516)
    u_kh = sc.bf16(516)
    u_vh = sc.bf16(2 * 516)
    cqh = sc.bf16(512)
    ckh = sc.bf16(512)
    cvh = sc.bf16(2 * 512)
    cvt = sc.bf16(512)
    gate_tm = sc.bf16(1536)
    o_tm = sc.bf16(4 * 1536)
    g_fm = sc.f32(512, rows=32)
    b_fm = sc.f32(512, rows=32)
    c_fm = sc.f32(512, rows=32)
    sp_fm = sc.f32(512, rows=32)
    c_tm = sc.f32(4 * 6)
    b_tm = sc.f32(4 * 6)
    nc_tm = sc.f32(4 * 6)
    nb_tm = sc.f32(4 * 6)
    crow = sc.f32(512, rows=1)
    nrow = sc.f32(512, rows=1)
    erow = sc.f32(512, rows=1)
    rnk = sc.f32(512, rows=1)
    rnq = sc.f32(512, rows=1)
    rowsf = sc.f32(512, rows=1)
    rowsb = sc.bf16(5 * 512, rows=1)
    kbrow = sc.bf16(128, rows=1)
    sqb = sc.bf16(512)
    Xl = sc.bf16(128)
    Xu = sc.bf16(128)
    Gt = sc.bf16(128)
    Gn = sc.bf16(128)
    Pn = sc.bf16(128)
    Qn = sc.bf16(128)
    Qt = sc.bf16(128)
    Aqt = sc.bf16(128)
    kpf = sc.bf16(128)
    khf = sc.bf16(128)
    kbf = sc.bf16(128)
    qhf = sc.bf16(128)
    qpf = sc.bf16(128)
    kbt = sc.bf16(128)
    v_t = sc.bf16(256)
    Rt = sc.bf16(256)
    Dl = sc.bf16(256)
    gcol = sc.f32(1)
    ln_st = sc.f32(8)
    rres = sc.f32(1024)
    ogf = sc.bf16(12 * 128)
    ln_sq = st.ldbuf[:, 1024:2048]

    ps_bc = B[0]
    ps_kk = B[1]
    ps_qk = B[2]
    ps_pj = [B[3], B[4]]
    ps_d = [B[3], B[4], B[5]]
    ps_ks = B[6]
    ps_ab = B[6]
    ps_o = B[5]
    ps_tp = st.tp_bank[:, 0:128]

    gf = S.sem("l1_gf")
    tf = S.sem("l1_tf")
    df = S.sem("l1_df")
    RD = S.sem("l1_rd")

    for grp in range(NG):
        t0 = grp * 4
        S.sy(lambda e, d=wbuf1[:, 0:6144].rearrange("p (c o) -> p c o", c=8),
             s=cache["l1_qT"][:].rearrange("(c p) o -> p c o", p=128):
             e.dma_start(out=d, in_=s),
             wait=[(HS, cache_tok), (TN, TN.count)], inc=WS1, incn=16)
        S.sy(lambda e, d=wbuf1[:, 6144:12288].rearrange("p (c o) -> p c o", c=8),
             s=cache["l1_kT"][:].rearrange("(c p) o -> p c o", p=128):
             e.dma_start(out=d, in_=s), inc=WS1, incn=16)
        S.sy(lambda e, d=wbuf2[:].rearrange("p (c o) -> p c o", c=8),
             s=cache["l1_vT"][:].rearrange("(c p) o -> p c o", p=128):
             e.dma_start(out=d, in_=s),
             wait=[(TN, TN.count)], inc=WS2, incn=16)
        qkv_rdy1 = WS1.count
        qkv_rdy2 = WS2.count
        qTv = wbuf1[:, 0:6144]
        kTv = wbuf1[:, 6144:12288]
        vTv = wbuf2

        for tt in range(4):
            gt = t0 + tt
            for c in range(8):
                w = [(tf, tf.count)] if (grp > 0 or tt > 0 or c > 0) else [(pv, prep_tok)]
                if c == 0:
                    w = w + [st.tile_tok[gt]]
                tk = S.t(lambda e, sl=st.stream[:, gt * D + c * 128: gt * D + (c + 1) * 128]:
                         e.transpose(ps_tp, sl, st.ident[:]), wait=w, inc=TN)
                S.v(lambda e, d=x2fm[:].rearrange("p (a b) -> p a b", b=512)
                    [:, c, tt * 128:(tt + 1) * 128]:
                    e.tensor_copy(d, ps_tp), wait=[(TN, tk)], inc=tf)
        x2_tok = tf.count

        for which, wmat2 in ((0, aT32b), (1, bT32b)):
            for ic in range(8):
                w = [(df, df.count), (tf, x2_tok), (CS, sm_tok)] if ic == 0 and which == 0 else []
                S.t(lambda e, p2=ps_ab[32 * which:32 * which + 32, :],
                    wsl=wmat2[:, ic * 32:(ic + 1) * 32],
                    r=x2fm[:].rearrange("p (a b) -> p a b", b=512)[:, ic, :],
                    s0=(ic == 0), s1=(ic == 7):
                    e.matmul(p2, wsl, r, start=s0, stop=s1),
                    wait=w, inc=(TN if ic == 7 else None))
        ab_tok = TN.count
        v1 = S.v(lambda e, bb=dtb.broadcast_to([32, 512]):
                 e.tensor_tensor(g_fm, ps_ab[0:32, :], bb, ALU.add),
                 wait=[(TN, ab_tok), (SV, SV.count), (SS, SS.count)], inc=SV)
        v2 = S.s(lambda e: e.activation(sp_fm, g_fm, AF.Abs),
                 wait=[(SV, v1), (SS, SS.count)], inc=SS)
        s1 = S.s(lambda e: e.activation(sp_fm, sp_fm, AF.Exp, scale=-1.0),
                 wait=[(SS, v2)], inc=SS)
        s2 = S.s(lambda e: e.activation(sp_fm, sp_fm, AF.Ln, bias=1.0),
                 wait=[(SS, s1)], inc=SS)
        v3 = S.v(lambda e: e.tensor_relu(c_fm, g_fm), wait=[(SV, v1), (SS, v2)], inc=SV)
        v4 = S.v(lambda e: e.tensor_add(g_fm, c_fm, sp_fm),
                 wait=[(SV, v3), (SS, s2)], inc=SV)
        v5 = S.v(lambda e, bb=negeA.broadcast_to([32, 512]):
                 e.tensor_tensor(g_fm, g_fm, bb, ALU.mult),
                 wait=[(SV, v4)], inc=SV)
        s3 = S.s(lambda e: e.activation(b_fm, ps_ab[32:64, :], AF.Sigmoid),
                 wait=[(TN, ab_tok), (SS, s2), (SV, v4)], inc=SS)
        cprev = v5
        for cc in range(4):
            cprev = S.v(lambda e, d=c_fm[:, cc * 128:(cc + 1) * 128],
                        s=g_fm[:, cc * 128:(cc + 1) * 128]:
                        e.tensor_tensor_scan(d, s, s, 0.0, ALU.add, ALU.bypass),
                        wait=[(SV, cprev)], inc=SV)
        cums_tok = cprev
        for srcb, dstb, negb, wtok in ((c_fm, c_tm, nc_tm, cums_tok),
                                       (b_fm, b_tm, nb_tm, None)):
            for cc in range(4):
                w = [(SV, SV.count), (SS, SS.count), (TN, TN.count)]
                tk = S.t(lambda e, sl=srcb[0:6, cc * 128:(cc + 1) * 128]:
                         e.transpose(ps_qk[:, 0:6], sl, st.ident_f32[0:6, 0:6]),
                         wait=w, inc=TN)
                vv = S.v(lambda e, d=dstb[:, cc * 6:(cc + 1) * 6]:
                         e.tensor_copy(d, ps_qk[:, 0:6]), wait=[(TN, tk)], inc=SV)
                S.v(lambda e, d=negb[:, cc * 6:(cc + 1) * 6],
                    s=dstb[:, cc * 6:(cc + 1) * 6]:
                    e.tensor_scalar_mul(d, s, -1.0), wait=[(SV, vv)], inc=SV)
        ctb_tok = SV.count

        for h in range(NH):
            for wv_, ubuf, sliv, noc, base in ((qTv, u_qh, qsliv, 1, h),
                                               (kTv, u_kh, ksliv, 1, h),
                                               (vTv, u_vh, vsliv, 2, 2 * h)):
                for oc in range(noc):
                    psb = ps_pj[oc % 2]
                    for ic in range(8):
                        w = []
                        if ic == 0:
                            w = [(gf, gf.count), (SV, SV.count)]
                            if h == 0 and wv_ is qTv and oc == 0:
                                w += [(WS1, qkv_rdy1), (WS2, qkv_rdy2), (SV, ctb_tok),
                                      (df, df.count), (SS, SS.count)]
                        stride = 768 if noc == 1 else 1536
                        woff = ic * stride + (base + oc) * 128
                        S.t(lambda e, p2=psb, wsl=wv_[:, woff:woff + 128],
                            r=x2fm[:].rearrange("p (a b) -> p a b", b=512)[:, ic, :],
                            s0=(ic == 0), s1=(ic == 7):
                            e.matmul(p2[:], wsl, r, start=s0, stop=s1),
                            wait=w, inc=(TN if ic == 7 else None))
                    tk = TN.count
                    S.v(lambda e, d=ubuf[:].rearrange("p (a b) -> p a b", b=516)[:, oc, 0:4],
                        s=sliv[:, (base + oc) * 4:(base + oc + 1) * 4]:
                        e.tensor_copy(d, s), wait=[(SV, SV.count)], inc=SV)
                    S.v(lambda e, p2=psb, d=ubuf[:].rearrange("p (a b) -> p a b", b=516)
                        [:, oc, 4:516]:
                        e.tensor_copy(d, p2[:]),
                        wait=[(TN, tk), (SV, SV.count)], inc=gf)
            projh_tok = gf.count
            conv_sets = ((u_qh, cqh, convq, 1, h, qsliv), (u_kh, ckh, convk, 1, h, ksliv),
                         (u_vh, cvh, convv, 2, 2 * h, vsliv))
            for ubuf, cbuf, wcols, noc, base, sliv in conv_sets:
                vv = S.v(lambda e, d=cbuf[:].rearrange("p (a b) -> p a b", b=512)[:, 0:noc, :],
                         s=ubuf[:].rearrange("p (a b) -> p a b", b=516)[:, 0:noc, 1:513],
                         wt=wcols[:].rearrange("p (a b) -> p a b", b=4)
                         [:, base:base + noc, 0:1].broadcast_to([128, noc, 512]):
                         e.tensor_tensor(d, s, wt, ALU.mult),
                         wait=[(gf, projh_tok), (SV, SV.count)], inc=SV)
                for j in range(1, 4):
                    for oc in range(noc):
                        vt_ = S.v(lambda e, d=cvt[:],
                                  s=ubuf[:].rearrange("p (a b) -> p a b", b=516)
                                  [:, oc, 1 + j:513 + j],
                                  wt=wcols[:, (base + oc) * 4 + j:(base + oc) * 4 + j + 1]:
                                  e.tensor_scalar_mul(d, s, wt),
                                  wait=[(SV, SV.count)], inc=SV)
                        vv = S.v(lambda e, d=cbuf[:].rearrange("p (a b) -> p a b", b=512)[:, oc, :],
                                 s=cvt[:]:
                                 e.tensor_tensor(d, d, s, ALU.add),
                                 wait=[(SV, vt_)], inc=SV)
                S.v(lambda e, d=sliv[:, base * 4:(base + noc) * 4].rearrange(
                    "p (a b) -> p a b", b=4)[:, :, 1:4],
                    s=ubuf[:].rearrange("p (a b) -> p a b", b=516)[:, 0:noc, 513:516],
                    sl2=0:
                    e.tensor_copy(d, s), wait=[(SV, vv)], inc=SV)
                S.s(lambda e, d=cbuf[:, 0:noc * 512]:
                    e.activation(d, d, AF.Silu),
                    wait=[(SV, SV.count), (SS, SS.count)], inc=SS)
            conv_tok = SS.count
            import os as _os
            if _os.environ.get("L1A_DBG") == f"conv{grp}_{h}":
                DB = S.sem(f"dbg{grp}{h}")
                for nmb2, off in ((cqh, 0), (ckh, 512), (cvh, 1024)):
                    S.g(lambda e, d=dram["dbg"][:, off:off + (1024 if off == 1024 else 512)],
                        s=nmb2[:, 0:(1024 if off == 1024 else 512)]:
                        e.dma_start(out=d, in_=s),
                        wait=[(SS, conv_tok)], inc=DB, incn=16)
                S.waitonly("vector", [(DB, DB.count)])
                S.waitonly("scalar", [(DB, DB.count)])

            r1 = S.sy(lambda e, d=crow, s=c_fm[h:h + 1, :]: e.dma_start(out=d, in_=s),
                      wait=[(SV, ctb_tok), (TN, TN.count), (SS, SS.count),
                            (SV, SV.count)], inc=RD, incn=16)
            S.sy(lambda e, d=rowsf, s=b_fm[h:h + 1, :]: e.dma_start(out=d, in_=s),
                 wait=[(RD, RD.count)], inc=RD, incn=16)
            rd_tok = RD.count
            nv = S.v(lambda e: e.tensor_scalar_mul(nrow, crow, -1.0),
                     wait=[(RD, rd_tok), (SV, SV.count), (SS, SS.count)], inc=SV)
            bv = S.v(lambda e: e.tensor_copy(rowsb[0:1, 1536:2048], rowsf),
                     wait=[(SV, nv)], inc=SV)
            ev = S.s(lambda e: e.activation(erow, crow, AF.Exp),
                     wait=[(RD, rd_tok), (SS, conv_tok)], inc=SS)
            sqk = S.v(lambda e: e.tensor_tensor(sqb, ckh, ckh, ALU.mult),
                      wait=[(SS, conv_tok), (SV, bv)], inc=SV)
            tk1 = S.t(lambda e: e.matmul(ps_bc[0:1, :], onescol, sqb, start=True, stop=True),
                      wait=[(SV, sqk), (TN, TN.count)], inc=TN)
            sk1 = S.s(lambda e: e.activation(rnk, ps_bc[0:1, 0:512], AF.Ln,
                                             bias=st.c_eps6[0:1, :]),
                      wait=[(TN, tk1), (SS, ev), (SV, SV.count)], inc=SS)
            sk2 = S.s(lambda e: e.activation(rnk, rnk, AF.Exp, scale=-0.5),
                      wait=[(SS, sk1)], inc=SS)
            sqq = S.v(lambda e: e.tensor_tensor(sqb, cqh, cqh, ALU.mult),
                      wait=[(SV, sqk), (TN, tk1)], inc=SV)
            tk2 = S.t(lambda e: e.matmul(ps_bc[0:1, :], onescol, sqb, start=True, stop=True),
                      wait=[(SV, sqq), (SS, sk2)], inc=TN)
            sq1 = S.s(lambda e: e.activation(rnq, ps_bc[0:1, 0:512], AF.Ln,
                                             bias=st.c_eps6[0:1, :]),
                      wait=[(TN, tk2), (SS, sk2)], inc=SS)
            sq2 = S.s(lambda e: e.activation(rnq, rnq, AF.Exp, scale=-0.5,
                                             bias=st.c_qln[0:1, :]),
                      wait=[(SS, sq1)], inc=SS)
            w1 = S.v(lambda e: e.tensor_copy(rowsb[0:1, 0:512], rnk),
                     wait=[(SS, sq2), (SV, SV.count)], inc=SV)
            w2 = S.v(lambda e: e.tensor_mul(rowsf, rnk, erow), wait=[(SV, w1)], inc=SV)
            w3 = S.v(lambda e: e.tensor_copy(rowsb[0:1, 512:1024], rowsf),
                     wait=[(SV, w2)], inc=SV)
            w4 = S.v(lambda e: e.tensor_copy(rowsb[0:1, 1024:1536], rnq),
                     wait=[(SV, w3)], inc=SV)
            w5 = S.v(lambda e: e.tensor_copy(rowsb[0:1, 2048:2560], erow),
                     wait=[(SV, w4)], inc=SV)
            rows_tok = SV.count
            if _os.environ.get("L1A_DBG") == f"rows{grp}_{h}":
                DB2 = S.sem(f"dbg2{grp}{h}")
                for nmb2, off in ((crow, 0), (erow, 512), (rnk, 1024), (rnq, 1536)):
                    S.g(lambda e, d=dram["dbg"][0:1, off:off + 512], s=nmb2:
                        e.dma_start(out=d, in_=s),
                        wait=[(SV, rows_tok)], inc=DB2, incn=16)
                S.waitonly("vector", [(DB2, DB2.count)])
                S.waitonly("scalar", [(DB2, DB2.count)])
                S.waitonly("tensor", [(DB2, DB2.count)])

            for cc in range(4):
                cs = slice(cc * 128, (cc + 1) * 128)
                ccol = c_tm[:, cc * 6 + h:cc * 6 + h + 1]
                nccol = nc_tm[:, cc * 6 + h:cc * 6 + h + 1]
                nbcol = nb_tm[:, cc * 6 + h:cc * 6 + h + 1]
                bcol = b_tm[:, cc * 6 + h:cc * 6 + h + 1]
                for vh in range(2):
                    tkv = S.t(lambda e, sl=cvh[:].rearrange("p (a b) -> p a b", b=512)
                              [:, vh, cs]:
                              e.transpose(ps_tp, sl, st.ident[:]),
                              wait=[(tf, tf.count), (SV, rows_tok)], inc=TN)
                    S.v(lambda e, d=v_t[:, vh * 128:(vh + 1) * 128]:
                        e.tensor_copy(d, ps_tp), wait=[(TN, tkv)], inc=tf)
                vt_tok = tf.count
                tb1 = S.t(lambda e, r=rowsb[0:1, cs]:
                          e.matmul(ps_bc[:, 0:128], st.ones_row[:, 0:128], r,
                                   start=True, stop=True),
                          wait=[(SV, rows_tok), (TN, TN.count)], inc=TN)
                vk1 = S.v(lambda e, s=ckh[:, cs]:
                          e.tensor_mul(kpf, s, ps_bc[:, 0:128]),
                          wait=[(TN, tb1), (SV, SV.count)], inc=SV)
                tb2 = S.t(lambda e, r=rowsb[0:1, 512 + cc * 128:512 + (cc + 1) * 128]:
                          e.matmul(ps_bc[:, 128:256], st.ones_row[:, 0:128], r,
                                   start=True, stop=True),
                          wait=[(SV, vk1)], inc=TN)
                vk2 = S.v(lambda e, s=ckh[:, cs]:
                          e.tensor_mul(khf, s, ps_bc[:, 128:256]),
                          wait=[(TN, tb2)], inc=SV)
                tb3 = S.t(lambda e, r=rowsb[0:1, 1024 + cc * 128:1024 + (cc + 1) * 128]:
                          e.matmul(ps_bc[:, 256:384], st.ones_row[:, 0:128], r,
                                   start=True, stop=True),
                          wait=[(SV, vk2)], inc=TN)
                vq0 = S.v(lambda e, s=cqh[:, cs]:
                          e.tensor_mul(qpf, s, ps_bc[:, 256:384]),
                          wait=[(TN, tb3)], inc=SV)
                tb3b = S.t(lambda e, r=rowsb[0:1, 2048 + cc * 128:2048 + (cc + 1) * 128]:
                           e.matmul(ps_bc[:, 256:384], st.ones_row[:, 0:128], r,
                                    start=True, stop=True),
                           wait=[(SV, vq0)], inc=TN)
                vq1 = S.v(lambda e:
                          e.tensor_mul(qhf, qpf, ps_bc[:, 256:384]),
                          wait=[(TN, tb3b)], inc=SV)
                sd1 = S.s(lambda e, r=crow[0:1, cs],
                          bia=nrow[0:1, cc * 128 + 127:cc * 128 + 128]:
                          e.activation(kbrow, r, AF.Relu, bias=bia),
                          wait=[(SV, vq1), (SS, SS.count)], inc=SS)
                sd2 = S.s(lambda e: e.activation(kbrow, kbrow, AF.Exp, scale=-1.0),
                          wait=[(SS, sd1)], inc=SS)
                vd1 = S.v(lambda e, r=rnk[0:1, cs]:
                          e.tensor_mul(kbrow, kbrow, r),
                          wait=[(SS, sd2), (SV, SV.count)], inc=SV)
                tb4 = S.t(lambda e:
                          e.matmul(ps_bc[:, 384:512], st.ones_row[:, 0:128], kbrow,
                                   start=True, stop=True),
                          wait=[(SV, vd1)], inc=TN)
                vk3 = S.v(lambda e, s=ckh[:, cs]:
                          e.tensor_mul(kbf, s, ps_bc[:, 384:512]),
                          wait=[(TN, tb4)], inc=SV)
                tkb = S.t(lambda e: e.transpose(ps_tp, kbf, st.ident[:]),
                          wait=[(SV, vk3), (tf, vt_tok)], inc=TN)
                vkb = S.v(lambda e: e.tensor_copy(kbt, ps_tp), wait=[(TN, tkb)], inc=tf)
                tr1 = S.t(lambda e, r=crow[0:1, cs]:
                          e.matmul(ps_kk[:, 0:128], onesrow_f, r, start=True, stop=True),
                          wait=[(SV, SV.count), (TN, TN.count), (SS, SS.count)], inc=TN)
                if _os.environ.get("L1A_DBG", "").startswith("core"):
                    S.v(lambda e: e.tensor_copy(cvt[:, 0:128], ps_kk[:, 0:128]),
                        wait=[(TN, tr1), (SV, SV.count)], inc=SV)
                sx1 = S.s(lambda e, bia=nccol: e.activation(Xl, ps_kk[:, 0:128], AF.Relu,
                                                            scale=1.0, bias=bia),
                          wait=[(TN, tr1), (SS, SS.count), (SV, SV.count)], inc=SS)
                sx2 = S.s(lambda e: e.activation(Xl, Xl, AF.Exp, scale=-1.0),
                          wait=[(SS, sx1)], inc=SS)
                sx3 = S.s(lambda e, bia=ccol: e.activation(Xu, ps_kk[:, 0:128], AF.Relu,
                                                           scale=-1.0, bias=bia),
                          wait=[(SS, sx2)], inc=SS)
                sx4 = S.s(lambda e: e.activation(Xu, Xu, AF.Exp, scale=-1.0),
                          wait=[(SS, sx3)], inc=SS)
                sx5 = S.s(lambda e: e.activation(gcol, ps_kk[:, 127:128], AF.Exp),
                          wait=[(SS, sx4)], inc=SS)
                tkk = S.t(lambda e: e.matmul(ps_qk[:, 0:128], kpf, kpf,
                                             start=True, stop=True),
                          wait=[(SS, sx5), (SV, vk3)], inc=TN)
                tqk = S.t(lambda e: e.matmul(ps_qk[:, 128:256], kpf, qpf,
                                             start=True, stop=True), inc=TN)
                gv1 = S.v(lambda e: e.tensor_mul(Gt, Xl, ps_qk[:, 0:128]),
                          wait=[(TN, tkk), (SV, SV.count)], inc=SV)
                gv2 = S.v(lambda e: e.tensor_mul(Gt, Gt, msl), wait=[(SV, gv1)], inc=SV)
                gv3 = S.v(lambda e, bia=nbcol: e.tensor_scalar_mul(Gt, Gt, bia),
                          wait=[(SV, gv2)], inc=SV)
                tb5 = S.t(lambda e, r=rowsb[0:1, 1536 + cc * 128:1536 + (cc + 1) * 128]:
                          e.matmul(ps_bc[:, 0:128], st.ones_row[:, 0:128], r,
                                   start=True, stop=True),
                          wait=[(SV, gv1)], inc=TN)
                gn1 = S.v(lambda e: e.tensor_mul(Gn, Xu, ps_qk[:, 0:128]),
                          wait=[(TN, tb5), (SV, gv3)], inc=SV)
                gn2 = S.v(lambda e: e.tensor_mul(Gn, Gn, msu), wait=[(SV, gn1)], inc=SV)
                gn3 = S.v(lambda e: e.tensor_mul(Gn, Gn, ps_bc[:, 0:128]),
                          wait=[(SV, gn2)], inc=SV)
                gn4 = S.v(lambda e: e.tensor_scalar_mul(Gn, Gn, -1.0),
                          wait=[(SV, gn3)], inc=SV)
                aq1 = S.v(lambda e: e.tensor_mul(Aqt, Xu, ps_qk[:, 128:256]),
                          wait=[(TN, tqk), (SV, gn4)], inc=SV)
                aq2 = S.v(lambda e: e.tensor_mul(Aqt, Aqt, miu), wait=[(SV, aq1)], inc=SV)
                pv1 = S.v(lambda e: e.tensor_add(Pn, Gn, st.ident[:]),
                          wait=[(SV, gn4)], inc=SV)
                tq1 = S.t(lambda e: e.matmul(ps_d[0][:, 0:128], Gt, Gn,
                                             start=True, stop=True),
                          wait=[(SV, pv1), (SV, aq2)], inc=TN)
                tq2 = S.t(lambda e: e.matmul(ps_d[1][:, 0:128], Gn, Gt,
                                             start=True, stop=True), inc=TN)
                qtokn = S.v(lambda e: e.tensor_copy(Qn, ps_d[0][:, 0:128]),
                            wait=[(TN, tq2)], inc=SV)
                qtokt = S.v(lambda e: e.tensor_copy(Qt, ps_d[1][:, 0:128]),
                            wait=[(TN, tq2)], inc=SV)
                ptok = pv1
                for it in range(6):
                    tqp = S.t(lambda e: e.matmul(ps_d[2][:, 0:128], Qt, Pn,
                                                 start=True, stop=True),
                              wait=[(SV, qtokt), (SV, ptok)], inc=TN)
                    ptok = S.v(lambda e: e.tensor_add(Pn, Pn, ps_d[2][:, 0:128]),
                               wait=[(TN, tqp)], inc=SV)
                    if it < 5:
                        ta = S.t(lambda e: e.matmul(ps_d[0][:, 0:128], Qt, Qn,
                                                    start=True, stop=True),
                                 wait=[(SV, ptok)], inc=TN)
                        tb_ = S.t(lambda e: e.matmul(ps_d[1][:, 0:128], Qn, Qt,
                                                     start=True, stop=True), inc=TN)
                        qtokn = S.v(lambda e: e.tensor_copy(Qn, ps_d[0][:, 0:128]),
                                    wait=[(TN, tb_)], inc=SV)
                        qtokt = S.v(lambda e: e.tensor_copy(Qt, ps_d[1][:, 0:128]),
                                    wait=[(TN, tb_)], inc=SV)
                tks = S.t(lambda e, sb=S_bf[h]: e.matmul(ps_ks[:, 0:256], khf, sb,
                                                         start=True, stop=True),
                          wait=[(SV, ptok), (tf, tf.count)], inc=TN)
                rv1 = S.v(lambda e: e.tensor_sub(Rt, v_t, ps_ks[:, 0:256]),
                          wait=[(TN, tks), (SV, SV.count)], inc=SV)
                rv2 = S.v(lambda e, bc=bcol: e.tensor_scalar_mul(Rt, Rt, bc),
                          wait=[(SV, rv1)], inc=SV)
                td = S.t(lambda e: e.matmul(ps_d[2][:, 0:256], Pn, Rt,
                                            start=True, stop=True),
                         wait=[(SV, rv2)], inc=TN)
                dv = S.v(lambda e: e.tensor_copy(Dl, ps_d[2][:, 0:256]),
                         wait=[(TN, td)], inc=SV)
                to1 = S.t(lambda e: e.matmul(ps_o[:, 0:256], Aqt, Dl,
                                             start=True, stop=False),
                          wait=[(SV, dv)], inc=TN)
                to2 = S.t(lambda e, sb=S_bf[h]: e.matmul(ps_o[:, 0:256], qhf, sb,
                                                         start=False, stop=True), inc=TN)
                ov = S.v(lambda e, d=o_tm[:].rearrange("p (a b) -> p a b", b=1536)
                         [:, cc, h * 256:(h + 1) * 256]:
                         e.tensor_copy(d, ps_o[:, 0:256]), wait=[(TN, to2)], inc=SV)
                tsu = S.t(lambda e: e.matmul(ps_d[0][:, 0:256], kbt, Dl,
                                             start=True, stop=True),
                          wait=[(SV, ov)], inc=TN)
                su1 = S.v(lambda e, d=S_f[h]: e.tensor_scalar_mul(d, d, gcol),
                          wait=[(SS, sx5), (TN, to2), (SV, ov)], inc=SV)
                su2 = S.v(lambda e, d=S_f[h]: e.tensor_add(d, d, ps_d[0][:, 0:256]),
                          wait=[(TN, tsu), (SV, su1)], inc=SV)
                S.v(lambda e, d=S_bf[h], s=S_f[h]: e.tensor_copy(d, s),
                    wait=[(SV, su2)], inc=df)
                if _os.environ.get("L1A_DBG") == f"core{grp}_{h}_{cc}":
                    DB4 = S.sem("dbg4")
                    for nmb2, off, wd in ((cvt[:, 0:128], 2200, 128), (nc_tm, 2328, 24),
                                          (c_tm, 2048, 24), (b_tm, 2100, 24),
                                          (Xl, 0, 128), (Xu, 128, 128), (Gt, 256, 128),
                                          (Gn, 384, 128), (Pn, 512, 128), (Aqt, 640, 128),
                                          (Rt, 768, 256), (Dl, 1024, 256), (kpf, 1280, 128),
                                          (qpf, 1408, 128), (khf, 1536, 128), (v_t, 1664, 256),
                                          (kbt, 1920, 128)):
                        S.g(lambda e, d=dram["dbg"][:, off:off + wd], s=nmb2:
                            e.dma_start(out=d, in_=s), wait=[(df, df.count)], inc=DB4, incn=16)
                    S.waitonly("vector", [(DB4, DB4.count)])
                    S.waitonly("scalar", [(DB4, DB4.count)])
                    S.waitonly("tensor", [(DB4, DB4.count)])
        delta_tok = df.count
        if _os.environ.get("L1A_DBG") == f"otm{grp}":
            DB3 = S.sem(f"dbg3{grp}")
            S.g(lambda e, d=dram["dbg"][:, 0:6144], s=o_tm[:]:
                e.dma_start(out=d, in_=s), wait=[(df, delta_tok)], inc=DB3, incn=16)
            S.waitonly("vector", [(DB3, DB3.count)])

        S.sy(lambda e, d=wbuf1[:].rearrange("p (c o) -> p c o", c=8),
             s=cache["l1_gT"][:].rearrange("(c p) o -> p c o", p=128):
             e.dma_start(out=d, in_=s),
             wait=[(df, delta_tok), (TN, TN.count)], inc=WS1, incn=16)
        S.sy(lambda e, d=wbuf2[:].rearrange("p (c o) -> p c o", c=12),
             s=cache["l1_oT"][:].rearrange("(c p) o -> p c o", p=128):
             e.dma_start(out=d, in_=s),
             wait=[(df, delta_tok), (TN, TN.count)], inc=WS2, incn=16)
        go_rdy1 = WS1.count
        go_rdy2 = WS2.count

        for cc in range(4):
            gt = t0 + cc
            for ocg in range(3):
                psb = ps_pj[ocg % 2]
                for ic in range(8):
                    w = []
                    if ic == 0:
                        w = [(TN, TN.count), (df, delta_tok), (SV, SV.count),
                             (SS, SS.count)]
                        if cc == 0 and ocg == 0:
                            w += [(WS1, go_rdy1), (WS2, go_rdy2)]
                    S.t(lambda e, p2=psb,
                        lhs=x2fm[:].rearrange("p (a b) -> p a b", b=512)
                        [:, ic, cc * 128:(cc + 1) * 128],
                        wsl=wbuf1[:, ic * 1536 + ocg * 512: ic * 1536 + (ocg + 1) * 512],
                        s0=(ic == 0), s1=(ic == 7):
                        e.matmul(p2[:], lhs, wsl, start=s0, stop=s1),
                        wait=w, inc=(TN if ic == 7 else None))
                tkg = TN.count
                S.s(lambda e, p2=psb, d=gate_tm[:, ocg * 512:(ocg + 1) * 512]:
                    e.activation(d, p2[:], AF.Silu),
                    wait=[(TN, tkg), (SS, SS.count)], inc=SS)
            gate_tok = SS.count
            og = o_tm[:].rearrange("p (a b) -> p a b", b=1536)[:, cc, :]
            m1 = S.v(lambda e, d=og: e.tensor_mul(d, d, gate_tm),
                     wait=[(SS, gate_tok), (SV, SV.count)], inc=SV)
            m2 = S.s(lambda e, d=cvh[:, 0:1024], s=og[:, 0:1024]:
                     e.activation(d, s, AF.Square, accum_out=ln_st[:, 1:2]),
                     wait=[(SV, m1), (SS, SS.count)], inc=SS)
            m2b = S.s(lambda e, d=cvh[:, 0:512], s=og[:, 1024:1536]:
                      e.activation(d, s, AF.Square, accum_out=ln_st[:, 4:5]),
                      wait=[(SS, m2)], inc=SS)
            m3 = S.v(lambda e: e.tensor_scalar(ln_st[:, 2:3], ln_st[:, 1:2],
                                               ln_st[:, 4:5], None, ALU.add),
                     wait=[(SS, m2b), (SV, SV.count)], inc=SV)
            m3b = S.v(lambda e: e.tensor_scalar(ln_st[:, 2:3], ln_st[:, 2:3],
                                                1.0 / 1536, None, ALU.mult),
                      wait=[(SV, m3)], inc=SV)
            m4 = S.s(lambda e: e.activation(ln_st[:, 3:4], ln_st[:, 2:3], AF.Sqrt,
                                            bias=st.c_eps_ln),
                     wait=[(SV, m3b)], inc=SS)
            m5 = S.v(lambda e: e.reciprocal(ln_st[:, 3:4], ln_st[:, 3:4]),
                     wait=[(SS, m4)], inc=SV)
            m6 = S.v(lambda e, d=og: e.tensor_scalar_mul(d, d, ln_st[:, 3:4]),
                     wait=[(SV, m5)], inc=SV)
            for ic in range(12):
                tko = S.t(lambda e, sl=og[:, ic * 128:(ic + 1) * 128]:
                          e.transpose(ps_tp, sl, st.ident[:]),
                          wait=[(SV, m6), (tf, tf.count)], inc=TN)
                S.v(lambda e, d=ogf[:, ic * 128:(ic + 1) * 128]:
                    e.tensor_copy(d, ps_tp), wait=[(TN, tko)], inc=tf)
            og_tok = tf.count
            for oc2 in range(2):
                psb = ps_pj[oc2]
                for ic in range(12):
                    w = [(tf, og_tok)] if ic == 0 else []
                    S.t(lambda e, p2=psb, lhs=ogf[:, ic * 128:(ic + 1) * 128],
                        wsl=wbuf2[:, ic * 1024 + oc2 * 512: ic * 1024 + (oc2 + 1) * 512],
                        s0=(ic == 0), s1=(ic == 11):
                        e.matmul(p2[:], lhs, wsl, start=s0, stop=s1),
                        wait=w, inc=(TN if ic == 11 else None))
            tko2 = TN.count
            wprev = [st.tile_tok[gt - 1]] if gt > 0 else []
            rv1 = S.v(lambda e, p2=ps_pj[0], d=rres[:, 0:512],
                      s=st.stream[:, gt * D: gt * D + 512]:
                      e.tensor_add(d, p2[:], s),
                      wait=[(TN, tko2), (SS, SS.count), (SV, SV.count)] + wprev, inc=SV)
            S.v(lambda e, p2=ps_pj[1], d=rres[:, 512:1024],
                s=st.stream[:, gt * D + 512: gt * D + 1024]:
                e.tensor_add(d, p2[:], s), wait=[(SV, rv1)], inc=SV)
            res_tok = SV.count
            st.tile_tok[gt] = emit_ln(nc, es, S, st, rres, ln_sq, ln_st,
                                      st.stream[:, gt * D:(gt + 1) * D],
                                      wait=[(SV, res_tok)])

    st.l1a_end = {"TN": TN.count}
    sc.close()


def emit_l1_ffn(nc, es, S, st, final_out=False):
    nt = st.nt
    dram = st.dram
    SV, SS, TN = st.SV, st.SS, st.TN
    B = st.banks
    phase_barrier(S, st)
    sc = Scope(st.arena)

    cache = {
        "g": nc.dram_tensor("c_wgq", [1024, 2816], BF16, kind="Internal").ap(),
        "u": nc.dram_tensor("c_wuq", [1024, 2816], BF16, kind="Internal").ap(),
        "d": nc.dram_tensor("c_wdq", [2816, 1024], BF16, kind="Internal").ap(),
    }

    def emit_quant2(xin, xout, nf, qu, qr, qm, bmbuf, tbuf, wait0, dbg_tag=None):
        nb = nf // 16
        a1 = S.v(lambda e: e.reduce_max(
            bmbuf.rearrange("p (a b) -> p a b", b=1),
            xin.rearrange("p (a b) -> p a b", b=16), axis=AX.X,
            apply_absolute_value=True), wait=wait0 + [(SV, SV.count)], inc=SV)
        a2 = S.v(lambda e: e.tensor_scalar(tbuf, bmbuf, 1.0 / 12.0, 5e-13,
                                           ALU.mult, ALU.max), wait=[(SV, a1)], inc=SV)
        a2b = S.v(lambda e: e.tensor_scalar_mul(bmbuf, tbuf, 1.0000005),
                  wait=[(SV, a2)], inc=SV)
        a3 = S.v(lambda e: e.reciprocal(bmbuf, bmbuf), wait=[(SV, a2b)], inc=SV)
        a4 = S.s(lambda e: e.activation(qu, xin, AF.Abs),
                 wait=wait0 + [(SS, SS.count), (SV, a1)], inc=SS)
        a5 = S.v(lambda e, bb=bmbuf.rearrange("p (a b) -> p a b", b=1)
                 .broadcast_to([128, nb, 16]):
                 e.tensor_tensor(qu.rearrange("p (a b) -> p a b", b=16),
                                 qu.rearrange("p (a b) -> p a b", b=16),
                                 bb, ALU.mult),
                 wait=[(SV, a3), (SS, a4)], inc=SV)
        import os as _os
        if dbg_tag and _os.environ.get("L1F_DBG") == dbg_tag:
            DQ1 = S.sem("dbq1")
            S.g(lambda e, d=dram["dbg"][:, 4096:4096 + nf], s=qu:
                e.dma_start(out=d, in_=s), wait=[(SV, a5)], inc=DQ1, incn=16)
            S.g(lambda e, d=dram["dbg"][:, 6656:6656 + min(nf, 1024)], s=xin[:, 0:min(nf, 1024)]:
                e.dma_start(out=d, in_=s), wait=[(SV, a5)], inc=DQ1, incn=16)
            S.g(lambda e, d=dram["dbg"][0:128, 7680:7680 + nf // 16], s=bmbuf:
                e.dma_start(out=d, in_=s), wait=[(SV, a2)], inc=DQ1, incn=16)
            S.waitonly("vector", [(DQ1, DQ1.count)])
            S.waitonly("scalar", [(DQ1, DQ1.count)])
        a6 = S.v(lambda e: e.tensor_scalar_max(qr, qu, 1.0), wait=[(SV, a5)], inc=SV)
        a7 = S.v(lambda e: e.tensor_scalar_add(qr.bitcast(I32), qr.bitcast(I32),
                                               2097151), wait=[(SV, a6)], inc=SV)
        a8 = S.v(lambda e: e.tensor_scalar(qr.bitcast(I32), qr.bitcast(I32),
                                           -4194304, None, ALU.bitwise_and),
                 wait=[(SV, a7)], inc=SV)
        aD1 = S.v(lambda e: e.tensor_scalar(qm, qu, 1.5, 0.5, ALU.is_gt, ALU.subtract),
                  wait=[(SV, a8)], inc=SV)
        aD2 = S.v(lambda e: e.scalar_tensor_tensor(qm, qr, 1.5, qm,
                                                   ALU.is_equal, ALU.mult),
                  wait=[(SV, aD1)], inc=SV)
        a8 = S.v(lambda e: e.tensor_add(qr, qr, qm), wait=[(SV, aD2)], inc=SV)
        if dbg_tag and _os.environ.get("L1F_DBG") == dbg_tag:
            DQ2 = S.sem("dbq2")
            S.g(lambda e, d=dram["dbg"][:, 5632:5632 + nf], s=qr:
                e.dma_start(out=d, in_=s), wait=[(SV, a8)], inc=DQ2, incn=16)
            S.waitonly("vector", [(DQ2, DQ2.count)])
        a9 = S.v(lambda e: e.scalar_tensor_tensor(qr, qu, 0.5, qr,
                                                  ALU.is_gt, ALU.mult),
                 wait=[(SV, a8)], inc=SV)
        aA = S.s(lambda e: e.activation(qu, xin, AF.Sign),
                 wait=[(SV, a9), (SS, SS.count)], inc=SS)
        aB = S.v(lambda e: e.tensor_mul(qr, qr, qu), wait=[(SS, aA)], inc=SV)
        aC = S.v(lambda e, bb=tbuf.rearrange("p (a b) -> p a b", b=1)
                 .broadcast_to([128, nb, 16]):
                 e.tensor_tensor(xout.rearrange("p (a b) -> p a b", b=16),
                                 qr.rearrange("p (a b) -> p a b", b=16),
                                 bb, ALU.mult),
                 wait=[(SV, aB)], inc=SV)
        return aC

    WL = S.sem("f_wl")
    CSQ = S.sem("f_csq")
    HSQ = S.sem("f_hsq")
    TFQ = S.sem("f_tfq")
    psc = Scope(st.arena)
    wstg = psc.f32(2816)
    wbf = psc.bf16(2816)
    tstg = psc.bf16(22 * 128)
    qu = sc.f32(2816)
    qr = sc.f32(2816)
    qm = sc.f32(2816)
    qbm = sc.f32(2816 // 16)
    qt = sc.f32(2816 // 16)
    ps_tp = st.tp_bank[:, 0:128]

    for wnm, dnm, rows, cols in (("l1_wgate", "g", 22, 1024),
                                 ("l1_wup", "u", 22, 1024),
                                 ("l1_wdown", "d", 8, 2816)):
        npc = cols // 128
        for rc in range(rows):
            S.sy(lambda e, d=wstg[:, 0:cols],
                 s=dram[wnm][rc * 128:(rc + 1) * 128, :]:
                 e.dma_start(out=d, in_=s),
                 wait=[(CSQ, CSQ.count)] + ([st.tile_tok[nt - 1]] if rc == 0 and wnm == "l1_wgate" else []),
                 inc=WL, incn=16)
            qtok = emit_quant2(wstg[:, 0:cols], wbf[:, 0:cols], cols,
                               qu[:, 0:cols], qr[:, 0:cols], qm[:, 0:cols],
                               qbm[:, 0:cols // 16],
                               qt[:, 0:cols // 16], wait0=[(WL, WL.count)])
            for pc in range(npc):
                tk = S.t(lambda e, sl=wbf[:, pc * 128:(pc + 1) * 128]:
                         e.transpose(ps_tp, sl, st.ident[:]),
                         wait=[(SV, qtok), (TFQ, TFQ.count)] + [(TN, TN.count)], inc=TN)
                S.v(lambda e, d=tstg[:, pc * 128:(pc + 1) * 128]:
                    e.tensor_copy(d, ps_tp), wait=[(TN, tk)], inc=TFQ)
            st_tok = TFQ.count
            S.sy(lambda e, d=cache[dnm].rearrange("(c p) o -> p c o", p=128)
                 [:, :, rc * 128:(rc + 1) * 128],
                 s=tstg[:, 0:npc * 128].rearrange("p (c o) -> p c o", c=npc):
                 e.dma_start(out=d, in_=s),
                 wait=[(TFQ, st_tok)], inc=HSQ, incn=16)
            S.waitonly("vector", [(HSQ, HSQ.count)])
    cache_tok = HSQ.count
    S.waitonly("sync", [(HSQ, cache_tok)])
    import os as _os
    if _os.environ.get("L1F_DBG") == "wg":
        DW = S.sem("dbgw")
        S.g(lambda e, d=dram["dbg"][:, 0:2816],
            s=cache["g"][0:128, :]: e.dma_start(out=d, in_=s),
            wait=[(HSQ, cache_tok)], inc=DW, incn=16)
        S.g(lambda e, d=dram["dbg"][:, 2816:5632],
            s=cache["g"].rearrange("(c p) o -> p c o", p=128)[:, 1, :]:
            e.dma_start(out=d, in_=s), inc=DW, incn=16)
        S.waitonly("sync", [(DW, DW.count)])
    psc.close()

    xq = sc.bf16(1024)
    xqfm = sc.bf16(4 * 1024)
    hbuf = sc.f32(4 * 2816)
    hqfm = sc.bf16(2816)
    wg1 = sc.bf16(8 * 512)
    wu1 = sc.bf16(8 * 512)
    wd1 = sc.bf16(22 * 256)
    wd2 = wd1
    t1sb = st.ldbuf[:, 0:512]
    ln_sq = st.ldbuf[:, 1024:2048]
    ln_st = sc.f32(8)
    rres = sc.f32(1024)
    outf = sc.f32(1024) if final_out else None

    WG = [S.sem("f_wg0"), S.sem("f_wg1")]
    WD = [S.sem("f_wd0"), S.sem("f_wd1")]
    qf = S.sem("f_qf")
    tf = S.sem("f_tf")
    hf = S.sem("f_hf")
    OF = S.sem("f_of")

    ps_g = B[1]
    ps_u = B[2]
    ps_d2 = [B[3], B[4]]

    NSUP = nt // 4
    ocg_sizes = [512, 512, 512, 512, 512, 256]
    for sp in range(NSUP):
        t0 = sp * 4
        for tt in range(4):
            gt = t0 + tt
            qtok = emit_quant2(st.stream[:, gt * D:(gt + 1) * D], xq[:],
                               D, qu[:, 0:D], qr[:, 0:D], qm[:, 0:D],
                               qbm[:, 0:64], qt[:, 0:64],
                               wait0=[st.tile_tok[gt], (HSQ, cache_tok), (TN, TN.count)],
                               dbg_tag=f"xq{sp}_{tt}")
            import os as _os
            if _os.environ.get("L1F_DBG") == f"xq{sp}_{tt}":
                DBX = S.sem(f"dbgx{sp}{tt}")
                S.g(lambda e, d=dram["dbg"][:, 0:1024], s=xq[:]:
                    e.dma_start(out=d, in_=s), wait=[(SV, qtok)], inc=DBX, incn=16)
                S.g(lambda e, d=dram["dbg"][:, 1024:2048], s=qu[:, 0:1024]:
                    e.dma_start(out=d, in_=s), wait=[(SV, qtok)], inc=DBX, incn=16)
                S.g(lambda e, d=dram["dbg"][:, 2048:3072], s=qr[:, 0:1024]:
                    e.dma_start(out=d, in_=s), wait=[(SV, qtok)], inc=DBX, incn=16)
                S.g(lambda e, d=dram["dbg"][:, 3072:3136], s=qbm[:, 0:64]:
                    e.dma_start(out=d, in_=s), wait=[(SV, qtok)], inc=DBX, incn=16)
                S.g(lambda e, d=dram["dbg"][:, 3136:3200], s=qt[:, 0:64]:
                    e.dma_start(out=d, in_=s), wait=[(SV, qtok)], inc=DBX, incn=16)
                S.waitonly("vector", [(DBX, DBX.count)])
                S.waitonly("tensor", [(DBX, DBX.count)])
            for c in range(8):
                tk = S.t(lambda e, sl=xq[:, c * 128:(c + 1) * 128]:
                         e.transpose(ps_tp, sl, st.ident[:]),
                         wait=[(SV, qtok), (tf, tf.count), (TN, TN.count)], inc=TN)
                S.v(lambda e, d=xqfm[:, tt * D + c * 128: tt * D + (c + 1) * 128]:
                    e.tensor_copy(d, ps_tp), wait=[(TN, tk)], inc=tf)
        xq_tok = tf.count

        off = 0
        for ocg, osz in enumerate(ocg_sizes):
            sel = 0
            wgb = wg1
            wub = wu1
            S.sy(lambda e, d=wgb[:, 0:8 * osz].rearrange("p (c o) -> p c o", c=8),
                 s=cache["g"].rearrange("(c p) o -> p c o", p=128)[:, :, off:off + osz]:
                 e.dma_start(out=d, in_=s),
                 wait=[(HSQ, cache_tok), (TN, TN.count)], inc=WG[0], incn=16)
            S.sy(lambda e, d=wub[:, 0:8 * osz].rearrange("p (c o) -> p c o", c=8),
                 s=cache["u"].rearrange("(c p) o -> p c o", p=128)[:, :, off:off + osz]:
                 e.dma_start(out=d, in_=s),
                 wait=[(TN, TN.count)], inc=WG[0], incn=16)
            wrdy = WG[0].count
            for tt in range(4):
                for which, wb in ((0, wgb), (1, wub)):
                    psb = ps_g if which == 0 else ps_u
                    for ic in range(8):
                        w = []
                        if ic == 0:
                            w = [(hf, hf.count)]
                            if tt == 0 and which == 0:
                                w += [(WG[0], wrdy), (tf, xq_tok)]
                        S.t(lambda e, p2=psb[:, 0:osz],
                            lhs=xqfm[:, tt * D + ic * 128: tt * D + (ic + 1) * 128],
                            wsl=wb[:, ic * osz:(ic + 1) * osz],
                            s0=(ic == 0), s1=(ic == 7):
                            e.matmul(p2, lhs, wsl, start=s0, stop=s1),
                            wait=w, inc=(TN if ic == 7 else None))
                tkgu = TN.count
                sg = S.s(lambda e, p2=ps_g[:, 0:osz], d=t1sb[:, 0:osz]:
                         e.activation(d, p2, AF.Silu),
                         wait=[(TN, tkgu), (SS, SS.count)], inc=SS)
                S.v(lambda e, p2=ps_u[:, 0:osz],
                    d=hbuf[:, tt * 2816 + off: tt * 2816 + off + osz], s=t1sb[:, 0:osz]:
                    e.tensor_mul(d, s, p2),
                    wait=[(SS, sg), (SV, SV.count)], inc=hf)
            off += osz
        h_tok = hf.count

        for tt in range(4):
            gt = t0 + tt
            if _os.environ.get("L1F_DBG") == f"h{sp}_{tt}":
                DBH2 = S.sem(f"dbgh2{sp}{tt}")
                S.g(lambda e, d=dram["dbg"][:, 0:2816], s=hbuf[:, tt * 2816:(tt + 1) * 2816]:
                    e.dma_start(out=d, in_=s), wait=[(hf, h_tok)], inc=DBH2, incn=16)
                S.waitonly("vector", [(DBH2, DBH2.count)])
                S.waitonly("tensor", [(DBH2, DBH2.count)])
            hqv = hbuf[:, tt * 2816:(tt + 1) * 2816].bitcast(BF16)[:, 0:2816]
            qtok = emit_quant2(hbuf[:, tt * 2816:(tt + 1) * 2816],
                               hqv, 2816,
                               qu[:, 0:2816], qr[:, 0:2816], qm[:, 0:2816],
                               qbm[:, 0:176], qt[:, 0:176],
                               wait0=[(hf, h_tok)])
            if _os.environ.get("L1F_DBG") == f"hq{sp}_{tt}":
                DBH = S.sem(f"dbgh{sp}{tt}")
                S.g(lambda e, d=dram["dbg"][:, 0:2816], s=hqv:
                    e.dma_start(out=d, in_=s), wait=[(SV, qtok)], inc=DBH, incn=16)
                S.waitonly("vector", [(DBH, DBH.count)])
                S.waitonly("tensor", [(DBH, DBH.count)])
            for c in range(22):
                tk = S.t(lambda e, sl=hqv[:, c * 128:(c + 1) * 128]:
                         e.transpose(ps_tp, sl, st.ident[:]),
                         wait=[(SV, qtok), (tf, tf.count), (TN, TN.count)], inc=TN)
                S.v(lambda e, d=hqfm[:, c * 128:(c + 1) * 128]:
                    e.tensor_copy(d, ps_tp), wait=[(TN, tk)], inc=tf)
            hq_tok = tf.count
            for oc2 in range(4):
                sel = 0
                wdb = wd1
                psb = ps_d2[oc2 // 2]
                S.sy(lambda e, d=wdb[:].rearrange("p (c o) -> p c o", c=22),
                     s=cache["d"].rearrange("(c p) o -> p c o", p=128)
                     [:, :, oc2 * 256:(oc2 + 1) * 256]:
                     e.dma_start(out=d, in_=s),
                     wait=[(HSQ, cache_tok), (TN, TN.count), (OF, OF.count)],
                     inc=WD[sel], incn=16)
                wd_tok = WD[sel].count
                for ic in range(22):
                    w = []
                    if ic == 0:
                        w = [(OF, OF.count), (WD[sel], wd_tok), (tf, hq_tok)]
                    S.t(lambda e, p2=psb[:, (oc2 % 2) * 256:(oc2 % 2) * 256 + 256],
                        lhs=hqfm[:, ic * 128:(ic + 1) * 128],
                        wsl=wdb[:, ic * 256:(ic + 1) * 256],
                        s0=(ic == 0), s1=(ic == 21):
                        e.matmul(p2, lhs, wsl, start=s0, stop=s1),
                        wait=w, inc=(TN if ic == 21 else None))
            tkd = TN.count
            wprev = [st.tile_tok[gt - 1]] if gt > 0 else []
            rv1 = S.v(lambda e, p2=ps_d2[0], d=rres[:, 0:512],
                      s=st.stream[:, gt * D: gt * D + 512]:
                      e.tensor_add(d, p2[:], s),
                      wait=[(TN, tkd), (SS, SS.count), (SV, SV.count)] + wprev, inc=SV)
            S.v(lambda e, p2=ps_d2[1], d=rres[:, 512:1024],
                s=st.stream[:, gt * D + 512: gt * D + 1024]:
                e.tensor_add(d, p2[:], s), wait=[(SV, rv1)], inc=OF)
            res_tok = OF.count
            if final_out:
                st.tile_tok[gt] = emit_ln(nc, es, S, st, rres, ln_sq, ln_st,
                                          outf[:], wait=[(OF, res_tok)])
                S.sy(lambda e, d=dram["out"][gt * 128:(gt + 1) * 128, :], s=outf[:]:
                     e.dma_start(out=d, in_=s),
                     wait=[st.tile_tok[gt]], inc=st.ST, incn=16)
                S.waitonly("vector", [(st.ST, st.ST.count)])
            else:
                st.tile_tok[gt] = emit_ln(nc, es, S, st, rres, ln_sq, ln_st,
                                          st.stream[:, gt * D:(gt + 1) * D],
                                          wait=[(OF, res_tok)])
    sc.close()



def kernel(**inputs):
    x = np.ascontiguousarray(np.asarray(inputs["L0_attn_query"], np.float32))
    arrays = host_prep(inputs)
    nc, need = build_program(arrays, ("l0a", "l0f", "l1a", "l1f"), nt=32,
                             stream_in="x", final_out=True)
    in_maps = []
    for b in range(NCORES):
        m = {"x": np.ascontiguousarray(x[b])}
        for nm in need:
            m[nm] = arrays[nm]
        in_maps.append(m)
    res = run_bass_kernel_spmd(nc, in_maps, core_ids=list(range(NCORES)))
    return np.stack([r["out"] for r in res.results], axis=0)


# version 02
# speedup vs baseline: 1.0641x; 1.0641x over previous
import numpy as np
from contextlib import ExitStack
import concourse.bass as bass
import concourse.mybir as mybir
from concourse.bass_utils import run_bass_kernel_spmd

F32 = mybir.dt.float32
BF16 = mybir.dt.bfloat16
I16 = mybir.dt.int16
I32 = mybir.dt.int32
AF = mybir.ActivationFunctionType
ALU = mybir.AluOpType
AX = mybir.AxisListType

D = 1024
NCORES = 8


class Sem:
    def __init__(self, h, name):
        self.h = h
        self.name = name
        self.count = 0


class Sched:
    ENGINES = ("sync", "vector", "tensor", "scalar", "gpsimd")

    def __init__(self, nc, es):
        self.nc = nc
        self.es = es
        self.q = {e: [] for e in self.ENGINES}
        self.scope = None

    def sem(self, name):
        h = self.es.enter_context(self.nc.semaphore(name))
        return Sem(h, name)

    def emit(self, eng, fn, wait=(), inc=None, incn=1):
        tok = None
        if inc is not None:
            inc.count += incn
            tok = inc.count
        self.q[eng].append((fn, tuple(wait), inc, incn, self.scope))
        return tok

    def sy(self, fn, **kw): return self.emit("sync", fn, **kw)
    def v(self, fn, **kw): return self.emit("vector", fn, **kw)
    def t(self, fn, **kw): return self.emit("tensor", fn, **kw)
    def s(self, fn, **kw): return self.emit("scalar", fn, **kw)
    def g(self, fn, **kw): return self.emit("gpsimd", fn, **kw)

    def waitonly(self, eng, wait):
        self.q[eng].append((None, tuple(wait), None, 0, self.scope))

    def replay(self, eng_name, eng):
        cur = None
        sid = None
        for fn, wait, inc, incn, scope in self.q[eng_name]:
            if scope != cur:
                if cur is not None:
                    self.nc.leave_named_scope(cur, sid, False)
                cur, sid = scope, None
                if scope is not None:
                    sid, _ = self.nc.enter_named_scope(scope, False)
            for (sem, val) in wait:
                eng.wait_ge(sem.h, val)
            if fn is not None:
                instr = fn(eng)
                if inc is not None:
                    instr.then_inc(inc.h, incn)
        if cur is not None:
            self.nc.leave_named_scope(cur, sid, False)

    def build_blocks(self, block):
        sched = self

        @block.sync
        def _(sync):
            sched.replay("sync", sync)

        @block.vector
        def _(vector):
            sched.replay("vector", vector)

        @block.tensor
        def _(tensor):
            sched.replay("tensor", tensor)

        @block.scalar
        def _(scalar):
            sched.replay("scalar", scalar)

        @block.gpsimd
        def _(gpsimd):
            sched.replay("gpsimd", gpsimd)


class Arena:

    def __init__(self, nc, es, nbytes):
        self.t_bf = _sb(nc, es, "arena", [128, nbytes // 2], BF16)
        self.t_f32 = self.t_bf[:].bitcast(F32)
        self.n = nbytes
        self.free_list = [(0, nbytes)]

    def alloc(self, nbytes, align=4):
        nbytes = -(-nbytes // align) * align
        for i, (off, sz) in enumerate(self.free_list):
            aoff = -(-off // align) * align
            if aoff + nbytes <= off + sz:
                pre = (off, aoff - off)
                post = (aoff + nbytes, off + sz - aoff - nbytes)
                repl = [p for p in (pre, post) if p[1] > 0]
                self.free_list[i:i + 1] = repl
                return aoff
        raise MemoryError(f"arena: cannot alloc {nbytes}B; free={self.free_list}")

    def free(self, off, nbytes, align=4):
        nbytes = -(-nbytes // align) * align
        self.free_list.append((off, nbytes))
        self.free_list.sort()
        merged = []
        for o, s in self.free_list:
            if merged and merged[-1][0] + merged[-1][1] == o:
                merged[-1] = (merged[-1][0], merged[-1][1] + s)
            else:
                merged.append((o, s))
        self.free_list = [tuple(m) for m in merged]

    def f32(self, np_, nfree):
        off = self.alloc(np_ * 0 + nfree * 4)
        return self.t_f32[:, off // 4: off // 4 + nfree], (off, nfree * 4)

    def bf16(self, np_, nfree):
        off = self.alloc(nfree * 2)
        return self.t_bf[:, off // 2: off // 2 + nfree], (off, nfree * 2)


class Scope:

    def __init__(self, arena):
        self.a = arena
        self.allocs = []

    def f32(self, nfree, rows=128):
        v, rec = self.a.f32(rows, nfree)
        self.allocs.append(rec)
        return v[0:rows, :]

    def bf16(self, nfree, rows=128):
        v, rec = self.a.bf16(rows, nfree)
        self.allocs.append(rec)
        return v[0:rows, :]

    def close(self):
        for off, sz in self.allocs:
            self.a.free(off, sz)
        self.allocs = []



def _blk(wT, ob=128):
    I, O = wT.shape
    return np.ascontiguousarray(wT.reshape(I, O // ob, ob).transpose(1, 0, 2))


def _colchunks(v):
    return np.ascontiguousarray(v.reshape(-1, 128).T)


def host_prep(inputs):
    f32 = np.float32
    a = {}
    for nm in ("Wq", "Wk", "Wv", "Wo"):
        a[f"l0_{nm}T"] = np.ascontiguousarray(inputs[f"L0_attn_{nm}"].T.astype(f32))
    a["l0_bq_cols"] = _colchunks(inputs["L0_attn_bq"].astype(f32))
    a["l0_bkv_row"] = np.concatenate(
        [inputs["L0_attn_bk"], inputs["L0_attn_bv"]]).astype(f32).reshape(1, 2 * D)
    a["l0_bo_row"] = inputs["L0_attn_bo"].astype(f32).reshape(1, D)
    a["l0_w1T"] = _blk(np.ascontiguousarray(inputs["L0_ffn_w1_w"].T.astype(f32)))
    a["l0_w2T"] = _blk(np.ascontiguousarray(inputs["L0_ffn_w2_w"].T.astype(f32)))
    a["l0_h1T"] = _blk(np.ascontiguousarray(inputs["L0_ffn_h1_w"].T.astype(f32)))
    a["l0_h2T"] = _blk(np.ascontiguousarray(inputs["L0_ffn_h2_w"].T.astype(f32)))
    a["l0_w3T"] = _blk(np.ascontiguousarray(inputs["L0_ffn_w3_w"].T.astype(f32)))
    for nm in ("w1_s", "w2_s", "h1_s", "h2_s"):
        a[f"l0_{nm}_cols"] = _colchunks(inputs[f"L0_ffn_{nm}"].astype(f32))
    a["l0_w3_s_cols"] = _colchunks(inputs["L0_ffn_w3_s"].astype(f32))
    a["l1_qT"] = np.ascontiguousarray(inputs["L1_attn_q_w"].T.astype(f32))
    a["l1_kT"] = np.ascontiguousarray(inputs["L1_attn_k_w"].T.astype(f32))
    a["l1_vT"] = np.ascontiguousarray(inputs["L1_attn_v_w"].T.astype(f32))
    a["l1_gT"] = np.ascontiguousarray(inputs["L1_attn_g_w"].T.astype(f32))
    aT32 = np.zeros((1024, 32), f32)
    aT32[:, 0:6] = inputs["L1_attn_a_w"].T.astype(f32)
    bT32 = np.zeros((1024, 32), f32)
    bT32[:, 0:6] = inputs["L1_attn_b_w"].T.astype(f32)
    a["l1_aT32"] = np.ascontiguousarray(aT32)
    a["l1_bT32"] = np.ascontiguousarray(bT32)
    a["l1_oT"] = np.ascontiguousarray(inputs["L1_attn_o_w"].T.astype(f32))
    a["l1_convq_cols"] = np.ascontiguousarray(
        inputs["L1_attn_conv_q_w"].astype(f32).reshape(6, 128, 4).transpose(1, 0, 2).reshape(128, 24))
    a["l1_convk_cols"] = np.ascontiguousarray(
        inputs["L1_attn_conv_k_w"].astype(f32).reshape(6, 128, 4).transpose(1, 0, 2).reshape(128, 24))
    a["l1_convv_cols"] = np.ascontiguousarray(
        inputs["L1_attn_conv_v_w"].astype(f32).reshape(12, 128, 4).transpose(1, 0, 2).reshape(128, 48))
    a["l1_Alog_row"] = inputs["L1_attn_A_log"].astype(f32).reshape(1, 6)
    a["l1_dtb_row"] = inputs["L1_attn_dt_bias"].astype(f32).reshape(1, 6)
    a["l1_nw_cols"] = _colchunks(inputs["L1_attn_norm_weight"].astype(f32))
    a["l1_wgate"] = np.ascontiguousarray(inputs["L1_ffn_w_gate"].astype(f32))
    a["l1_wup"] = np.ascontiguousarray(inputs["L1_ffn_w_up"].astype(f32))
    a["l1_wdown"] = np.ascontiguousarray(inputs["L1_ffn_w_down"].astype(f32))
    a["identity"] = np.eye(128, dtype=f32)
    a["ones_row"] = np.ones((1, 512), f32)
    a["tri_incl"] = np.ascontiguousarray(np.triu(np.ones((128, 128), f32)))
    a["mask_strict_low"] = np.ascontiguousarray(np.tril(np.ones((128, 128), f32), k=-1))
    bsel = np.zeros((16, 8 * 128), f32)
    for c in range(8):
        bsel[2 * c, c * 128:c * 128 + 64] = 1.0
        bsel[2 * c + 1, c * 128 + 64:c * 128 + 128] = 1.0
    a["bsel"] = np.ascontiguousarray(bsel)
    consts = [1e-5, 1e-6, 1.0 / D, -1.0, float(np.log(128.0 ** -0.5))]
    a["constbank"] = np.ascontiguousarray(
        np.tile(np.array(consts, f32)[None, :], (128, 1)))
    return a



class St:
    pass


def _sb(nc, es, name, shape, dt=F32):
    return es.enter_context(nc.sbuf_tensor(name, shape, dt))


ARENA_BYTES = 134 * 1024


def build_program(host_arrays, phases, nt=32, stream_in="x", stream_out=None,
                  final_out=False):
    nc = bass.Bass("TRN2", target_bir_lowering=False)
    T = nt * 128

    dram = {}
    if stream_in == "x":
        dram["x"] = nc.dram_tensor("x", [T, D], F32, kind="ExternalInput").ap()
    else:
        dram["stream_in"] = nc.dram_tensor("stream_in", [T, D], BF16,
                                           kind="ExternalInput").ap()
    if stream_out == "stream":
        dram["stream_out"] = nc.dram_tensor("stream_out", [T, D], BF16,
                                            kind="ExternalOutput").ap()
    if final_out:
        dram["out"] = nc.dram_tensor("out", [T, D], F32, kind="ExternalOutput").ap()
    import os as _os
    if _os.environ.get("L1A_DBG") or _os.environ.get("L1F_DBG"):
        dram["dbg"] = nc.dram_tensor("dbg", [128, 8192], F32, kind="ExternalOutput").ap()

    need = {"identity", "constbank"}
    if "l0a" in phases:
        need |= {"l0_WqT", "l0_WkT", "l0_WvT", "l0_WoT", "l0_bq_cols",
                 "l0_bkv_row", "l0_bo_row", "bsel"}
    if "l0f" in phases:
        need |= {"l0_w1T", "l0_w2T", "l0_h1T", "l0_h2T", "l0_w3T",
                 "l0_w1_s_cols", "l0_w2_s_cols", "l0_h1_s_cols", "l0_h2_s_cols",
                 "l0_w3_s_cols"}
    if "l1a" in phases:
        need |= {"l1_qT", "l1_kT", "l1_vT", "l1_gT", "l1_aT32", "l1_bT32", "l1_oT",
                 "l1_convq_cols", "l1_convk_cols", "l1_convv_cols",
                 "l1_Alog_row", "l1_dtb_row", "l1_nw_cols", "tri_incl",
                 "mask_strict_low"}
    if "l1f" in phases:
        need |= {"l1_wgate", "l1_wup", "l1_wdown"}
    for nm in sorted(need):
        arr = host_arrays[nm]
        dram[nm] = nc.dram_tensor(nm, list(arr.shape), F32, kind="ExternalInput").ap()

    with ExitStack() as es:
        S = Sched(nc, es)
        st = St()
        st.nt = nt
        st.dram = dram
        st.need = need

        st.stream = _sb(nc, es, "stream", [128, nt * D], BF16)
        st.ldbuf = _sb(nc, es, "ldbuf", [128, 2 * D], F32)
        st.ident = _sb(nc, es, "sb_ident", [128, 128], BF16)
        st.ones_row = _sb(nc, es, "sb_ones", [1, 512], BF16)
        st.cb = _sb(nc, es, "sb_cb", [128, 5], F32)
        ident_f32 = _sb(nc, es, "ident_f32", [128, 128], F32)
        st.ident_f32 = ident_f32
        st.c_eps_ln = st.cb[:, 0:1]
        st.c_eps6 = st.cb[:, 1:2]
        st.c_invD = st.cb[:, 2:3]
        st.c_neg1 = st.cb[:, 3:4]
        st.c_qln = st.cb[:, 4:5]
        st.arena = Arena(nc, es, ARENA_BYTES)

        st.banks = [es.enter_context(nc.psum_tensor(f"bank{i}", [128, 512], F32))
                    for i in range(7)]
        st.tp_bank = es.enter_context(nc.psum_tensor("tp_bank", [128, 256], BF16))

        st.LD = S.sem("LD")
        st.SV = S.sem("SV")
        st.SS = S.sem("SS")
        st.TN = S.sem("TN")
        st.GP = S.sem("GP")
        st.ST = S.sem("ST")
        st.tile_tok = {}

        S.sy(lambda e: e.dma_start(out=ident_f32[:], in_=dram["identity"]),
             inc=st.LD, incn=16)
        S.sy(lambda e: e.dma_start(out=st.cb[:], in_=dram["constbank"]),
             inc=st.LD, incn=16)
        consts_tok = st.LD.count

        if stream_in == "x":
            LX = [S.sem("LX0"), S.sem("LX1")]
            xf32 = st.ldbuf
            for t in range(nt):
                sl = xf32[:, (t % 2) * D:(t % 2) * D + D]
                if t >= 2:
                    S.waitonly("sync", [(st.SV, t - 1)])
                S.sy(lambda e, d=sl, s=dram["x"][t * 128:(t + 1) * 128, :]:
                     e.dma_start(out=d, in_=s), inc=LX[t % 2], incn=16)
                ctok = S.v(lambda e, d=st.stream[:, t * D:(t + 1) * D], s=sl:
                           e.tensor_copy(d, s),
                           wait=[(LX[t % 2], LX[t % 2].count)], inc=st.SV)
                st.tile_tok[t] = (st.SV, ctok)
        else:
            LS = S.sem("LS")
            for t in range(nt):
                S.sy(lambda e, d=st.stream[:, t * D:(t + 1) * D],
                     s=dram["stream_in"][t * 128:(t + 1) * 128, :]:
                     e.dma_start(out=d, in_=s), inc=LS, incn=16)
            S.waitonly("vector", [(LS, LS.count)])
            for t in range(nt):
                st.tile_tok[t] = (LS, LS.count)

        S.v(lambda e: e.tensor_copy(st.ident[:], ident_f32[:]),
            wait=[(st.LD, consts_tok)], inc=st.SV)
        S.v(lambda e: e.memset(st.ones_row[:], 1.0), inc=st.SV)
        st.base_ready = (st.SV, st.SV.count)

        if "l0a" in phases:
            S.scope = "l0a"
            emit_l0_attn(nc, es, S, st)
        if "l0f" in phases:
            S.scope = "l0f"
            emit_l0_ffn(nc, es, S, st)
        if "l1a" in phases:
            S.scope = "l1a"
            emit_l1_attn(nc, es, S, st)
        if "l1f" in phases:
            S.scope = "l1f"
            emit_l1_ffn(nc, es, S, st, final_out=final_out)

        if stream_out == "stream":
            for t in range(nt):
                w = [st.tile_tok[t]]
                S.sy(lambda e, d=dram["stream_out"][t * 128:(t + 1) * 128, :],
                     s=st.stream[:, t * D:(t + 1) * D]: e.dma_start(out=d, in_=s),
                     wait=w, inc=st.ST, incn=16)
        if st.ST.count:
            S.waitonly("sync", [(st.ST, st.ST.count)])

        block = es.enter_context(nc.Block())
        S.build_blocks(block)
    return nc, sorted(need)


def phase_barrier(S, st):
    toks = [(st.TN, st.TN.count), (st.SV, st.SV.count), (st.SS, st.SS.count)]
    for eng in ("tensor", "vector", "scalar", "sync", "gpsimd"):
        S.waitonly(eng, toks)


def emit_ln(nc, es, S, st, r, ln_sq, ln_st, out_bf, wait):
    SV, SS = st.SV, st.SS
    w0 = list(wait) + [(SS, SS.count)]
    v1 = S.v(lambda e: e.reduce_sum(ln_st[:, 0:1], r[:], axis=AX.X), wait=w0, inc=SV)
    s1 = S.s(lambda e: e.activation(ln_sq[:], r[:], AF.Square, accum_out=ln_st[:, 1:2]),
             wait=w0, inc=SS)
    v2 = S.v(lambda e: e.tensor_scalar_mul(ln_st[:, 2:3], ln_st[:, 0:1], 1.0 / D),
             wait=[(SV, v1)], inc=SV)
    v3 = S.v(lambda e: e.tensor_scalar_mul(ln_st[:, 3:4], ln_st[:, 2:3], ln_st[:, 2:3]),
             wait=[(SV, v2)], inc=SV)
    v4 = S.v(lambda e: e.tensor_scalar(ln_st[:, 4:5], ln_st[:, 1:2], 1.0 / D,
                                       ln_st[:, 3:4], ALU.mult, ALU.subtract),
             wait=[(SS, s1), (SV, v3)], inc=SV)
    s2 = S.s(lambda e: e.activation(ln_st[:, 5:6], ln_st[:, 4:5], AF.Sqrt,
                                    bias=st.c_eps_ln), wait=[(SV, v4)], inc=SS)
    v5 = S.v(lambda e: e.reciprocal(ln_st[:, 5:6], ln_st[:, 5:6]),
             wait=[(SS, s2)], inc=SV)
    v6 = S.v(lambda e: e.tensor_scalar(ln_st[:, 6:7], ln_st[:, 2:3], ln_st[:, 5:6],
                                       -1.0, ALU.mult, ALU.mult), wait=[(SV, v5)], inc=SV)
    tok = S.v(lambda e: e.tensor_scalar(out_bf, r[:], ln_st[:, 5:6],
                                        ln_st[:, 6:7], ALU.mult, ALU.add),
              wait=[(SV, v6)], inc=SV)
    return (SV, tok)



def emit_l0_attn(nc, es, S, st):
    import os
    dbg = os.environ.get("L0A_STOP", "")
    nt = st.nt
    dram = st.dram
    LD, SV, SS, TN = st.LD, st.SV, st.SS, st.TN
    B = st.banks
    sc = Scope(st.arena)

    WL = S.sem("l0a_wl")
    wcast = S.sem("l0a_wcast")
    wA = sc.bf16(8 * D)
    wB = sc.bf16(8 * D)
    wtmp = sc.f32(D)

    def load_wt(nm, dst, extra_wait=()):
        for j in range(8):
            src = dram[nm][128 * j:128 * (j + 1), :]
            S.sy(lambda e, d=wtmp, s=src: e.dma_start(out=d, in_=s),
                 wait=[(wcast, wcast.count)] + list(extra_wait), inc=WL, incn=16)
            S.v(lambda e, d=dst[:, j * D:(j + 1) * D], s=wtmp:
                e.tensor_copy(d, s), wait=[(WL, WL.count)], inc=wcast)

    load_wt("l0_WkT", wA)
    load_wt("l0_WvT", wB)
    wk, wv = wA, wB
    if dbg == "loadw":
        return

    WS = S.sem("l0a_ws")
    bkv = sc.bf16(2 * D, rows=1)
    bo_row = sc.bf16(D, rows=1)
    bq = sc.f32(8)
    nbq = sc.f32(8)
    bsel = sc.bf16(1024, rows=16)
    stage = st.ldbuf[0:16, 0:2048]
    S.sy(lambda e: e.dma_start(out=stage[0:1, 0:2 * D], in_=dram["l0_bkv_row"]),
         wait=[st.base_ready], inc=WS, incn=16)
    c1 = S.v(lambda e: e.tensor_copy(bkv, stage[0:1, 0:2 * D]),
             wait=[(WS, WS.count)], inc=wcast)
    S.sy(lambda e: e.dma_start(out=stage[0:1, 0:D], in_=dram["l0_bo_row"]),
         wait=[(wcast, c1)], inc=WS, incn=16)
    c2 = S.v(lambda e: e.tensor_copy(bo_row, stage[0:1, 0:D]),
             wait=[(WS, WS.count)], inc=wcast)
    S.sy(lambda e: e.dma_start(out=stage[0:16, 0:1024], in_=dram["bsel"]),
         wait=[(wcast, c2)], inc=WS, incn=16)
    S.v(lambda e: e.tensor_copy(bsel, stage[0:16, 0:1024]),
        wait=[(WS, WS.count)], inc=wcast)
    WS2 = S.sem("l0a_ws2")
    S.sy(lambda e: e.dma_start(out=bq, in_=dram["l0_bq_cols"]), inc=WS2, incn=16)
    S.v(lambda e: e.tensor_scalar_mul(nbq, bq, -1.0),
        wait=[(WS2, WS2.count)], inc=wcast)

    scr64 = sc.bf16(nt * D)
    st.l0_scr = scr64
    kv_tm = sc.bf16(D)
    vx_tm = sc.bf16(16 * 65)
    S_sh = sc.f32(8 * 130)
    S_acc = S_sh
    rres = S_sh[:, 0:D]
    S_b = sc.bf16(8 * 130)
    phi_a = st.ldbuf[:, 0:512]
    phi_b = st.ldbuf[:, 512:1024]
    ln_sq = st.ldbuf[:, 1024:2048]

    ps_tp = st.tp_bank[:, 0:128]
    ps_p = [B[1], B[2]]
    ps_s = [B[3][:, 0:130], B[4][:, 0:130], B[5][:, 0:130], B[6][:, 0:130]]

    S.v(lambda e: e.memset(vx_tm, 1.0), wait=[st.base_ready], inc=SV)
    S.v(lambda e: e.memset(S_acc, 0.0), inc=SV)
    prep_tok = SV.count
    wready_tok = wcast.count

    tfree = S.sem("l0a_tfree")
    pfree = S.sem("l0a_pfree")
    sfree = S.sem("l0a_sfree")

    for t in range(nt):
        for c in range(8):
            w = []
            if t == 0 and c == 0:
                w = [(SV, prep_tok)]
            if c == 0:
                w = w + [st.tile_tok[t]]
            if t > 0 or c > 0:
                w = w + [(tfree, tfree.count)]
            tk = S.t(lambda e, sl=st.stream[:, t * D + c * 128: t * D + (c + 1) * 128]:
                     e.transpose(ps_tp, sl, st.ident[:]), wait=w, inc=TN)
            S.v(lambda e, d=scr64[:, t * D + c * 128: t * D + (c + 1) * 128]:
                e.tensor_copy(d, ps_tp), wait=[(TN, tk)], inc=tfree)
        xfm = scr64[:, t * D:(t + 1) * D]
        tfree_tok = tfree.count
        if dbg == "tp":
            continue

        for which, (wmat, brow_off) in enumerate([(wk, 0), (wv, D)]):
            for h in range(2):
                psb = ps_p[h]
                for c in range(8):
                    w = []
                    if c == 0:
                        if which == 0 and h == 0:
                            w = [(tfree, tfree_tok), (wcast, wready_tok)]
                        w = w + [(pfree, pfree.count)]
                    S.t(lambda e, wsl=wmat[:, c * D + h * 512: c * D + h * 512 + 512],
                        lhs=xfm[:, c * 128:(c + 1) * 128], p=psb, s0=(c == 0):
                        e.matmul(p[:], lhs, wsl, start=s0, stop=False),
                        wait=w)
                import os as _os
                if _os.environ.get("NOBIAS", ""):
                    tk = S.t(lambda e, wsl=wmat[:, 7 * D + h * 512: 7 * D + h * 512 + 512],
                             lhs=xfm[:, 7 * 128:8 * 128], p=psb:
                             e.matmul(p[:], lhs, wsl, start=False, stop=True), inc=TN)
                else:
                    tk = S.t(lambda e, b=bkv[0:1, brow_off + h * 512: brow_off + h * 512 + 512],
                             p=psb: e.matmul(p[:], st.ones_row[:, 0:128], b,
                                             start=False, stop=True), inc=TN)
                import os as _os2
                if _os2.environ.get("PLAINPHI", "") and which == 0:
                    S.v(lambda e, p=psb, d=kv_tm[:, h * 512:h * 512 + 512]:
                        e.tensor_copy(d, p[:]),
                        wait=[(TN, tk), (TN, TN.count)], inc=pfree)
                elif _os2.environ.get("PLAINPHI", ""):
                    S.v(lambda e, p=psb,
                        d=vx_tm[:, h * 8 * 65:(h + 1) * 8 * 65].rearrange(
                            "p (a b) -> p a b", b=65)[:, :, 0:64]:
                        e.tensor_copy(d, p[:].rearrange("p (a b) -> p a b", b=64)),
                        wait=[(TN, tk), (TN, TN.count)], inc=pfree)
                elif which == 0:
                    va = S.v(lambda e, p=psb: e.tensor_relu(phi_a, p[:]),
                             wait=[(TN, tk), (pfree, pfree.count)], inc=SV)
                    vb = S.v(lambda e, p=psb: e.tensor_scalar_min(phi_b, p[:], 0.0),
                             wait=[(SV, va)], inc=SV)
                    s2 = S.s(lambda e: e.activation(phi_b, phi_b, AF.Exp),
                             wait=[(SV, vb), (SS, SS.count)], inc=SS)
                    S.v(lambda e, d=kv_tm[:, h * 512:h * 512 + 512]:
                        e.tensor_add(d, phi_a, phi_b),
                        wait=[(SS, s2), (SV, va), (TN, TN.count)], inc=pfree)
                else:
                    S.v(lambda e, p=psb,
                        d=vx_tm[:, h * 8 * 65:(h + 1) * 8 * 65].rearrange(
                            "p (a b) -> p a b", b=65)[:, :, 0:64]:
                        e.tensor_copy(d, p[:].rearrange("p (a b) -> p a b", b=64)),
                        wait=[(TN, tk), (TN, TN.count)], inc=pfree)
        vready = pfree.count
        if dbg == "proj":
            continue

        for pr in range(8):
            bank = ps_s[pr % 4]
            w = [(pfree, vready)]
            if t > 0 or pr >= 4:
                w.append((sfree, sfree.count))
            tk = S.t(lambda e, b=bank, lhs=kv_tm[:, pr * 128:(pr + 1) * 128],
                     r=vx_tm[:, pr * 130:(pr + 1) * 130]:
                     e.matmul(b, lhs, r, start=True, stop=True), wait=w, inc=TN)
            S.v(lambda e, b=bank, d=S_acc[:, pr * 130:(pr + 1) * 130]:
                e.tensor_add(d, d, b), wait=[(TN, tk)], inc=sfree)

    if dbg in ("tp", "proj"):
        return
    cast_tok = S.v(lambda e: e.tensor_copy(S_b, S_acc),
                   wait=[(sfree, sfree.count)], inc=SV)
    if dbg == "sweep1":
        return

    sweep1_done = (TN, TN.count)
    load_wt("l0_WqT", wA, extra_wait=[sweep1_done])
    load_wt("l0_WoT", wB, extra_wait=[sweep1_done])
    wq, wo = wA, wB
    wready2 = (wcast, wcast.count)

    zbd = sc.bf16(128)
    zm = S.v(lambda e: e.memset(zbd, 0.0), inc=SV)
    prev = zm
    for h in range(16):
        c = h // 2
        zcol = (h // 2) * 130 + 65 * (h % 2) + 64
        prev = S.v(lambda e, d=zbd[64 * (h % 2):64 * (h % 2) + 64, c * 16 + h:c * 16 + h + 1],
                   s=S_b[64 * (h % 2):64 * (h % 2) + 64, zcol:zcol + 1]:
                   e.tensor_copy(d, s), wait=[(SV, prev), (SV, cast_tok)], inc=SV)
    zbd_tok = SV.count

    phiQ = sc.bf16(8 * 512)
    ctx = phiQ
    rbuf = sc.bf16(512)
    dn = sc.bf16(512, rows=16)
    dn_f = sc.f32(512, rows=16)
    ln_st = sc.f32(8)

    ps_q = [B[1], B[2]]
    ps_dn = B[0][0:16, :]
    ps_bc = [B[3], B[4]]
    ps_n2 = [B[5], B[6]]

    qfree = S.sem("l0a_qfree")
    dnfree = S.sem("l0a_dnfree")
    bcfree = S.sem("l0a_bcfree")
    nmfree = S.sem("l0a_nmfree")
    wofree = S.sem("l0a_wofree")

    for g in range(nt // 4):
        for c in range(8):
            psb = ps_q[c % 2]
            for i in range(8):
                w = []
                if g == 0 and c == 0 and i == 0:
                    w = [(SV, cast_tok), wready2]
                if i == 0 and (g > 0 or c >= 2):
                    w = w + [(qfree, qfree.count)]
                S.t(lambda e, p=psb, wsl=wq[:, i * D + c * 128: i * D + (c + 1) * 128],
                    r=scr64[:].rearrange("p (a b) -> p a b", b=D)
                      [:, 4 * g:4 * g + 4, i * 128:(i + 1) * 128],
                    s0=(i == 0), s1=(i == 7):
                    e.matmul(p[:], wsl, r, start=s0, stop=s1),
                    wait=w, inc=(TN if i == 7 else None))
            tk = TN.count
            s1 = S.s(lambda e, p=psb, b=bq[:, c:c + 1]:
                     e.activation(phi_a, p[:], AF.Relu, bias=b),
                     wait=[(TN, tk), (qfree, qfree.count)], inc=SS)
            sr = S.s(lambda e, p=psb, b=nbq[:, c:c + 1]:
                     e.activation(phi_b, p[:], AF.Relu, bias=b, scale=-1.0),
                     wait=[(SS, s1)], inc=SS)
            s3 = S.s(lambda e: e.activation(phi_b, phi_b, AF.Exp, scale=-1.0),
                     wait=[(SS, sr)], inc=SS)
            S.v(lambda e, d=phiQ[:, c * 512:(c + 1) * 512]:
                e.tensor_add(d, phi_a, phi_b),
                wait=[(SS, s3), (TN, TN.count)], inc=qfree)
        q_tok = qfree.count

        for c in range(8):
            w = []
            if c == 0:
                w = [(SV, zbd_tok), (qfree, q_tok), (dnfree, dnfree.count)]
            S.t(lambda e, lhs=zbd[:, c * 16:(c + 1) * 16],
                r=phiQ[:, c * 512:(c + 1) * 512], s0=(c == 0), s1=(c == 7):
                e.matmul(ps_dn, lhs, r, start=s0, stop=s1),
                wait=w, inc=(TN if c == 7 else None))
        dn_tok = TN.count
        da = S.v(lambda e: e.tensor_scalar_add(dn_f, ps_dn, 1e-6),
                 wait=[(TN, dn_tok), (SV, SV.count)], inc=SV)
        db = S.v(lambda e: e.reciprocal(dn_f, dn_f), wait=[(SV, da)], inc=dnfree)
        S.v(lambda e: e.tensor_copy(dn, dn_f), wait=[(SV, da), (dnfree, db)], inc=SV)
        dn_ready = SV.count

        for pr in range(8):
            bcb = ps_bc[pr % 2]
            nmb = ps_n2[pr % 2]
            w = [(SV, dn_ready)]
            if g > 0 or pr >= 2:
                w.append((bcfree, bcfree.count))
            tb = S.t(lambda e, lhs=bsel[:, pr * 128:(pr + 1) * 128], b=bcb:
                     e.matmul(b[:], lhs, dn, start=True, stop=True),
                     wait=w, inc=TN)
            S.v(lambda e, b=bcb: e.tensor_copy(rbuf, b[:]),
                wait=[(TN, tb), (SV, SV.count), (nmfree, nmfree.count)], inc=bcfree)
            rb_tok = bcfree.count
            for hh in range(2):
                w = []
                if hh == 0 and (g > 0 or pr >= 2):
                    w = [(nmfree, nmfree.count)]
                scol = pr * 130 + 65 * hh
                S.t(lambda e, bsl2=nmb[64 * hh:64 * hh + 64, :],
                    lhs=S_b[64 * hh:64 * hh + 64, scol:scol + 64],
                    r=phiQ[64 * hh:64 * hh + 64, pr * 512:(pr + 1) * 512]:
                    e.matmul(bsl2, lhs, r, start=True, stop=True),
                    wait=w, inc=(TN if hh == 1 else None))
            tn = TN.count
            S.v(lambda e, b=nmb, d=ctx[:, pr * 512:(pr + 1) * 512]:
                e.tensor_mul(d, b[:], rbuf),
                wait=[(TN, tn), (bcfree, rb_tok)], inc=nmfree)
        ctx_tok = nmfree.count

        for stile in range(4):
            t = 4 * g + stile
            for h in range(2):
                psb = ps_bc[h]
                for i in range(8):
                    w = []
                    if i == 0:
                        w = [(nmfree, ctx_tok)]
                        if g > 0 or stile > 0:
                            w.append((wofree, wofree.count))
                    S.t(lambda e, p=psb,
                        lhs=ctx[:, i * 512 + stile * 128: i * 512 + stile * 128 + 128],
                        wsl=wo[:, i * D + h * 512: i * D + h * 512 + 512], s0=(i == 0):
                        e.matmul(p[:], lhs, wsl, start=s0, stop=False),
                        wait=w)
                S.t(lambda e, p=psb, b=bo_row[0:1, h * 512:h * 512 + 512]:
                    e.matmul(p[:], st.ones_row[:, 0:128], b, start=False, stop=True),
                    inc=TN)
            tk = TN.count
            wprev = [st.tile_tok[t - 1]] if (t - 1) in st.tile_tok else []
            S.v(lambda e, p=ps_bc[0], d=rres[:, 0:512],
                s=st.stream[:, t * D: t * D + 512]:
                e.tensor_add(d, p[:], s),
                wait=[(TN, tk), (SS, SS.count), (SV, cast_tok)] + wprev, inc=SV)
            S.v(lambda e, p=ps_bc[1], d=rres[:, 512:1024],
                s=st.stream[:, t * D + 512: t * D + 1024]:
                e.tensor_add(d, p[:], s), inc=wofree)
            res_tok = wofree.count
            st.tile_tok[t] = emit_ln(nc, es, S, st, rres, ln_sq, ln_st,
                                     st.stream[:, t * D:(t + 1) * D],
                                     wait=[(wofree, res_tok)])

    st.l0a_end = {"TN": TN.count, "SV": SV.count, "SS": SS.count}
    sc.close()


def emit_l0_ffn(nc, es, S, st):
    nt = st.nt
    dram = st.dram
    SV, SS, TN = st.SV, st.SS, st.TN
    B = st.banks
    phase_barrier(S, st)
    sc = Scope(st.arena)
    TT = 4
    NP = nt // TT
    TTK = TT * 128

    c_w1 = nc.dram_tensor("l0f_cw1", [32, 128, 8 * 128], BF16, kind="Internal").ap()
    c_w2 = nc.dram_tensor("l0f_cw2", [32, 128, 8 * 128], BF16, kind="Internal").ap()
    c_h1 = nc.dram_tensor("l0f_ch1", [32, 128, 32 * 128], BF16, kind="Internal").ap()
    c_h2 = nc.dram_tensor("l0f_ch2", [32, 128, 32 * 128], BF16, kind="Internal").ap()
    c_w3 = nc.dram_tensor("l0f_cw3", [8, 128, 32 * 128], BF16, kind="Internal").ap()

    x1fm = sc.bf16(TTK * 8)
    yfm = x1fm
    hbufA = sc.bf16(TTK * 32)
    hbufB = sc.bf16(TTK * 32)
    wld = sc.f32(1024)
    w12 = [sc.bf16(1024), sc.bf16(1024)]
    w12b = [sc.bf16(1024), sc.bf16(1024)]
    hld = sc.f32(4096)
    hbf = [sc.bf16(4096), sc.bf16(4096)]
    s1c = sc.f32(32)
    s2c = sc.f32(32)
    h1c = sc.f32(32)
    h2c = sc.f32(32)
    s3c = sc.f32(8)
    rres = sc.f32(1024)
    ln_st = sc.f32(8)
    t1sb2 = [st.ldbuf[:, 0:512], sc.f32(512)[:]]
    t2sb2 = [st.ldbuf[:, 512:1024], sc.f32(512)[:]]
    ln_sq = st.ldbuf[:, 1024:2048]

    SL = S.sem("l0f_sl")
    S.sy(lambda e: e.dma_start(out=s1c, in_=dram["l0_w1_s_cols"]), inc=SL, incn=16)
    S.sy(lambda e: e.dma_start(out=s2c, in_=dram["l0_w2_s_cols"]), inc=SL, incn=16)
    S.sy(lambda e: e.dma_start(out=h1c, in_=dram["l0_h1_s_cols"]), inc=SL, incn=16)
    S.sy(lambda e: e.dma_start(out=h2c, in_=dram["l0_h2_s_cols"]), inc=SL, incn=16)
    S.sy(lambda e: e.dma_start(out=s3c, in_=dram["l0_w3_s_cols"]), inc=SL, incn=16)
    sl_tok = SL.count

    WL = S.sem("l0f_wl")
    HB = [S.sem("l0f_hb0"), S.sem("l0f_hb1")]
    WB = [S.sem("l0f_wb0"), S.sem("l0f_wb1")]
    CS = S.sem("l0f_cs")
    HS = S.sem("l0f_hs")
    tpf = S.sem("l0f_tpf")
    pf = S.sem("l0f_pf")
    hf = S.sem("l0f_hf")
    yf = S.sem("l0f_yf")
    lnf = S.sem("l0f_lnf")

    ps_tp = st.tp_bank[:, 0:128]
    ps_w4 = [[B[1], B[2]], [B[5], B[6]]]
    ps_h = [B[3], B[4]]
    ps_y = [B[5], B[6]]

    def ldcast_pass0(cache, src_blk, dst_bf, wait_w, cast_cols):
        S.sy(lambda e, d=hld[:, 0:cast_cols] if cast_cols > 1024 else wld,
             s=src_blk: e.dma_start(out=d, in_=s),
             wait=wait_w, inc=WL, incn=16)
        ct = S.v(lambda e, d=dst_bf, s=hld[:, 0:cast_cols] if cast_cols > 1024 else wld:
                 e.tensor_copy(d, s), wait=[(WL, WL.count)], inc=CS)
        S.sy(lambda e, d=cache, s=dst_bf: e.dma_start(out=d, in_=s),
             wait=[(CS, ct)], inc=HS, incn=16)
        return ct

    slot_last = {}
    bank_free = {}
    def bw(b):
        e = bank_free.get(id(b))
        return [e] if e else []
    prev_pass_tok = None
    for p in range(NP):
        for t in range(TT):
            gt = p * TT + t
            for c in range(8):
                w = [(tpf, tpf.count)] if (p > 0 or t > 0 or c > 0) else []
                if c == 0:
                    w = w + [st.tile_tok[gt]]
                tk = S.t(lambda e, sl=st.stream[:, gt * D + c * 128: gt * D + (c + 1) * 128]:
                         e.transpose(ps_tp, sl, st.ident[:]), wait=w, inc=TN)
                S.v(lambda e, d=x1fm[:].rearrange("p (a b) -> p a b", b=TTK)
                    [:, c, t * 128:(t + 1) * 128]:
                    e.tensor_copy(d, ps_tp), wait=[(TN, tk)], inc=tpf)
        x1_tok = tpf.count

        mult_hist = []
        for ob in range(32):
            sel = ob % 2
            ps_w = ps_w4[ob % 2]
            wcur = w12 if sel == 0 else w12b
            wwait = []
            if p == 0:
                for which, (cac, dnm) in enumerate([(c_w1, "l0_w1T"), (c_w2, "l0_w2T")]):
                    src = dram[dnm][ob].rearrange("(c p) o -> p c o", p=128)
                    wwait2 = [(CS, CS.count)]
                    S.sy(lambda e, d=wld.rearrange("p (c o) -> p c o", c=8), s=src:
                         e.dma_start(out=d, in_=s),
                         wait=wwait2 + ([(TN, TN.count)] if ob >= 2 else []),
                         inc=WL, incn=16)
                    ct = S.v(lambda e, d=wcur[which], s=wld: e.tensor_copy(d, s),
                             wait=[(WL, WL.count), (HS, HS.count)], inc=CS)
                    S.sy(lambda e, d=cac[ob], s=wcur[which]: e.dma_start(out=d, in_=s),
                         wait=[(CS, ct)], inc=HS, incn=16)
                wrdy = (CS, CS.count)
            else:
                for which, cac in enumerate([c_w1, c_w2]):
                    lw = ([(HS, HS.count)] if ob < 2 and p == 1 else [])
                    e_ = slot_last.get(id(wcur))
                    if e_:
                        lw = lw + [e_]
                    S.sy(lambda e, d=wcur[which], s=cac[ob]: e.dma_start(out=d, in_=s),
                         wait=lw, inc=WB[sel], incn=16)
                wrdy = (WB[sel], WB[sel].count)
            for which in range(2):
                psb = ps_w[which]
                for ic in range(8):
                    w = [wrdy] if ic == 0 else []
                    if ic == 0 and which == 0:
                        w = w + [(tpf, x1_tok), (SL, sl_tok)] + bw(ps_w[0]) + bw(ps_w[1])
                    S.t(lambda e, p2=psb, lhs=x1fm[:].rearrange("p (a b) -> p a b", b=TTK)[:, ic, :],
                        wsl=wcur[which][:, ic * 128:(ic + 1) * 128], s0=(ic == 0), s1=(ic == 7):
                        e.matmul(p2[:], wsl, lhs, start=s0, stop=s1),
                        wait=w, inc=(TN if ic == 7 else None))
            tk = TN.count
            slot_last[id(wcur)] = (TN, tk)
            t1sb = t1sb2[ob % 2]
            t2sb = t2sb2[ob % 2]
            mwar = [mult_hist[ob - 2]] if ob >= 2 else []
            sg = S.s(lambda e, sl=s1c[:, ob:ob + 1], d=t1sb, p2=ps_w[0]:
                     e.activation(d, p2[:], AF.Silu, scale=sl),
                     wait=[(TN, tk), (SS, SS.count)] + mwar, inc=SS)
            vg = S.v(lambda e, sl=s2c[:, ob:ob + 1], d=t2sb, p2=ps_w[1]:
                     e.tensor_scalar_mul(d, p2[:], sl),
                     wait=[(TN, tk), (SV, SV.count)] + mwar, inc=SV)
            bank_free[id(ps_w[0])] = (SS, sg)
            mt = S.v(lambda e, d=hbufA[:].rearrange("p (a b) -> p a b", b=TTK)[:, ob, :],
                     a=t1sb, b2=t2sb:
                     e.tensor_mul(d, a, b2), wait=[(SS, sg), (SV, vg)], inc=pf)
            mult_hist.append((pf, mt))
            bank_free[id(ps_w[1])] = (SV, vg)
        h_tok = pf.count

        for stage, (cac, dnm, hin, hout, scol) in enumerate(
                [(c_h1, "l0_h1T", hbufA, hbufB, h1c), (c_h2, "l0_h2T", hbufB, hbufA, h2c)]):
            for ob in range(32):
                sel = ob % 2
                if p == 0:
                    src = dram[dnm][ob].rearrange("(c p) o -> p c o", p=128)
                    S.sy(lambda e, d=hld.rearrange("p (c o) -> p c o", c=32), s=src:
                         e.dma_start(out=d, in_=s),
                         wait=[(CS, CS.count)], inc=WL, incn=16)
                    ct = S.v(lambda e, d=hbf[sel], s=hld: e.tensor_copy(d, s),
                             wait=[(WL, WL.count), (HS, HS.count)] +
                                  [slot_last[id(hbf[sel])]] if id(hbf[sel]) in slot_last
                                  else [(WL, WL.count), (HS, HS.count)], inc=CS)
                    S.sy(lambda e, d=cac[ob], s=hbf[sel]: e.dma_start(out=d, in_=s),
                         wait=[(CS, ct)], inc=HS, incn=16)
                    hrdy = (CS, ct)
                else:
                    lw = [slot_last[id(hbf[sel])]] if id(hbf[sel]) in slot_last else []
                    if p == 1 and stage == 0 and ob < 2:
                        lw = lw + [(HS, HS.count)]
                    S.sy(lambda e, d=hbf[sel], s=cac[ob]: e.dma_start(out=d, in_=s),
                         wait=lw, inc=HB[sel], incn=16)
                    hrdy = (HB[sel], HB[sel].count)
                psb = ps_h[sel]
                for ic in range(32):
                    w = []
                    if ic == 0:
                        w = [hrdy] + bw(psb)
                        if ob == 0:
                            w.append((pf if stage == 0 else hf, h_tok if stage == 0 else h2_tok))
                    S.t(lambda e, p2=psb,
                        lhs=hbf[sel][:, ic * 128:(ic + 1) * 128],
                        r=hin[:].rearrange("p (a b) -> p a b", b=TTK)[:, ic, :],
                        s0=(ic == 0), s1=(ic == 31):
                        e.matmul(p2[:], lhs, r, start=s0, stop=s1),
                        wait=w, inc=(TN if ic == 31 else None))
                tk = TN.count
                slot_last[id(hbf[sel])] = (TN, tk)
                gt_ = S.s(lambda e, p2=psb, sl=scol[:, ob:ob + 1],
                          d=hout[:].rearrange("p (a b) -> p a b", b=TTK)[:, ob, :]:
                          e.activation(d, p2[:], AF.Gelu, scale=sl),
                          wait=[(TN, tk), (SS, SS.count)], inc=hf)
                bank_free[id(psb)] = (hf, gt_)
            h2_tok = hf.count

        for ob in range(8):
            sel = ob % 2
            if p == 0:
                src = dram["l0_w3T"][ob].rearrange("(c p) o -> p c o", p=128)
                S.sy(lambda e, d=hld.rearrange("p (c o) -> p c o", c=32), s=src:
                     e.dma_start(out=d, in_=s),
                     wait=[(CS, CS.count)], inc=WL, incn=16)
                ct = S.v(lambda e, d=hbf[sel], s=hld: e.tensor_copy(d, s),
                         wait=[(WL, WL.count), (HS, HS.count)] +
                              ([slot_last[id(hbf[sel])]] if id(hbf[sel]) in slot_last else []),
                         inc=CS)
                S.sy(lambda e, d=c_w3[ob], s=hbf[sel]: e.dma_start(out=d, in_=s),
                     wait=[(CS, ct)], inc=HS, incn=16)
                w3rdy = (CS, ct)
            else:
                lw = [slot_last[id(hbf[sel])]] if id(hbf[sel]) in slot_last else []
                S.sy(lambda e, d=hbf[sel], s=c_w3[ob]: e.dma_start(out=d, in_=s),
                     wait=lw, inc=HB[sel], incn=16)
                w3rdy = (HB[sel], HB[sel].count)
            psb = ps_y[sel]
            for ic in range(32):
                w = []
                if ic == 0:
                    w = [w3rdy] + bw(psb)
                    if ob == 0:
                        w.append((hf, h2_tok))
                S.t(lambda e, p2=psb,
                    lhs=hbf[sel][:, ic * 128:(ic + 1) * 128],
                    r=hbufA[:].rearrange("p (a b) -> p a b", b=TTK)[:, ic, :],
                    s0=(ic == 0), s1=(ic == 31):
                    e.matmul(p2[:], lhs, r, start=s0, stop=s1),
                    wait=w, inc=(TN if ic == 31 else None))
            tk = TN.count
            slot_last[id(hbf[sel])] = (TN, tk)
            yv = S.v(lambda e, p2=psb, sl=s3c[:, ob:ob + 1],
                     d=yfm[:].rearrange("p (a b) -> p a b", b=TTK)[:, ob, :]:
                     e.tensor_scalar_mul(d, p2[:], sl),
                     wait=[(TN, tk), (SV, SV.count)], inc=yf)
            bank_free[id(psb)] = (yf, yv)
        y_tok = yf.count

        for t in range(TT):
            gt = p * TT + t
            for c in range(8):
                w = [(tpf, tpf.count)]
                if t == 0 and c == 0:
                    w.append((yf, y_tok))
                tk = S.t(lambda e, sl=yfm[:].rearrange("p (a b) -> p a b", b=TTK)
                         [:, c, t * 128:(t + 1) * 128]:
                         e.transpose(ps_tp, sl, st.ident[:]), wait=w, inc=TN)
                S.v(lambda e, d=rres[:, c * 128:(c + 1) * 128],
                    s=st.stream[:, gt * D + c * 128: gt * D + (c + 1) * 128]:
                    e.tensor_add(d, ps_tp, s),
                    wait=[(TN, tk), (lnf, lnf.count), (SS, SS.count)], inc=tpf)
            res_tok = tpf.count
            st.tile_tok[gt] = emit_ln(nc, es, S, st, rres, ln_sq, ln_st,
                                      st.stream[:, gt * D:(gt + 1) * D],
                                      wait=[(tpf, res_tok)])
            S.v(lambda e: e.engine_nop(), wait=[st.tile_tok[gt]], inc=lnf)

    st.l0f_end = {"TN": TN.count}
    sc.close()


def emit_l1_attn(nc, es, S, st):
    nt = st.nt
    dram = st.dram
    SV, SS, TN = st.SV, st.SS, st.TN
    B = st.banks
    phase_barrier(S, st)
    sc = Scope(st.arena)
    NH = 6
    NG = nt // 4

    WL = S.sem("l1_wl")
    CS = S.sem("l1_cs")
    HS = S.sem("l1_hs")
    cache = {}
    sizes = {"l1_qT": (8, 768), "l1_kT": (8, 768), "l1_vT": (8, 1536),
             "l1_gT": (8, 1536), "l1_oT": (12, 1024)}
    for nm, (blocks, ob) in sizes.items():
        cache[nm] = nc.dram_tensor("c_" + nm, [blocks * 128, ob], BF16,
                                   kind="Internal").ap()
    nw_cols = sc.f32(12)
    NW = S.sem("l1_nw")
    S.sy(lambda e: e.dma_start(out=nw_cols, in_=dram["l1_nw_cols"]), inc=NW, incn=16)
    S.waitonly("vector", [(NW, NW.count)])
    psc = Scope(st.arena)
    wstg = psc.f32(4096)
    wstgb = psc.bf16(4096)
    for nm, (blocks, ob) in sizes.items():
        rows_per = 4096 // ob
        ic = 0
        while ic < blocks:
            n = min(rows_per, blocks - ic)
            S.sy(lambda e, d=wstg[:, 0:n * ob].rearrange("p (c o) -> p c o", c=n),
                 s=dram[nm][ic * 128:(ic + n) * 128, :].rearrange("(c p) o -> p c o", p=128):
                 e.dma_start(out=d, in_=s),
                 wait=[(CS, CS.count)], inc=WL, incn=16)
            if nm == "l1_oT":
                ct = None
                for j in range(n):
                    ct = S.v(lambda e, d=wstgb[:, j * ob:(j + 1) * ob],
                             s=wstg[:, j * ob:(j + 1) * ob],
                             sl=nw_cols[:, ic + j:ic + j + 1]:
                             e.tensor_scalar_mul(d, s, sl),
                             wait=[(WL, WL.count), (HS, HS.count)], inc=CS)
            else:
                ct = S.v(lambda e, d=wstgb[:, 0:n * ob], s=wstg[:, 0:n * ob]:
                         e.tensor_copy(d, s),
                         wait=[(WL, WL.count), (HS, HS.count)], inc=CS)
            S.sy(lambda e, d=cache[nm][ic * 128:(ic + n) * 128, :].rearrange(
                 "(c p) o -> p c o", p=128),
                 s=wstgb[:, 0:n * ob].rearrange("p (c o) -> p c o", c=n):
                 e.dma_start(out=d, in_=s),
                 wait=[(CS, CS.count)], inc=HS, incn=16)
            ic += n
    cache_tok = HS.count
    psc.close()

    SM = S.sem("l1_sm")
    aT32b = sc.bf16(8 * 32)
    bT32b = sc.bf16(8 * 32)
    abstg = st.ldbuf[:, 0:256]
    convq = sc.f32(24)
    convk = sc.f32(24)
    convv = sc.f32(48)
    msl = sc.bf16(128)
    miu = sc.bf16(128)
    msu = sc.bf16(128)
    stg2 = st.ldbuf[:, 0:1024]
    S.sy(lambda e: e.dma_start(out=convq, in_=dram["l1_convq_cols"]), inc=SM, incn=16)
    S.sy(lambda e: e.dma_start(out=convk, in_=dram["l1_convk_cols"]), inc=SM, incn=16)
    S.sy(lambda e: e.dma_start(out=convv, in_=dram["l1_convv_cols"]), inc=SM, incn=16)
    m1 = S.sy(lambda e: e.dma_start(out=abstg.rearrange("p (c o) -> p c o", c=8),
                                    in_=dram["l1_aT32"].rearrange("(c p) o -> p c o", p=128)),
              wait=[st.tile_tok[nt - 1]], inc=SM, incn=16)
    c0a = S.v(lambda e: e.tensor_copy(aT32b, abstg), wait=[(SM, SM.count)], inc=CS)
    S.sy(lambda e: e.dma_start(out=abstg.rearrange("p (c o) -> p c o", c=8),
                               in_=dram["l1_bT32"].rearrange("(c p) o -> p c o", p=128)),
         wait=[(CS, c0a)], inc=SM, incn=16)
    c0 = S.v(lambda e: e.tensor_copy(bT32b, abstg), wait=[(SM, SM.count)], inc=CS)
    S.sy(lambda e: e.dma_start(out=stg2[:, 0:128], in_=dram["mask_strict_low"]),
         wait=[(CS, c0)], inc=SM, incn=16)
    c1 = S.v(lambda e: e.tensor_copy(msl, stg2[:, 0:128]), wait=[(SM, SM.count)], inc=CS)
    S.sy(lambda e: e.dma_start(out=stg2[:, 0:128], in_=dram["tri_incl"]),
         wait=[(CS, c1)], inc=SM, incn=16)
    c2 = S.v(lambda e: e.tensor_copy(miu, stg2[:, 0:128]), wait=[(SM, SM.count)], inc=CS)
    c3 = S.v(lambda e: e.tensor_sub(msu, miu, st.ident[:]), wait=[(CS, c2)], inc=CS)
    alog = sc.f32(1, rows=32)
    dtb = sc.f32(1, rows=32)
    negeA = sc.f32(1, rows=32)
    zz1 = S.v(lambda e: e.memset(alog, 0.0), inc=CS)
    zz2 = S.v(lambda e: e.memset(dtb, 0.0), wait=[(CS, zz1)], inc=CS)
    SM2 = S.sem("l1_sm2")
    S.sy(lambda e: e.dma_start(out=alog[0:6, :], in_=dram["l1_Alog_row"].rearrange("a b -> b a")),
         wait=[(CS, zz2)], inc=SM2, incn=16)
    S.sy(lambda e: e.dma_start(out=dtb[0:6, :], in_=dram["l1_dtb_row"].rearrange("a b -> b a")),
         inc=SM2, incn=16)
    e1 = S.s(lambda e: e.activation(negeA, alog, AF.Exp), wait=[(SM2, SM2.count)], inc=SS)
    S.v(lambda e: e.tensor_scalar_mul(negeA, negeA, -1.0), wait=[(SS, e1)], inc=CS)
    sm_tok = CS.count

    S_f = [sc.f32(256) for _ in range(NH)]
    S_bf = [sc.bf16(256) for _ in range(NH)]
    pv = S.sem("l1_pv")
    for h in range(NH):
        S.v(lambda e, d=S_f[h]: e.memset(d, 0.0), inc=pv)
        S.v(lambda e, d=S_bf[h]: e.memset(d, 0.0), inc=pv)
    onescol = sc.bf16(1)
    onesrow_f = sc.f32(128, rows=1)
    S.v(lambda e: e.memset(onescol, 1.0), inc=pv)
    S.v(lambda e: e.memset(onesrow_f, 1.0), inc=pv)
    qsliv = sc.bf16(6 * 4)
    ksliv = sc.bf16(6 * 4)
    vsliv = sc.bf16(12 * 4)
    for sl_ in (qsliv, ksliv, vsliv):
        S.v(lambda e, d=sl_: e.memset(d, 0.0), inc=pv)
    prep_tok = pv.count

    wbuf1 = sc.bf16(12288)
    wbuf2 = sc.bf16(12288)
    WS1 = S.sem("l1_ws1")
    WS2 = S.sem("l1_ws2")

    x2fm = sc.bf16(8 * 512)
    u_qh = sc.bf16(516)
    u_kh = sc.bf16(516)
    u_vh = sc.bf16(2 * 516)
    cqh = sc.bf16(512)
    ckh = sc.bf16(512)
    cvh = sc.bf16(2 * 512)
    cvt = sc.bf16(512)
    gate_tm = sc.bf16(1536)
    o_tm = sc.bf16(4 * 1536)
    g_fm = sc.f32(512, rows=32)
    b_fm = sc.f32(512, rows=32)
    c_fm = sc.f32(512, rows=32)
    sp_fm = sc.f32(512, rows=32)
    c_tm = sc.f32(4 * 6)
    b_tm = sc.f32(4 * 6)
    nc_tm = sc.f32(4 * 6)
    nb_tm = sc.f32(4 * 6)
    crow = sc.f32(512, rows=1)
    nrow = sc.f32(512, rows=1)
    erow = sc.f32(512, rows=1)
    rnk = sc.f32(512, rows=1)
    rnq = sc.f32(512, rows=1)
    rowsf = sc.f32(512, rows=1)
    rowsb = sc.bf16(5 * 512, rows=1)
    kbrow = sc.bf16(128, rows=1)
    sqb = sc.bf16(512)
    Xl = sc.bf16(128)
    Xu = sc.bf16(128)
    Gt = sc.bf16(128)
    Gn = sc.bf16(128)
    Pn = sc.bf16(128)
    Qn = sc.bf16(128)
    Qt = sc.bf16(128)
    Aqt = sc.bf16(128)
    kpf = sc.bf16(128)
    khf = sc.bf16(128)
    kbf = sc.bf16(128)
    qhf = sc.bf16(128)
    qpf = sc.bf16(128)
    kbt = sc.bf16(128)
    v_t = sc.bf16(256)
    Rt = sc.bf16(256)
    Dl = sc.bf16(256)
    gcol = sc.f32(1)
    ln_st = sc.f32(8)
    rres = sc.f32(1024)
    ogf = sc.bf16(12 * 128)
    ln_sq = st.ldbuf[:, 1024:2048]

    ps_bc = B[0]
    ps_kk = B[1]
    ps_qk = B[2]
    ps_pj = [B[3], B[4]]
    ps_d = [B[3], B[4], B[5]]
    ps_ks = B[6]
    ps_ab = B[6]
    ps_o = B[5]
    ps_tp = st.tp_bank[:, 0:128]

    gf = S.sem("l1_gf")
    tf = S.sem("l1_tf")
    df = S.sem("l1_df")
    RD = S.sem("l1_rd")

    for grp in range(NG):
        t0 = grp * 4
        S.sy(lambda e, d=wbuf1[:, 0:6144].rearrange("p (c o) -> p c o", c=8),
             s=cache["l1_qT"][:].rearrange("(c p) o -> p c o", p=128):
             e.dma_start(out=d, in_=s),
             wait=[(HS, cache_tok), (TN, TN.count)], inc=WS1, incn=16)
        S.sy(lambda e, d=wbuf1[:, 6144:12288].rearrange("p (c o) -> p c o", c=8),
             s=cache["l1_kT"][:].rearrange("(c p) o -> p c o", p=128):
             e.dma_start(out=d, in_=s), inc=WS1, incn=16)
        S.sy(lambda e, d=wbuf2[:].rearrange("p (c o) -> p c o", c=8),
             s=cache["l1_vT"][:].rearrange("(c p) o -> p c o", p=128):
             e.dma_start(out=d, in_=s),
             wait=[(TN, TN.count)], inc=WS2, incn=16)
        qkv_rdy1 = WS1.count
        qkv_rdy2 = WS2.count
        qTv = wbuf1[:, 0:6144]
        kTv = wbuf1[:, 6144:12288]
        vTv = wbuf2

        for tt in range(4):
            gt = t0 + tt
            for c in range(8):
                w = [(tf, tf.count)] if (grp > 0 or tt > 0 or c > 0) else [(pv, prep_tok)]
                if c == 0:
                    w = w + [st.tile_tok[gt]]
                tk = S.t(lambda e, sl=st.stream[:, gt * D + c * 128: gt * D + (c + 1) * 128]:
                         e.transpose(ps_tp, sl, st.ident[:]), wait=w, inc=TN)
                S.v(lambda e, d=x2fm[:].rearrange("p (a b) -> p a b", b=512)
                    [:, c, tt * 128:(tt + 1) * 128]:
                    e.tensor_copy(d, ps_tp), wait=[(TN, tk)], inc=tf)
        x2_tok = tf.count

        for which, wmat2 in ((0, aT32b), (1, bT32b)):
            for ic in range(8):
                w = [(df, df.count), (tf, x2_tok), (CS, sm_tok)] if ic == 0 and which == 0 else []
                S.t(lambda e, p2=ps_ab[32 * which:32 * which + 32, :],
                    wsl=wmat2[:, ic * 32:(ic + 1) * 32],
                    r=x2fm[:].rearrange("p (a b) -> p a b", b=512)[:, ic, :],
                    s0=(ic == 0), s1=(ic == 7):
                    e.matmul(p2, wsl, r, start=s0, stop=s1),
                    wait=w, inc=(TN if ic == 7 else None))
        ab_tok = TN.count
        v1 = S.v(lambda e, bb=dtb.broadcast_to([32, 512]):
                 e.tensor_tensor(g_fm, ps_ab[0:32, :], bb, ALU.add),
                 wait=[(TN, ab_tok), (SV, SV.count), (SS, SS.count)], inc=SV)
        v2 = S.s(lambda e: e.activation(sp_fm, g_fm, AF.Abs),
                 wait=[(SV, v1), (SS, SS.count)], inc=SS)
        s1 = S.s(lambda e: e.activation(sp_fm, sp_fm, AF.Exp, scale=-1.0),
                 wait=[(SS, v2)], inc=SS)
        s2 = S.s(lambda e: e.activation(sp_fm, sp_fm, AF.Ln, bias=1.0),
                 wait=[(SS, s1)], inc=SS)
        v3 = S.v(lambda e: e.tensor_relu(c_fm, g_fm), wait=[(SV, v1), (SS, v2)], inc=SV)
        v4 = S.v(lambda e: e.tensor_add(g_fm, c_fm, sp_fm),
                 wait=[(SV, v3), (SS, s2)], inc=SV)
        v5 = S.v(lambda e, bb=negeA.broadcast_to([32, 512]):
                 e.tensor_tensor(g_fm, g_fm, bb, ALU.mult),
                 wait=[(SV, v4)], inc=SV)
        s3 = S.s(lambda e: e.activation(b_fm, ps_ab[32:64, :], AF.Sigmoid),
                 wait=[(TN, ab_tok), (SS, s2), (SV, v4)], inc=SS)
        cprev = v5
        for cc in range(4):
            cprev = S.v(lambda e, d=c_fm[:, cc * 128:(cc + 1) * 128],
                        s=g_fm[:, cc * 128:(cc + 1) * 128]:
                        e.tensor_tensor_scan(d, s, s, 0.0, ALU.add, ALU.bypass),
                        wait=[(SV, cprev)], inc=SV)
        cums_tok = cprev
        for srcb, dstb, negb, wtok in ((c_fm, c_tm, nc_tm, cums_tok),
                                       (b_fm, b_tm, nb_tm, None)):
            for cc in range(4):
                w = [(SV, SV.count), (SS, SS.count), (TN, TN.count)]
                tk = S.t(lambda e, sl=srcb[0:6, cc * 128:(cc + 1) * 128]:
                         e.transpose(ps_qk[:, 0:6], sl, st.ident_f32[0:6, 0:6]),
                         wait=w, inc=TN)
                vv = S.v(lambda e, d=dstb[:, cc * 6:(cc + 1) * 6]:
                         e.tensor_copy(d, ps_qk[:, 0:6]), wait=[(TN, tk)], inc=SV)
                S.v(lambda e, d=negb[:, cc * 6:(cc + 1) * 6],
                    s=dstb[:, cc * 6:(cc + 1) * 6]:
                    e.tensor_scalar_mul(d, s, -1.0), wait=[(SV, vv)], inc=SV)
        ctb_tok = SV.count

        for h in range(NH):
            for wv_, ubuf, sliv, noc, base in ((qTv, u_qh, qsliv, 1, h),
                                               (kTv, u_kh, ksliv, 1, h),
                                               (vTv, u_vh, vsliv, 2, 2 * h)):
                for oc in range(noc):
                    psb = ps_pj[oc % 2]
                    for ic in range(8):
                        w = []
                        if ic == 0:
                            w = [(gf, gf.count), (SV, SV.count)]
                            if h == 0 and wv_ is qTv and oc == 0:
                                w += [(WS1, qkv_rdy1), (WS2, qkv_rdy2), (SV, ctb_tok),
                                      (df, df.count), (SS, SS.count)]
                        stride = 768 if noc == 1 else 1536
                        woff = ic * stride + (base + oc) * 128
                        S.t(lambda e, p2=psb, wsl=wv_[:, woff:woff + 128],
                            r=x2fm[:].rearrange("p (a b) -> p a b", b=512)[:, ic, :],
                            s0=(ic == 0), s1=(ic == 7):
                            e.matmul(p2[:], wsl, r, start=s0, stop=s1),
                            wait=w, inc=(TN if ic == 7 else None))
                    tk = TN.count
                    S.v(lambda e, d=ubuf[:].rearrange("p (a b) -> p a b", b=516)[:, oc, 0:4],
                        s=sliv[:, (base + oc) * 4:(base + oc + 1) * 4]:
                        e.tensor_copy(d, s), wait=[(SV, SV.count)], inc=SV)
                    S.v(lambda e, p2=psb, d=ubuf[:].rearrange("p (a b) -> p a b", b=516)
                        [:, oc, 4:516]:
                        e.tensor_copy(d, p2[:]),
                        wait=[(TN, tk), (SV, SV.count)], inc=gf)
            projh_tok = gf.count
            conv_sets = ((u_qh, cqh, convq, 1, h, qsliv), (u_kh, ckh, convk, 1, h, ksliv),
                         (u_vh, cvh, convv, 2, 2 * h, vsliv))
            for ubuf, cbuf, wcols, noc, base, sliv in conv_sets:
                vv = S.v(lambda e, d=cbuf[:].rearrange("p (a b) -> p a b", b=512)[:, 0:noc, :],
                         s=ubuf[:].rearrange("p (a b) -> p a b", b=516)[:, 0:noc, 1:513],
                         wt=wcols[:].rearrange("p (a b) -> p a b", b=4)
                         [:, base:base + noc, 0:1].broadcast_to([128, noc, 512]):
                         e.tensor_tensor(d, s, wt, ALU.mult),
                         wait=[(gf, projh_tok), (SV, SV.count)], inc=SV)
                for j in range(1, 4):
                    for oc in range(noc):
                        vt_ = S.v(lambda e, d=cvt[:],
                                  s=ubuf[:].rearrange("p (a b) -> p a b", b=516)
                                  [:, oc, 1 + j:513 + j],
                                  wt=wcols[:, (base + oc) * 4 + j:(base + oc) * 4 + j + 1]:
                                  e.tensor_scalar_mul(d, s, wt),
                                  wait=[(SV, SV.count)], inc=SV)
                        vv = S.v(lambda e, d=cbuf[:].rearrange("p (a b) -> p a b", b=512)[:, oc, :],
                                 s=cvt[:]:
                                 e.tensor_tensor(d, d, s, ALU.add),
                                 wait=[(SV, vt_)], inc=SV)
                S.v(lambda e, d=sliv[:, base * 4:(base + noc) * 4].rearrange(
                    "p (a b) -> p a b", b=4)[:, :, 1:4],
                    s=ubuf[:].rearrange("p (a b) -> p a b", b=516)[:, 0:noc, 513:516],
                    sl2=0:
                    e.tensor_copy(d, s), wait=[(SV, vv)], inc=SV)
                S.s(lambda e, d=cbuf[:, 0:noc * 512]:
                    e.activation(d, d, AF.Silu),
                    wait=[(SV, SV.count), (SS, SS.count)], inc=SS)
            conv_tok = SS.count
            import os as _os
            if _os.environ.get("L1A_DBG") == f"conv{grp}_{h}":
                DB = S.sem(f"dbg{grp}{h}")
                for nmb2, off in ((cqh, 0), (ckh, 512), (cvh, 1024)):
                    S.g(lambda e, d=dram["dbg"][:, off:off + (1024 if off == 1024 else 512)],
                        s=nmb2[:, 0:(1024 if off == 1024 else 512)]:
                        e.dma_start(out=d, in_=s),
                        wait=[(SS, conv_tok)], inc=DB, incn=16)
                S.waitonly("vector", [(DB, DB.count)])
                S.waitonly("scalar", [(DB, DB.count)])

            r1 = S.sy(lambda e, d=crow, s=c_fm[h:h + 1, :]: e.dma_start(out=d, in_=s),
                      wait=[(SV, ctb_tok), (TN, TN.count), (SS, SS.count),
                            (SV, SV.count)], inc=RD, incn=16)
            S.sy(lambda e, d=rowsf, s=b_fm[h:h + 1, :]: e.dma_start(out=d, in_=s),
                 wait=[(RD, RD.count)], inc=RD, incn=16)
            rd_tok = RD.count
            nv = S.v(lambda e: e.tensor_scalar_mul(nrow, crow, -1.0),
                     wait=[(RD, rd_tok), (SV, SV.count), (SS, SS.count)], inc=SV)
            bv = S.v(lambda e: e.tensor_copy(rowsb[0:1, 1536:2048], rowsf),
                     wait=[(SV, nv)], inc=SV)
            ev = S.s(lambda e: e.activation(erow, crow, AF.Exp),
                     wait=[(RD, rd_tok), (SS, conv_tok)], inc=SS)
            sqk = S.v(lambda e: e.tensor_tensor(sqb, ckh, ckh, ALU.mult),
                      wait=[(SS, conv_tok), (SV, bv)], inc=SV)
            tk1 = S.t(lambda e: e.matmul(ps_bc[0:1, :], onescol, sqb, start=True, stop=True),
                      wait=[(SV, sqk), (TN, TN.count)], inc=TN)
            sk1 = S.s(lambda e: e.activation(rnk, ps_bc[0:1, 0:512], AF.Ln,
                                             bias=st.c_eps6[0:1, :]),
                      wait=[(TN, tk1), (SS, ev), (SV, SV.count)], inc=SS)
            sk2 = S.s(lambda e: e.activation(rnk, rnk, AF.Exp, scale=-0.5),
                      wait=[(SS, sk1)], inc=SS)
            sqq = S.v(lambda e: e.tensor_tensor(sqb, cqh, cqh, ALU.mult),
                      wait=[(SV, sqk), (TN, tk1)], inc=SV)
            tk2 = S.t(lambda e: e.matmul(ps_bc[0:1, :], onescol, sqb, start=True, stop=True),
                      wait=[(SV, sqq), (SS, sk2)], inc=TN)
            sq1 = S.s(lambda e: e.activation(rnq, ps_bc[0:1, 0:512], AF.Ln,
                                             bias=st.c_eps6[0:1, :]),
                      wait=[(TN, tk2), (SS, sk2)], inc=SS)
            sq2 = S.s(lambda e: e.activation(rnq, rnq, AF.Exp, scale=-0.5,
                                             bias=st.c_qln[0:1, :]),
                      wait=[(SS, sq1)], inc=SS)
            w1 = S.v(lambda e: e.tensor_copy(rowsb[0:1, 0:512], rnk),
                     wait=[(SS, sq2), (SV, SV.count)], inc=SV)
            w2 = S.v(lambda e: e.tensor_mul(rowsf, rnk, erow), wait=[(SV, w1)], inc=SV)
            w3 = S.v(lambda e: e.tensor_copy(rowsb[0:1, 512:1024], rowsf),
                     wait=[(SV, w2)], inc=SV)
            w4 = S.v(lambda e: e.tensor_copy(rowsb[0:1, 1024:1536], rnq),
                     wait=[(SV, w3)], inc=SV)
            w5 = S.v(lambda e: e.tensor_copy(rowsb[0:1, 2048:2560], erow),
                     wait=[(SV, w4)], inc=SV)
            rows_tok = SV.count
            if _os.environ.get("L1A_DBG") == f"rows{grp}_{h}":
                DB2 = S.sem(f"dbg2{grp}{h}")
                for nmb2, off in ((crow, 0), (erow, 512), (rnk, 1024), (rnq, 1536)):
                    S.g(lambda e, d=dram["dbg"][0:1, off:off + 512], s=nmb2:
                        e.dma_start(out=d, in_=s),
                        wait=[(SV, rows_tok)], inc=DB2, incn=16)
                S.waitonly("vector", [(DB2, DB2.count)])
                S.waitonly("scalar", [(DB2, DB2.count)])
                S.waitonly("tensor", [(DB2, DB2.count)])

            for cc in range(4):
                cs = slice(cc * 128, (cc + 1) * 128)
                ccol = c_tm[:, cc * 6 + h:cc * 6 + h + 1]
                nccol = nc_tm[:, cc * 6 + h:cc * 6 + h + 1]
                nbcol = nb_tm[:, cc * 6 + h:cc * 6 + h + 1]
                bcol = b_tm[:, cc * 6 + h:cc * 6 + h + 1]
                for vh in range(2):
                    tkv = S.t(lambda e, sl=cvh[:].rearrange("p (a b) -> p a b", b=512)
                              [:, vh, cs]:
                              e.transpose(ps_tp, sl, st.ident[:]),
                              wait=[(tf, tf.count), (SV, rows_tok)], inc=TN)
                    S.v(lambda e, d=v_t[:, vh * 128:(vh + 1) * 128]:
                        e.tensor_copy(d, ps_tp), wait=[(TN, tkv)], inc=tf)
                vt_tok = tf.count
                tb1 = S.t(lambda e, r=rowsb[0:1, cs]:
                          e.matmul(ps_bc[:, 0:128], st.ones_row[:, 0:128], r,
                                   start=True, stop=True),
                          wait=[(SV, rows_tok), (TN, TN.count)], inc=TN)
                vk1 = S.v(lambda e, s=ckh[:, cs]:
                          e.tensor_mul(kpf, s, ps_bc[:, 0:128]),
                          wait=[(TN, tb1), (SV, SV.count)], inc=SV)
                tb2 = S.t(lambda e, r=rowsb[0:1, 512 + cc * 128:512 + (cc + 1) * 128]:
                          e.matmul(ps_bc[:, 128:256], st.ones_row[:, 0:128], r,
                                   start=True, stop=True),
                          wait=[(SV, vk1)], inc=TN)
                vk2 = S.v(lambda e, s=ckh[:, cs]:
                          e.tensor_mul(khf, s, ps_bc[:, 128:256]),
                          wait=[(TN, tb2)], inc=SV)
                tb3 = S.t(lambda e, r=rowsb[0:1, 1024 + cc * 128:1024 + (cc + 1) * 128]:
                          e.matmul(ps_bc[:, 256:384], st.ones_row[:, 0:128], r,
                                   start=True, stop=True),
                          wait=[(SV, vk2)], inc=TN)
                vq0 = S.v(lambda e, s=cqh[:, cs]:
                          e.tensor_mul(qpf, s, ps_bc[:, 256:384]),
                          wait=[(TN, tb3)], inc=SV)
                tb3b = S.t(lambda e, r=rowsb[0:1, 2048 + cc * 128:2048 + (cc + 1) * 128]:
                           e.matmul(ps_bc[:, 256:384], st.ones_row[:, 0:128], r,
                                    start=True, stop=True),
                           wait=[(SV, vq0)], inc=TN)
                vq1 = S.v(lambda e:
                          e.tensor_mul(qhf, qpf, ps_bc[:, 256:384]),
                          wait=[(TN, tb3b)], inc=SV)
                sd1 = S.s(lambda e, r=crow[0:1, cs],
                          bia=nrow[0:1, cc * 128 + 127:cc * 128 + 128]:
                          e.activation(kbrow, r, AF.Relu, bias=bia),
                          wait=[(SV, vq1), (SS, SS.count)], inc=SS)
                sd2 = S.s(lambda e: e.activation(kbrow, kbrow, AF.Exp, scale=-1.0),
                          wait=[(SS, sd1)], inc=SS)
                vd1 = S.v(lambda e, r=rnk[0:1, cs]:
                          e.tensor_mul(kbrow, kbrow, r),
                          wait=[(SS, sd2), (SV, SV.count)], inc=SV)
                tb4 = S.t(lambda e:
                          e.matmul(ps_bc[:, 384:512], st.ones_row[:, 0:128], kbrow,
                                   start=True, stop=True),
                          wait=[(SV, vd1)], inc=TN)
                vk3 = S.v(lambda e, s=ckh[:, cs]:
                          e.tensor_mul(kbf, s, ps_bc[:, 384:512]),
                          wait=[(TN, tb4)], inc=SV)
                tkb = S.t(lambda e: e.transpose(ps_tp, kbf, st.ident[:]),
                          wait=[(SV, vk3), (tf, vt_tok)], inc=TN)
                vkb = S.v(lambda e: e.tensor_copy(kbt, ps_tp), wait=[(TN, tkb)], inc=tf)
                tr1 = S.t(lambda e, r=crow[0:1, cs]:
                          e.matmul(ps_kk[:, 0:128], onesrow_f, r, start=True, stop=True),
                          wait=[(SV, SV.count), (TN, TN.count), (SS, SS.count)], inc=TN)
                if _os.environ.get("L1A_DBG", "").startswith("core"):
                    S.v(lambda e: e.tensor_copy(cvt[:, 0:128], ps_kk[:, 0:128]),
                        wait=[(TN, tr1), (SV, SV.count)], inc=SV)
                sx1 = S.s(lambda e, bia=nccol: e.activation(Xl, ps_kk[:, 0:128], AF.Relu,
                                                            scale=1.0, bias=bia),
                          wait=[(TN, tr1), (SS, SS.count), (SV, SV.count)], inc=SS)
                sx2 = S.s(lambda e: e.activation(Xl, Xl, AF.Exp, scale=-1.0),
                          wait=[(SS, sx1)], inc=SS)
                sx3 = S.s(lambda e, bia=ccol: e.activation(Xu, ps_kk[:, 0:128], AF.Relu,
                                                           scale=-1.0, bias=bia),
                          wait=[(SS, sx2)], inc=SS)
                sx4 = S.s(lambda e: e.activation(Xu, Xu, AF.Exp, scale=-1.0),
                          wait=[(SS, sx3)], inc=SS)
                sx5 = S.s(lambda e: e.activation(gcol, ps_kk[:, 127:128], AF.Exp),
                          wait=[(SS, sx4)], inc=SS)
                tkk = S.t(lambda e: e.matmul(ps_qk[:, 0:128], kpf, kpf,
                                             start=True, stop=True),
                          wait=[(SS, sx5), (SV, vk3)], inc=TN)
                tqk = S.t(lambda e: e.matmul(ps_qk[:, 128:256], kpf, qpf,
                                             start=True, stop=True), inc=TN)
                gv1 = S.v(lambda e: e.tensor_mul(Gt, Xl, ps_qk[:, 0:128]),
                          wait=[(TN, tkk), (SV, SV.count)], inc=SV)
                gv2 = S.v(lambda e: e.tensor_mul(Gt, Gt, msl), wait=[(SV, gv1)], inc=SV)
                gv3 = S.v(lambda e, bia=nbcol: e.tensor_scalar_mul(Gt, Gt, bia),
                          wait=[(SV, gv2)], inc=SV)
                tb5 = S.t(lambda e, r=rowsb[0:1, 1536 + cc * 128:1536 + (cc + 1) * 128]:
                          e.matmul(ps_bc[:, 0:128], st.ones_row[:, 0:128], r,
                                   start=True, stop=True),
                          wait=[(SV, gv1)], inc=TN)
                gn1 = S.v(lambda e: e.tensor_mul(Gn, Xu, ps_qk[:, 0:128]),
                          wait=[(TN, tb5), (SV, gv3)], inc=SV)
                gn2 = S.v(lambda e: e.tensor_mul(Gn, Gn, msu), wait=[(SV, gn1)], inc=SV)
                gn3 = S.v(lambda e: e.tensor_mul(Gn, Gn, ps_bc[:, 0:128]),
                          wait=[(SV, gn2)], inc=SV)
                gn4 = S.v(lambda e: e.tensor_scalar_mul(Gn, Gn, -1.0),
                          wait=[(SV, gn3)], inc=SV)
                aq1 = S.v(lambda e: e.tensor_mul(Aqt, Xu, ps_qk[:, 128:256]),
                          wait=[(TN, tqk), (SV, gn4)], inc=SV)
                aq2 = S.v(lambda e: e.tensor_mul(Aqt, Aqt, miu), wait=[(SV, aq1)], inc=SV)
                pv1 = S.v(lambda e: e.tensor_add(Pn, Gn, st.ident[:]),
                          wait=[(SV, gn4)], inc=SV)
                tq1 = S.t(lambda e: e.matmul(ps_d[0][:, 0:128], Gt, Gn,
                                             start=True, stop=True),
                          wait=[(SV, pv1), (SV, aq2)], inc=TN)
                tq2 = S.t(lambda e: e.matmul(ps_d[1][:, 0:128], Gn, Gt,
                                             start=True, stop=True), inc=TN)
                qtokn = S.v(lambda e: e.tensor_copy(Qn, ps_d[0][:, 0:128]),
                            wait=[(TN, tq2)], inc=SV)
                qtokt = S.v(lambda e: e.tensor_copy(Qt, ps_d[1][:, 0:128]),
                            wait=[(TN, tq2)], inc=SV)
                ptok = pv1
                for it in range(6):
                    tqp = S.t(lambda e: e.matmul(ps_d[2][:, 0:128], Qt, Pn,
                                                 start=True, stop=True),
                              wait=[(SV, qtokt), (SV, ptok)], inc=TN)
                    ptok = S.v(lambda e: e.tensor_add(Pn, Pn, ps_d[2][:, 0:128]),
                               wait=[(TN, tqp)], inc=SV)
                    if it < 5:
                        ta = S.t(lambda e: e.matmul(ps_d[0][:, 0:128], Qt, Qn,
                                                    start=True, stop=True),
                                 wait=[(SV, ptok)], inc=TN)
                        tb_ = S.t(lambda e: e.matmul(ps_d[1][:, 0:128], Qn, Qt,
                                                     start=True, stop=True), inc=TN)
                        qtokn = S.v(lambda e: e.tensor_copy(Qn, ps_d[0][:, 0:128]),
                                    wait=[(TN, tb_)], inc=SV)
                        qtokt = S.v(lambda e: e.tensor_copy(Qt, ps_d[1][:, 0:128]),
                                    wait=[(TN, tb_)], inc=SV)
                tks = S.t(lambda e, sb=S_bf[h]: e.matmul(ps_ks[:, 0:256], khf, sb,
                                                         start=True, stop=True),
                          wait=[(SV, ptok), (tf, tf.count)], inc=TN)
                rv1 = S.v(lambda e: e.tensor_sub(Rt, v_t, ps_ks[:, 0:256]),
                          wait=[(TN, tks), (SV, SV.count)], inc=SV)
                rv2 = S.v(lambda e, bc=bcol: e.tensor_scalar_mul(Rt, Rt, bc),
                          wait=[(SV, rv1)], inc=SV)
                td = S.t(lambda e: e.matmul(ps_d[2][:, 0:256], Pn, Rt,
                                            start=True, stop=True),
                         wait=[(SV, rv2)], inc=TN)
                dv = S.v(lambda e: e.tensor_copy(Dl, ps_d[2][:, 0:256]),
                         wait=[(TN, td)], inc=SV)
                to1 = S.t(lambda e: e.matmul(ps_o[:, 0:256], Aqt, Dl,
                                             start=True, stop=False),
                          wait=[(SV, dv)], inc=TN)
                to2 = S.t(lambda e, sb=S_bf[h]: e.matmul(ps_o[:, 0:256], qhf, sb,
                                                         start=False, stop=True), inc=TN)
                ov = S.v(lambda e, d=o_tm[:].rearrange("p (a b) -> p a b", b=1536)
                         [:, cc, h * 256:(h + 1) * 256]:
                         e.tensor_copy(d, ps_o[:, 0:256]), wait=[(TN, to2)], inc=SV)
                tsu = S.t(lambda e: e.matmul(ps_d[0][:, 0:256], kbt, Dl,
                                             start=True, stop=True),
                          wait=[(SV, ov)], inc=TN)
                su1 = S.v(lambda e, d=S_f[h]: e.tensor_scalar_mul(d, d, gcol),
                          wait=[(SS, sx5), (TN, to2), (SV, ov)], inc=SV)
                su2 = S.v(lambda e, d=S_f[h]: e.tensor_add(d, d, ps_d[0][:, 0:256]),
                          wait=[(TN, tsu), (SV, su1)], inc=SV)
                S.v(lambda e, d=S_bf[h], s=S_f[h]: e.tensor_copy(d, s),
                    wait=[(SV, su2)], inc=df)
                if _os.environ.get("L1A_DBG") == f"core{grp}_{h}_{cc}":
                    DB4 = S.sem("dbg4")
                    for nmb2, off, wd in ((cvt[:, 0:128], 2200, 128), (nc_tm, 2328, 24),
                                          (c_tm, 2048, 24), (b_tm, 2100, 24),
                                          (Xl, 0, 128), (Xu, 128, 128), (Gt, 256, 128),
                                          (Gn, 384, 128), (Pn, 512, 128), (Aqt, 640, 128),
                                          (Rt, 768, 256), (Dl, 1024, 256), (kpf, 1280, 128),
                                          (qpf, 1408, 128), (khf, 1536, 128), (v_t, 1664, 256),
                                          (kbt, 1920, 128)):
                        S.g(lambda e, d=dram["dbg"][:, off:off + wd], s=nmb2:
                            e.dma_start(out=d, in_=s), wait=[(df, df.count)], inc=DB4, incn=16)
                    S.waitonly("vector", [(DB4, DB4.count)])
                    S.waitonly("scalar", [(DB4, DB4.count)])
                    S.waitonly("tensor", [(DB4, DB4.count)])
        delta_tok = df.count
        if _os.environ.get("L1A_DBG") == f"otm{grp}":
            DB3 = S.sem(f"dbg3{grp}")
            S.g(lambda e, d=dram["dbg"][:, 0:6144], s=o_tm[:]:
                e.dma_start(out=d, in_=s), wait=[(df, delta_tok)], inc=DB3, incn=16)
            S.waitonly("vector", [(DB3, DB3.count)])

        S.sy(lambda e, d=wbuf1[:].rearrange("p (c o) -> p c o", c=8),
             s=cache["l1_gT"][:].rearrange("(c p) o -> p c o", p=128):
             e.dma_start(out=d, in_=s),
             wait=[(df, delta_tok), (TN, TN.count)], inc=WS1, incn=16)
        S.sy(lambda e, d=wbuf2[:].rearrange("p (c o) -> p c o", c=12),
             s=cache["l1_oT"][:].rearrange("(c p) o -> p c o", p=128):
             e.dma_start(out=d, in_=s),
             wait=[(df, delta_tok), (TN, TN.count)], inc=WS2, incn=16)
        go_rdy1 = WS1.count
        go_rdy2 = WS2.count

        for cc in range(4):
            gt = t0 + cc
            for ocg in range(3):
                psb = ps_pj[ocg % 2]
                for ic in range(8):
                    w = []
                    if ic == 0:
                        w = [(TN, TN.count), (df, delta_tok), (SV, SV.count),
                             (SS, SS.count)]
                        if cc == 0 and ocg == 0:
                            w += [(WS1, go_rdy1), (WS2, go_rdy2)]
                    S.t(lambda e, p2=psb,
                        lhs=x2fm[:].rearrange("p (a b) -> p a b", b=512)
                        [:, ic, cc * 128:(cc + 1) * 128],
                        wsl=wbuf1[:, ic * 1536 + ocg * 512: ic * 1536 + (ocg + 1) * 512],
                        s0=(ic == 0), s1=(ic == 7):
                        e.matmul(p2[:], lhs, wsl, start=s0, stop=s1),
                        wait=w, inc=(TN if ic == 7 else None))
                tkg = TN.count
                S.s(lambda e, p2=psb, d=gate_tm[:, ocg * 512:(ocg + 1) * 512]:
                    e.activation(d, p2[:], AF.Silu),
                    wait=[(TN, tkg), (SS, SS.count)], inc=SS)
            gate_tok = SS.count
            og = o_tm[:].rearrange("p (a b) -> p a b", b=1536)[:, cc, :]
            m1 = S.v(lambda e, d=og: e.tensor_mul(d, d, gate_tm),
                     wait=[(SS, gate_tok), (SV, SV.count)], inc=SV)
            m2 = S.s(lambda e, d=cvh[:, 0:1024], s=og[:, 0:1024]:
                     e.activation(d, s, AF.Square, accum_out=ln_st[:, 1:2]),
                     wait=[(SV, m1), (SS, SS.count)], inc=SS)
            m2b = S.s(lambda e, d=cvh[:, 0:512], s=og[:, 1024:1536]:
                      e.activation(d, s, AF.Square, accum_out=ln_st[:, 4:5]),
                      wait=[(SS, m2)], inc=SS)
            m3 = S.v(lambda e: e.tensor_scalar(ln_st[:, 2:3], ln_st[:, 1:2],
                                               ln_st[:, 4:5], None, ALU.add),
                     wait=[(SS, m2b), (SV, SV.count)], inc=SV)
            m3b = S.v(lambda e: e.tensor_scalar(ln_st[:, 2:3], ln_st[:, 2:3],
                                                1.0 / 1536, None, ALU.mult),
                      wait=[(SV, m3)], inc=SV)
            m4 = S.s(lambda e: e.activation(ln_st[:, 3:4], ln_st[:, 2:3], AF.Sqrt,
                                            bias=st.c_eps_ln),
                     wait=[(SV, m3b)], inc=SS)
            m5 = S.v(lambda e: e.reciprocal(ln_st[:, 3:4], ln_st[:, 3:4]),
                     wait=[(SS, m4)], inc=SV)
            m6 = S.v(lambda e, d=og: e.tensor_scalar_mul(d, d, ln_st[:, 3:4]),
                     wait=[(SV, m5)], inc=SV)
            for ic in range(12):
                tko = S.t(lambda e, sl=og[:, ic * 128:(ic + 1) * 128]:
                          e.transpose(ps_tp, sl, st.ident[:]),
                          wait=[(SV, m6), (tf, tf.count)], inc=TN)
                S.v(lambda e, d=ogf[:, ic * 128:(ic + 1) * 128]:
                    e.tensor_copy(d, ps_tp), wait=[(TN, tko)], inc=tf)
            og_tok = tf.count
            for oc2 in range(2):
                psb = ps_pj[oc2]
                for ic in range(12):
                    w = [(tf, og_tok)] if ic == 0 else []
                    S.t(lambda e, p2=psb, lhs=ogf[:, ic * 128:(ic + 1) * 128],
                        wsl=wbuf2[:, ic * 1024 + oc2 * 512: ic * 1024 + (oc2 + 1) * 512],
                        s0=(ic == 0), s1=(ic == 11):
                        e.matmul(p2[:], lhs, wsl, start=s0, stop=s1),
                        wait=w, inc=(TN if ic == 11 else None))
            tko2 = TN.count
            wprev = [st.tile_tok[gt - 1]] if gt > 0 else []
            rv1 = S.v(lambda e, p2=ps_pj[0], d=rres[:, 0:512],
                      s=st.stream[:, gt * D: gt * D + 512]:
                      e.tensor_add(d, p2[:], s),
                      wait=[(TN, tko2), (SS, SS.count), (SV, SV.count)] + wprev, inc=SV)
            S.v(lambda e, p2=ps_pj[1], d=rres[:, 512:1024],
                s=st.stream[:, gt * D + 512: gt * D + 1024]:
                e.tensor_add(d, p2[:], s), wait=[(SV, rv1)], inc=SV)
            res_tok = SV.count
            st.tile_tok[gt] = emit_ln(nc, es, S, st, rres, ln_sq, ln_st,
                                      st.stream[:, gt * D:(gt + 1) * D],
                                      wait=[(SV, res_tok)])

    st.l1a_end = {"TN": TN.count}
    sc.close()


def emit_l1_ffn(nc, es, S, st, final_out=False):
    nt = st.nt
    dram = st.dram
    SV, SS, TN = st.SV, st.SS, st.TN
    B = st.banks
    phase_barrier(S, st)
    sc = Scope(st.arena)

    cache = {
        "g": nc.dram_tensor("c_wgq", [1024, 2816], BF16, kind="Internal").ap(),
        "u": nc.dram_tensor("c_wuq", [1024, 2816], BF16, kind="Internal").ap(),
        "d": nc.dram_tensor("c_wdq", [2816, 1024], BF16, kind="Internal").ap(),
    }

    def emit_quant2(xin, xout, nf, qu, qr, bmbuf, tbuf, wait0, dbg_tag=None):
        nb = nf // 16
        a1 = S.v(lambda e: e.reduce_max(
            bmbuf.rearrange("p (a b) -> p a b", b=1),
            xin.rearrange("p (a b) -> p a b", b=16), axis=AX.X,
            apply_absolute_value=True), wait=wait0 + [(SV, SV.count)], inc=SV)
        a2 = S.v(lambda e: e.tensor_scalar(tbuf, bmbuf, 1.0 / 12.0, 5e-13,
                                           ALU.mult, ALU.max), wait=[(SV, a1)], inc=SV)
        a2b = S.v(lambda e: e.tensor_scalar_mul(bmbuf, tbuf, 1.0000005),
                  wait=[(SV, a2)], inc=SV)
        a3 = S.v(lambda e: e.reciprocal(bmbuf, bmbuf), wait=[(SV, a2b)], inc=SV)
        a4 = S.s(lambda e: e.activation(qu, xin, AF.Abs),
                 wait=wait0 + [(SS, SS.count), (SV, a1)], inc=SS)
        a5 = S.v(lambda e, bb=bmbuf.rearrange("p (a b) -> p a b", b=1)
                 .broadcast_to([128, nb, 16]):
                 e.tensor_tensor(qu.rearrange("p (a b) -> p a b", b=16),
                                 qu.rearrange("p (a b) -> p a b", b=16),
                                 bb, ALU.mult),
                 wait=[(SV, a3), (SS, a4)], inc=SV)
        import os as _os
        if dbg_tag and _os.environ.get("L1F_DBG") == dbg_tag:
            DQ1 = S.sem("dbq1")
            S.g(lambda e, d=dram["dbg"][:, 4096:4096 + nf], s=qu:
                e.dma_start(out=d, in_=s), wait=[(SV, a5)], inc=DQ1, incn=16)
            S.g(lambda e, d=dram["dbg"][:, 6656:6656 + min(nf, 1024)], s=xin[:, 0:min(nf, 1024)]:
                e.dma_start(out=d, in_=s), wait=[(SV, a5)], inc=DQ1, incn=16)
            S.g(lambda e, d=dram["dbg"][0:128, 7680:7680 + nf // 16], s=bmbuf:
                e.dma_start(out=d, in_=s), wait=[(SV, a2)], inc=DQ1, incn=16)
            S.waitonly("vector", [(DQ1, DQ1.count)])
            S.waitonly("scalar", [(DQ1, DQ1.count)])
        a6 = S.v(lambda e: e.tensor_scalar_max(qr, qu, 1.0), wait=[(SV, a5)], inc=SV)
        a7 = S.v(lambda e: e.tensor_scalar_add(qr.bitcast(I32), qr.bitcast(I32),
                                               2097151), wait=[(SV, a6)], inc=SV)
        a8 = S.v(lambda e: e.tensor_scalar(qr.bitcast(I32), qr.bitcast(I32),
                                           -4194304, None, ALU.bitwise_and),
                 wait=[(SV, a7)], inc=SV)
        a8m = S.v(lambda e: e.scalar_tensor_tensor(qr, qu, 0.5, qr,
                                                   ALU.is_gt, ALU.mult),
                  wait=[(SV, a8)], inc=SV)
        aD1 = S.v(lambda e: e.tensor_scalar(qu, qu, 1.5, 0.5, ALU.is_gt, ALU.subtract),
                  wait=[(SV, a8m)], inc=SV)
        aD2 = S.v(lambda e: e.scalar_tensor_tensor(qu, qr, 1.5, qu,
                                                   ALU.is_equal, ALU.mult),
                  wait=[(SV, aD1)], inc=SV)
        a8 = S.v(lambda e: e.tensor_add(qr, qr, qu), wait=[(SV, aD2)], inc=SV)
        if dbg_tag and _os.environ.get("L1F_DBG") == dbg_tag:
            DQ2 = S.sem("dbq2")
            S.g(lambda e, d=dram["dbg"][:, 5632:5632 + nf], s=qr:
                e.dma_start(out=d, in_=s), wait=[(SV, a8)], inc=DQ2, incn=16)
            S.waitonly("vector", [(DQ2, DQ2.count)])
        aA = S.s(lambda e: e.activation(qu, xin, AF.Sign),
                 wait=[(SV, a8), (SS, SS.count)], inc=SS)
        aB = S.v(lambda e: e.tensor_mul(qr, qr, qu), wait=[(SS, aA)], inc=SV)
        aC = S.v(lambda e, bb=tbuf.rearrange("p (a b) -> p a b", b=1)
                 .broadcast_to([128, nb, 16]):
                 e.tensor_tensor(xout.rearrange("p (a b) -> p a b", b=16),
                                 qr.rearrange("p (a b) -> p a b", b=16),
                                 bb, ALU.mult),
                 wait=[(SV, aB)], inc=SV)
        return aC

    WL = S.sem("f_wl")
    CSQ = S.sem("f_csq")
    HSQ = S.sem("f_hsq")
    TFQ = S.sem("f_tfq")
    psc = Scope(st.arena)
    wstg2 = [psc.f32(2816), psc.f32(2816)]
    wbf2 = [psc.bf16(2816), psc.bf16(2816)]
    tstg2 = [psc.bf16(22 * 128), psc.bf16(22 * 128)]
    qu = sc.f32(2816)
    qr = sc.f32(2816)
    qbm = sc.f32(2816 // 16)
    qt = sc.f32(2816 // 16)
    ps_tp = st.tp_bank[:, 0:128]
    ps_tp2 = st.banks[6][:].bitcast(BF16)[:, 0:128]
    WL2 = [S.sem("f_wl0"), S.sem("f_wl1")]
    HSQ2 = [S.sem("f_hsq0"), S.sem("f_hsq1")]
    hist = [dict(), dict()]
    ri = 0
    for wnm, dnm, rows, cols in (("l1_wgate", "g", 22, 1024),
                                 ("l1_wup", "u", 22, 1024),
                                 ("l1_wdown", "d", 8, 2816)):
        npc = cols // 128
        for rc in range(rows):
            sl_ = ri % 2
            wstg, wbf, tstg = wstg2[sl_], wbf2[sl_], tstg2[sl_]
            h = hist[sl_]
            lw = [st.tile_tok[nt - 1]] if ri == 0 else []
            if "xin" in h:
                lw.append(h["xin"])
            S.sy(lambda e, d=wstg[:, 0:cols],
                 s=dram[wnm][rc * 128:(rc + 1) * 128, :]:
                 e.dma_start(out=d, in_=s), wait=lw, inc=WL2[sl_], incn=16)
            qtok = emit_quant2(wstg[:, 0:cols], wbf[:, 0:cols], cols,
                               qu[:, 0:cols], qr[:, 0:cols],
                               qbm[:, 0:cols // 16],
                               qt[:, 0:cols // 16],
                               wait0=[(WL2[sl_], WL2[sl_].count)] +
                                     ([h["tp"]] if "tp" in h else []))
            h["xin"] = (SS, SS.count)
            cp_toks = []
            for pc in range(npc):
                tb = ps_tp if pc % 2 == 0 else ps_tp2
                w = [(SV, qtok)]
                if pc == 0 and "store" in h:
                    w.append(h["store"])
                if pc in (0, 1):
                    w.append((TFQ, TFQ.count))
                else:
                    w.append((TFQ, cp_toks[pc - 2]))
                tk = S.t(lambda e, sl=wbf[:, pc * 128:(pc + 1) * 128], d=tb:
                         e.transpose(d, sl, st.ident[:]),
                         wait=w, inc=TN)
                ck = S.v(lambda e, d=tstg[:, pc * 128:(pc + 1) * 128], s=tb:
                         e.tensor_copy(d, s), wait=[(TN, tk)], inc=TFQ)
                cp_toks.append(ck)
            st_tok = TFQ.count
            h["tp"] = (TN, TN.count)
            S.sy(lambda e, d=cache[dnm].rearrange("(c p) o -> p c o", p=128)
                 [:, :, rc * 128:(rc + 1) * 128],
                 s=tstg[:, 0:npc * 128].rearrange("p (c o) -> p c o", c=npc):
                 e.dma_start(out=d, in_=s),
                 wait=[(TFQ, st_tok)], inc=HSQ2[sl_], incn=16)
            h["store"] = (HSQ2[sl_], HSQ2[sl_].count)
            ri += 1
    cache_tok = None
    cache_rdy = [(HSQ2[0], HSQ2[0].count), (HSQ2[1], HSQ2[1].count)]
    S.waitonly("sync", cache_rdy)
    import os as _os
    if _os.environ.get("L1F_DBG") == "wg":
        DW = S.sem("dbgw")
        S.g(lambda e, d=dram["dbg"][:, 0:2816],
            s=cache["g"][0:128, :]: e.dma_start(out=d, in_=s),
            wait=cache_rdy, inc=DW, incn=16)
        S.g(lambda e, d=dram["dbg"][:, 2816:5632],
            s=cache["g"].rearrange("(c p) o -> p c o", p=128)[:, 1, :]:
            e.dma_start(out=d, in_=s), inc=DW, incn=16)
        S.waitonly("sync", [(DW, DW.count)])
    psc.close()

    xq = sc.bf16(1024)
    xqfm = sc.bf16(4 * 1024)
    hbuf = sc.f32(4 * 2816)
    hqfm4 = sc.bf16(4 * 2816)
    wg1 = sc.bf16(8 * 512)
    wu1 = sc.bf16(8 * 512)
    wd1 = sc.bf16(22 * 128)
    t1sb = st.ldbuf[:, 0:512]
    ln_sq = st.ldbuf[:, 1024:2048]
    ln_st = sc.f32(8)
    rres = sc.f32(1024)
    outf = sc.f32(1024) if final_out else None

    WG = [S.sem("f_wg0"), S.sem("f_wg1")]
    WD = [S.sem("f_wd0"), S.sem("f_wd1")]
    qf = S.sem("f_qf")
    tf = S.sem("f_tf")
    hf = S.sem("f_hf")
    OF = S.sem("f_of")

    ps_g = B[1]
    ps_u = B[2]
    ps_d2 = [B[3], B[4]]

    NSUP = nt // 4
    ocg_sizes = [512, 512, 512, 512, 512, 256]
    wd_last = None
    hq_war = {tt: (TN, 0) for tt in range(4)}
    ev_hist = []
    for sp in range(NSUP):
        t0 = sp * 4
        for tt in range(4):
            gt = t0 + tt
            qtok = emit_quant2(st.stream[:, gt * D:(gt + 1) * D], xq[:],
                               D, qu[:, 0:D], qr[:, 0:D],
                               qbm[:, 0:64], qt[:, 0:64],
                               wait0=[st.tile_tok[gt], (TN, TN.count)] + (cache_rdy if sp == 0 else []),
                               dbg_tag=f"xq{sp}_{tt}")
            import os as _os
            if _os.environ.get("L1F_DBG") == f"xq{sp}_{tt}":
                DBX = S.sem(f"dbgx{sp}{tt}")
                S.g(lambda e, d=dram["dbg"][:, 0:1024], s=xq[:]:
                    e.dma_start(out=d, in_=s), wait=[(SV, qtok)], inc=DBX, incn=16)
                S.g(lambda e, d=dram["dbg"][:, 1024:2048], s=qu[:, 0:1024]:
                    e.dma_start(out=d, in_=s), wait=[(SV, qtok)], inc=DBX, incn=16)
                S.g(lambda e, d=dram["dbg"][:, 2048:3072], s=qr[:, 0:1024]:
                    e.dma_start(out=d, in_=s), wait=[(SV, qtok)], inc=DBX, incn=16)
                S.g(lambda e, d=dram["dbg"][:, 3072:3136], s=qbm[:, 0:64]:
                    e.dma_start(out=d, in_=s), wait=[(SV, qtok)], inc=DBX, incn=16)
                S.g(lambda e, d=dram["dbg"][:, 3136:3200], s=qt[:, 0:64]:
                    e.dma_start(out=d, in_=s), wait=[(SV, qtok)], inc=DBX, incn=16)
                S.waitonly("vector", [(DBX, DBX.count)])
                S.waitonly("tensor", [(DBX, DBX.count)])
            for c in range(8):
                tk = S.t(lambda e, sl=xq[:, c * 128:(c + 1) * 128]:
                         e.transpose(ps_tp, sl, st.ident[:]),
                         wait=[(SV, qtok), (tf, tf.count), (TN, TN.count)], inc=TN)
                S.v(lambda e, d=xqfm[:, tt * D + c * 128: tt * D + (c + 1) * 128]:
                    e.tensor_copy(d, ps_tp), wait=[(TN, tk)], inc=tf)
        xq_tok = tf.count

        off = 0
        for ocg, osz in enumerate(ocg_sizes):
            sel = 0
            wgb = wg1
            wub = wu1
            S.sy(lambda e, d=wgb[:, 0:8 * osz].rearrange("p (c o) -> p c o", c=8),
                 s=cache["g"].rearrange("(c p) o -> p c o", p=128)[:, :, off:off + osz]:
                 e.dma_start(out=d, in_=s),
                 wait=(cache_rdy if sp == 0 and ocg == 0 else []) + [(TN, TN.count)], inc=WG[0], incn=16)
            S.sy(lambda e, d=wub[:, 0:8 * osz].rearrange("p (c o) -> p c o", c=8),
                 s=cache["u"].rearrange("(c p) o -> p c o", p=128)[:, :, off:off + osz]:
                 e.dma_start(out=d, in_=s),
                 wait=[(TN, TN.count)], inc=WG[0], incn=16)
            wrdy = WG[0].count
            for tt in range(4):
                for which, wb in ((0, wgb), (1, wub)):
                    psb = ps_g if which == 0 else ps_u
                    for ic in range(8):
                        w = []
                        if ic == 0:
                            w = [(hf, hf.count)]
                            if tt == 0 and which == 0:
                                w += [(WG[0], wrdy), (tf, xq_tok)]
                        S.t(lambda e, p2=psb[:, 0:osz],
                            lhs=xqfm[:, tt * D + ic * 128: tt * D + (ic + 1) * 128],
                            wsl=wb[:, ic * osz:(ic + 1) * osz],
                            s0=(ic == 0), s1=(ic == 7):
                            e.matmul(p2, lhs, wsl, start=s0, stop=s1),
                            wait=w, inc=(TN if ic == 7 else None))
                tkgu = TN.count
                sg = S.s(lambda e, p2=ps_g[:, 0:osz], d=t1sb[:, 0:osz]:
                         e.activation(d, p2, AF.Silu),
                         wait=[(TN, tkgu), (SS, SS.count)], inc=SS)
                S.v(lambda e, p2=ps_u[:, 0:osz],
                    d=hbuf[:, tt * 2816 + off: tt * 2816 + off + osz], s=t1sb[:, 0:osz]:
                    e.tensor_mul(d, s, p2),
                    wait=[(SS, sg), (SV, SV.count)], inc=hf)
            off += osz
        h_tok = hf.count

        hq_rdy = []
        for tt in range(4):
            if _os.environ.get("L1F_DBG") == f"h{sp}_{tt}":
                DBH2 = S.sem(f"dbgh2{sp}{tt}")
                S.g(lambda e, d=dram["dbg"][:, 0:2816], s=hbuf[:, tt * 2816:(tt + 1) * 2816]:
                    e.dma_start(out=d, in_=s), wait=[(hf, h_tok)], inc=DBH2, incn=16)
                S.waitonly("vector", [(DBH2, DBH2.count)])
                S.waitonly("tensor", [(DBH2, DBH2.count)])
            hqv = hbuf[:, tt * 2816:(tt + 1) * 2816].bitcast(BF16)[:, 0:2816]
            qtok = emit_quant2(hbuf[:, tt * 2816:(tt + 1) * 2816],
                               hqv, 2816,
                               qu[:, 0:2816], qr[:, 0:2816],
                               qbm[:, 0:176], qt[:, 0:176],
                               wait0=[(hf, h_tok)])
            if _os.environ.get("L1F_DBG") == f"hq{sp}_{tt}":
                DBH = S.sem(f"dbgh{sp}{tt}")
                S.g(lambda e, d=dram["dbg"][:, 0:2816], s=hqv:
                    e.dma_start(out=d, in_=s), wait=[(SV, qtok)], inc=DBH, incn=16)
                S.waitonly("vector", [(DBH, DBH.count)])
                S.waitonly("tensor", [(DBH, DBH.count)])
            cp2 = []
            for c in range(22):
                tb = ps_tp if c % 2 == 0 else ps_tp2
                w = [(SV, qtok)]
                if c in (0, 1):
                    w.append((tf, tf.count))
                else:
                    w.append((tf, cp2[c - 2]))
                if c == 0:
                    w.append(hq_war[tt])
                tk = S.t(lambda e, sl=hqv[:, c * 128:(c + 1) * 128], d=tb:
                         e.transpose(d, sl, st.ident[:]), wait=w, inc=TN)
                ck = S.v(lambda e, d=hqfm4[:, tt * 2816 + c * 128:
                                          tt * 2816 + (c + 1) * 128], s=tb:
                         e.tensor_copy(d, s), wait=[(TN, tk)], inc=tf)
                cp2.append(ck)
            hq_rdy.append((tf, tf.count))

        for oc8 in range(8):
            lw = list(cache_rdy) if sp == 0 and oc8 == 0 else []
            if wd_last is not None:
                lw.append(wd_last)
            S.sy(lambda e, d=wd1[:].rearrange("p (c o) -> p c o", c=22),
                 s=cache["d"].rearrange("(c p) o -> p c o", p=128)
                 [:, :, oc8 * 128:(oc8 + 1) * 128]:
                 e.dma_start(out=d, in_=s), wait=lw, inc=WD[0], incn=16)
            wd_tok = WD[0].count
            for tt in range(4):
                gi = oc8 * 4 + tt
                psb = ps_d2[gi % 2]
                for ic in range(22):
                    w = []
                    if ic == 0:
                        w = [(WD[0], wd_tok), hq_rdy[tt]]
                        if len(ev_hist) >= 2:
                            w.append(ev_hist[-2])
                        if oc8 == 0 and tt == 0:
                            w.append((OF, OF.count))
                    S.t(lambda e, p2=psb[:, 0:128],
                        lhs=hqfm4[:, tt * 2816 + ic * 128: tt * 2816 + (ic + 1) * 128],
                        wsl=wd1[:, ic * 128:(ic + 1) * 128],
                        s0=(ic == 0), s1=(ic == 21):
                        e.matmul(p2, lhs, wsl, start=s0, stop=s1),
                        wait=w, inc=(TN if ic == 21 else None))
                tkd = TN.count
                if tt == 3:
                    wd_last = (TN, tkd)
                ev = S.v(lambda e, p2=psb[:, 0:128],
                         d=hbuf[:, tt * 2816 + 1408 + oc8 * 128: tt * 2816 + 1408 + (oc8 + 1) * 128]:
                         e.tensor_copy(d, p2), wait=[(TN, tkd), (SV, SV.count)], inc=OF)
                ev_hist.append((OF, ev))
        for tt in range(4):
            hq_war[tt] = wd_last
        dn_ev = OF.count

        for tt in range(4):
            gt = t0 + tt
            wprev = [st.tile_tok[gt - 1]] if gt > 0 else []
            rv1 = S.v(lambda e, d=rres[:, 0:1024],
                      p2=hbuf[:, tt * 2816 + 1408: tt * 2816 + 2432],
                      s=st.stream[:, gt * D: (gt + 1) * D]:
                      e.tensor_add(d, p2, s),
                      wait=[(OF, dn_ev), (SV, SV.count)] + wprev, inc=OF)
            res_tok = OF.count
            if final_out:
                st.tile_tok[gt] = emit_ln(nc, es, S, st, rres, ln_sq, ln_st,
                                          outf[:], wait=[(OF, res_tok)])
                S.sy(lambda e, d=dram["out"][gt * 128:(gt + 1) * 128, :], s=outf[:]:
                     e.dma_start(out=d, in_=s),
                     wait=[st.tile_tok[gt]], inc=st.ST, incn=16)
                S.waitonly("vector", [(st.ST, st.ST.count)])
            else:
                st.tile_tok[gt] = emit_ln(nc, es, S, st, rres, ln_sq, ln_st,
                                          st.stream[:, gt * D:(gt + 1) * D],
                                          wait=[(OF, res_tok)])
    sc.close()



def kernel(**inputs):
    x = np.ascontiguousarray(np.asarray(inputs["L0_attn_query"], np.float32))
    arrays = host_prep(inputs)
    nc, need = build_program(arrays, ("l0a", "l0f", "l1a", "l1f"), nt=32,
                             stream_in="x", final_out=True)
    in_maps = []
    for b in range(NCORES):
        m = {"x": np.ascontiguousarray(x[b])}
        for nm in need:
            m[nm] = arrays[nm]
        in_maps.append(m)
    res = run_bass_kernel_spmd(nc, in_maps, core_ids=list(range(NCORES)))
    return np.stack([r["out"] for r in res.results], axis=0)


# version 03 (final)
# speedup vs baseline: 1.3164x; 1.2371x over previous
import numpy as np
from contextlib import ExitStack
import concourse.bass as bass
import concourse.mybir as mybir
from concourse.bass_utils import run_bass_kernel_spmd

F32 = mybir.dt.float32
BF16 = mybir.dt.bfloat16
I16 = mybir.dt.int16
I32 = mybir.dt.int32
AF = mybir.ActivationFunctionType
ALU = mybir.AluOpType
AX = mybir.AxisListType

D = 1024
NCORES = 8


class Sem:
    def __init__(self, h, name):
        self.h = h
        self.name = name
        self.count = 0


class Sched:
    ENGINES = ("sync", "vector", "tensor", "scalar", "gpsimd")

    def __init__(self, nc, es):
        self.nc = nc
        self.es = es
        self.q = {e: [] for e in self.ENGINES}
        self.scope = None

    def sem(self, name):
        h = self.es.enter_context(self.nc.semaphore(name))
        return Sem(h, name)

    def emit(self, eng, fn, wait=(), inc=None, incn=1):
        tok = None
        if inc is not None:
            inc.count += incn
            tok = inc.count
        self.q[eng].append((fn, tuple(wait), inc, incn, self.scope))
        return tok

    def sy(self, fn, **kw): return self.emit("sync", fn, **kw)
    def v(self, fn, **kw): return self.emit("vector", fn, **kw)
    def t(self, fn, **kw): return self.emit("tensor", fn, **kw)
    def s(self, fn, **kw): return self.emit("scalar", fn, **kw)
    def g(self, fn, **kw): return self.emit("gpsimd", fn, **kw)

    def waitonly(self, eng, wait):
        self.q[eng].append((None, tuple(wait), None, 0, self.scope))

    def replay(self, eng_name, eng):
        cur = None
        sid = None
        for fn, wait, inc, incn, scope in self.q[eng_name]:
            if scope != cur:
                if cur is not None:
                    self.nc.leave_named_scope(cur, sid, False)
                cur, sid = scope, None
                if scope is not None:
                    sid, _ = self.nc.enter_named_scope(scope, False)
            for (sem, val) in wait:
                eng.wait_ge(sem.h, val)
            if fn is not None:
                instr = fn(eng)
                if inc is not None:
                    instr.then_inc(inc.h, incn)
        if cur is not None:
            self.nc.leave_named_scope(cur, sid, False)

    def build_blocks(self, block):
        sched = self

        @block.sync
        def _(sync):
            sched.replay("sync", sync)

        @block.vector
        def _(vector):
            sched.replay("vector", vector)

        @block.tensor
        def _(tensor):
            sched.replay("tensor", tensor)

        @block.scalar
        def _(scalar):
            sched.replay("scalar", scalar)

        @block.gpsimd
        def _(gpsimd):
            sched.replay("gpsimd", gpsimd)


class Arena:

    def __init__(self, nc, es, nbytes):
        self.t_bf = _sb(nc, es, "arena", [128, nbytes // 2], BF16)
        self.t_f32 = self.t_bf[:].bitcast(F32)
        self.n = nbytes
        self.free_list = [(0, nbytes)]

    def alloc(self, nbytes, align=4):
        nbytes = -(-nbytes // align) * align
        for i, (off, sz) in enumerate(self.free_list):
            aoff = -(-off // align) * align
            if aoff + nbytes <= off + sz:
                pre = (off, aoff - off)
                post = (aoff + nbytes, off + sz - aoff - nbytes)
                repl = [p for p in (pre, post) if p[1] > 0]
                self.free_list[i:i + 1] = repl
                return aoff
        raise MemoryError(f"arena: cannot alloc {nbytes}B; free={self.free_list}")

    def free(self, off, nbytes, align=4):
        nbytes = -(-nbytes // align) * align
        self.free_list.append((off, nbytes))
        self.free_list.sort()
        merged = []
        for o, s in self.free_list:
            if merged and merged[-1][0] + merged[-1][1] == o:
                merged[-1] = (merged[-1][0], merged[-1][1] + s)
            else:
                merged.append((o, s))
        self.free_list = [tuple(m) for m in merged]

    def f32(self, np_, nfree):
        off = self.alloc(np_ * 0 + nfree * 4)
        return self.t_f32[:, off // 4: off // 4 + nfree], (off, nfree * 4)

    def bf16(self, np_, nfree):
        off = self.alloc(nfree * 2)
        return self.t_bf[:, off // 2: off // 2 + nfree], (off, nfree * 2)


class Scope:

    def __init__(self, arena):
        self.a = arena
        self.allocs = []

    def f32(self, nfree, rows=128):
        v, rec = self.a.f32(rows, nfree)
        self.allocs.append(rec)
        return v[0:rows, :]

    def bf16(self, nfree, rows=128):
        v, rec = self.a.bf16(rows, nfree)
        self.allocs.append(rec)
        return v[0:rows, :]

    def close(self):
        for off, sz in self.allocs:
            self.a.free(off, sz)
        self.allocs = []



def _blk(wT, ob=128):
    I, O = wT.shape
    return np.ascontiguousarray(wT.reshape(I, O // ob, ob).transpose(1, 0, 2))


def _colchunks(v):
    return np.ascontiguousarray(v.reshape(-1, 128).T)


def host_prep(inputs):
    f32 = np.float32
    a = {}
    for nm in ("Wq", "Wk", "Wv", "Wo"):
        a[f"l0_{nm}T"] = np.ascontiguousarray(inputs[f"L0_attn_{nm}"].T.astype(f32))
    a["l0_bq_cols"] = _colchunks(inputs["L0_attn_bq"].astype(f32))
    a["l0_bkv_row"] = np.concatenate(
        [inputs["L0_attn_bk"], inputs["L0_attn_bv"]]).astype(f32).reshape(1, 2 * D)
    a["l0_bo_row"] = inputs["L0_attn_bo"].astype(f32).reshape(1, D)
    a["l0_w1T"] = _blk(np.ascontiguousarray(inputs["L0_ffn_w1_w"].T.astype(f32)))
    a["l0_w2T"] = _blk(np.ascontiguousarray(inputs["L0_ffn_w2_w"].T.astype(f32)))
    a["l0_h1T"] = _blk(np.ascontiguousarray(inputs["L0_ffn_h1_w"].T.astype(f32)))
    a["l0_h2T"] = _blk(np.ascontiguousarray(inputs["L0_ffn_h2_w"].T.astype(f32)))
    a["l0_w3T"] = _blk(np.ascontiguousarray(inputs["L0_ffn_w3_w"].T.astype(f32)))
    for nm in ("w1_s", "w2_s", "h1_s", "h2_s"):
        a[f"l0_{nm}_cols"] = _colchunks(inputs[f"L0_ffn_{nm}"].astype(f32))
    a["l0_w3_s_cols"] = _colchunks(inputs["L0_ffn_w3_s"].astype(f32))
    a["l1_qT"] = np.ascontiguousarray(inputs["L1_attn_q_w"].T.astype(f32))
    a["l1_kT"] = np.ascontiguousarray(inputs["L1_attn_k_w"].T.astype(f32))
    a["l1_vT"] = np.ascontiguousarray(inputs["L1_attn_v_w"].T.astype(f32))
    a["l1_gT"] = np.ascontiguousarray(inputs["L1_attn_g_w"].T.astype(f32))
    aT32 = np.zeros((1024, 32), f32)
    aT32[:, 0:6] = inputs["L1_attn_a_w"].T.astype(f32)
    bT32 = np.zeros((1024, 32), f32)
    bT32[:, 0:6] = inputs["L1_attn_b_w"].T.astype(f32)
    a["l1_aT32"] = np.ascontiguousarray(aT32)
    a["l1_bT32"] = np.ascontiguousarray(bT32)
    a["l1_oT"] = np.ascontiguousarray(inputs["L1_attn_o_w"].T.astype(f32))
    a["l1_convq_cols"] = np.ascontiguousarray(
        inputs["L1_attn_conv_q_w"].astype(f32).reshape(6, 128, 4).transpose(1, 0, 2).reshape(128, 24))
    a["l1_convk_cols"] = np.ascontiguousarray(
        inputs["L1_attn_conv_k_w"].astype(f32).reshape(6, 128, 4).transpose(1, 0, 2).reshape(128, 24))
    a["l1_convv_cols"] = np.ascontiguousarray(
        inputs["L1_attn_conv_v_w"].astype(f32).reshape(12, 128, 4).transpose(1, 0, 2).reshape(128, 48))
    a["l1_Alog_row"] = inputs["L1_attn_A_log"].astype(f32).reshape(1, 6)
    a["l1_dtb_row"] = inputs["L1_attn_dt_bias"].astype(f32).reshape(1, 6)
    a["l1_nw_cols"] = _colchunks(inputs["L1_attn_norm_weight"].astype(f32))
    a["l1_wgate"] = np.ascontiguousarray(inputs["L1_ffn_w_gate"].astype(f32))
    a["l1_wup"] = np.ascontiguousarray(inputs["L1_ffn_w_up"].astype(f32))
    a["l1_wdown"] = np.ascontiguousarray(inputs["L1_ffn_w_down"].astype(f32))
    a["identity"] = np.eye(128, dtype=f32)
    a["ones_row"] = np.ones((1, 512), f32)
    a["tri_incl"] = np.ascontiguousarray(np.triu(np.ones((128, 128), f32)))
    a["mask_strict_low"] = np.ascontiguousarray(np.tril(np.ones((128, 128), f32), k=-1))
    bsel = np.zeros((16, 8 * 128), f32)
    for c in range(8):
        bsel[2 * c, c * 128:c * 128 + 64] = 1.0
        bsel[2 * c + 1, c * 128 + 64:c * 128 + 128] = 1.0
    a["bsel"] = np.ascontiguousarray(bsel)
    consts = [1e-5, 1e-6, 1.0 / D, -1.0, float(np.log(128.0 ** -0.5))]
    a["constbank"] = np.ascontiguousarray(
        np.tile(np.array(consts, f32)[None, :], (128, 1)))
    return a



class St:
    pass


def _sb(nc, es, name, shape, dt=F32):
    return es.enter_context(nc.sbuf_tensor(name, shape, dt))


ARENA_BYTES = 134 * 1024


def build_program(host_arrays, phases, nt=32, stream_in="x", stream_out=None,
                  final_out=False):
    nc = bass.Bass("TRN2", target_bir_lowering=False)
    T = nt * 128

    dram = {}
    if stream_in == "x":
        dram["x"] = nc.dram_tensor("x", [T, D], F32, kind="ExternalInput").ap()
    else:
        dram["stream_in"] = nc.dram_tensor("stream_in", [T, D], BF16,
                                           kind="ExternalInput").ap()
    if stream_out == "stream":
        dram["stream_out"] = nc.dram_tensor("stream_out", [T, D], BF16,
                                            kind="ExternalOutput").ap()
    if final_out:
        dram["out"] = nc.dram_tensor("out", [T, D], F32, kind="ExternalOutput").ap()
    import os as _os
    if _os.environ.get("L1A_DBG") or _os.environ.get("L1F_DBG"):
        dram["dbg"] = nc.dram_tensor("dbg", [128, 8192], F32, kind="ExternalOutput").ap()

    need = {"identity", "constbank"}
    if "l0a" in phases:
        need |= {"l0_WqT", "l0_WkT", "l0_WvT", "l0_WoT", "l0_bq_cols",
                 "l0_bkv_row", "l0_bo_row", "bsel"}
    if "l0f" in phases:
        need |= {"l0_w1T", "l0_w2T", "l0_h1T", "l0_h2T", "l0_w3T",
                 "l0_w1_s_cols", "l0_w2_s_cols", "l0_h1_s_cols", "l0_h2_s_cols",
                 "l0_w3_s_cols"}
    if "l1a" in phases:
        need |= {"l1_qT", "l1_kT", "l1_vT", "l1_gT", "l1_aT32", "l1_bT32", "l1_oT",
                 "l1_convq_cols", "l1_convk_cols", "l1_convv_cols",
                 "l1_Alog_row", "l1_dtb_row", "l1_nw_cols", "tri_incl",
                 "mask_strict_low"}
    if "l1f" in phases:
        need |= {"l1_wgate", "l1_wup", "l1_wdown"}
    for nm in sorted(need):
        arr = host_arrays[nm]
        dram[nm] = nc.dram_tensor(nm, list(arr.shape), F32, kind="ExternalInput").ap()

    with ExitStack() as es:
        S = Sched(nc, es)
        st = St()
        st.nt = nt
        st.dram = dram
        st.need = need

        st.stream = _sb(nc, es, "stream", [128, nt * D], BF16)
        st.ldbuf = _sb(nc, es, "ldbuf", [128, 2 * D], F32)
        st.ident = _sb(nc, es, "sb_ident", [128, 128], BF16)
        st.ones_row = _sb(nc, es, "sb_ones", [1, 512], BF16)
        st.cb = _sb(nc, es, "sb_cb", [128, 5], F32)
        ident_f32 = _sb(nc, es, "ident_f32", [128, 128], F32)
        st.ident_f32 = ident_f32
        st.c_eps_ln = st.cb[:, 0:1]
        st.c_eps6 = st.cb[:, 1:2]
        st.c_invD = st.cb[:, 2:3]
        st.c_neg1 = st.cb[:, 3:4]
        st.c_qln = st.cb[:, 4:5]
        st.arena = Arena(nc, es, ARENA_BYTES)

        st.banks = [es.enter_context(nc.psum_tensor(f"bank{i}", [128, 512], F32))
                    for i in range(7)]
        st.tp_bank = es.enter_context(nc.psum_tensor("tp_bank", [128, 256], BF16))

        st.LD = S.sem("LD")
        st.SV = S.sem("SV")
        st.SS = S.sem("SS")
        st.TN = S.sem("TN")
        st.GP = S.sem("GP")
        st.ST = S.sem("ST")
        st.tile_tok = {}

        S.sy(lambda e: e.dma_start(out=ident_f32[:], in_=dram["identity"]),
             inc=st.LD, incn=16)
        S.sy(lambda e: e.dma_start(out=st.cb[:], in_=dram["constbank"]),
             inc=st.LD, incn=16)
        consts_tok = st.LD.count

        if stream_in == "x":
            LX = [S.sem("LX0"), S.sem("LX1")]
            xf32 = st.ldbuf
            for t in range(nt):
                sl = xf32[:, (t % 2) * D:(t % 2) * D + D]
                if t >= 2:
                    S.waitonly("sync", [(st.SV, t - 1)])
                S.sy(lambda e, d=sl, s=dram["x"][t * 128:(t + 1) * 128, :]:
                     e.dma_start(out=d, in_=s), inc=LX[t % 2], incn=16)
                ctok = S.v(lambda e, d=st.stream[:, t * D:(t + 1) * D], s=sl:
                           e.tensor_copy(d, s),
                           wait=[(LX[t % 2], LX[t % 2].count)], inc=st.SV)
                st.tile_tok[t] = (st.SV, ctok)
        else:
            LS = S.sem("LS")
            for t in range(nt):
                S.sy(lambda e, d=st.stream[:, t * D:(t + 1) * D],
                     s=dram["stream_in"][t * 128:(t + 1) * 128, :]:
                     e.dma_start(out=d, in_=s), inc=LS, incn=16)
            S.waitonly("vector", [(LS, LS.count)])
            for t in range(nt):
                st.tile_tok[t] = (LS, LS.count)

        S.v(lambda e: e.tensor_copy(st.ident[:], ident_f32[:]),
            wait=[(st.LD, consts_tok)], inc=st.SV)
        S.v(lambda e: e.memset(st.ones_row[:], 1.0), inc=st.SV)
        st.base_ready = (st.SV, st.SV.count)

        if "l0a" in phases:
            S.scope = "l0a"
            emit_l0_attn(nc, es, S, st)
        if "l0f" in phases:
            S.scope = "l0f"
            emit_l0_ffn(nc, es, S, st)
        if "l1a" in phases:
            S.scope = "l1a"
            emit_l1_attn(nc, es, S, st)
        if "l1f" in phases:
            S.scope = "l1f"
            emit_l1_ffn(nc, es, S, st, final_out=final_out)

        if stream_out == "stream":
            for t in range(nt):
                w = [st.tile_tok[t]]
                S.sy(lambda e, d=dram["stream_out"][t * 128:(t + 1) * 128, :],
                     s=st.stream[:, t * D:(t + 1) * D]: e.dma_start(out=d, in_=s),
                     wait=w, inc=st.ST, incn=16)
        if st.ST.count:
            S.waitonly("sync", [(st.ST, st.ST.count)])

        block = es.enter_context(nc.Block())
        S.build_blocks(block)
    return nc, sorted(need)


def phase_barrier(S, st):
    toks = [(st.TN, st.TN.count), (st.SV, st.SV.count), (st.SS, st.SS.count)]
    for eng in ("tensor", "vector", "scalar", "sync", "gpsimd"):
        S.waitonly(eng, toks)


def emit_ln(nc, es, S, st, r, ln_sq, ln_st, out_bf, wait):
    SV, SS = st.SV, st.SS
    w0 = list(wait) + [(SS, SS.count)]
    v1 = S.v(lambda e: e.reduce_sum(ln_st[:, 0:1], r[:], axis=AX.X), wait=w0, inc=SV)
    s1 = S.s(lambda e: e.activation(ln_sq[:], r[:], AF.Square, accum_out=ln_st[:, 1:2]),
             wait=w0, inc=SS)
    v2 = S.v(lambda e: e.tensor_scalar_mul(ln_st[:, 2:3], ln_st[:, 0:1], 1.0 / D),
             wait=[(SV, v1)], inc=SV)
    v3 = S.v(lambda e: e.tensor_scalar_mul(ln_st[:, 3:4], ln_st[:, 2:3], ln_st[:, 2:3]),
             wait=[(SV, v2)], inc=SV)
    v4 = S.v(lambda e: e.tensor_scalar(ln_st[:, 4:5], ln_st[:, 1:2], 1.0 / D,
                                       ln_st[:, 3:4], ALU.mult, ALU.subtract),
             wait=[(SS, s1), (SV, v3)], inc=SV)
    s2 = S.s(lambda e: e.activation(ln_st[:, 5:6], ln_st[:, 4:5], AF.Sqrt,
                                    bias=st.c_eps_ln), wait=[(SV, v4)], inc=SS)
    v5 = S.v(lambda e: e.reciprocal(ln_st[:, 5:6], ln_st[:, 5:6]),
             wait=[(SS, s2)], inc=SV)
    v6 = S.v(lambda e: e.tensor_scalar(ln_st[:, 6:7], ln_st[:, 2:3], ln_st[:, 5:6],
                                       -1.0, ALU.mult, ALU.mult), wait=[(SV, v5)], inc=SV)
    tok = S.v(lambda e: e.tensor_scalar(out_bf, r[:], ln_st[:, 5:6],
                                        ln_st[:, 6:7], ALU.mult, ALU.add),
              wait=[(SV, v6)], inc=SV)
    return (SV, tok)



def emit_l0_attn(nc, es, S, st):
    import os
    dbg = os.environ.get("L0A_STOP", "")
    nt = st.nt
    dram = st.dram
    LD, SV, SS, TN = st.LD, st.SV, st.SS, st.TN
    B = st.banks
    sc = Scope(st.arena)

    WL = S.sem("l0a_wl")
    wcast = S.sem("l0a_wcast")
    wA = sc.bf16(8 * D)
    wB = sc.bf16(8 * D)
    wtmp = sc.f32(D)

    def load_wt(nm, dst, extra_wait=()):
        for j in range(8):
            src = dram[nm][128 * j:128 * (j + 1), :]
            S.sy(lambda e, d=wtmp, s=src: e.dma_start(out=d, in_=s),
                 wait=[(wcast, wcast.count)] + list(extra_wait), inc=WL, incn=16)
            S.v(lambda e, d=dst[:, j * D:(j + 1) * D], s=wtmp:
                e.tensor_copy(d, s), wait=[(WL, WL.count)], inc=wcast)

    load_wt("l0_WkT", wA)
    load_wt("l0_WvT", wB)
    wk, wv = wA, wB
    if dbg == "loadw":
        return

    WS = S.sem("l0a_ws")
    bkv = sc.bf16(2 * D, rows=1)
    bo_row = sc.bf16(D, rows=1)
    bq = sc.f32(8)
    nbq = sc.f32(8)
    bsel = sc.bf16(1024, rows=16)
    stage = st.ldbuf[0:16, 0:2048]
    S.sy(lambda e: e.dma_start(out=stage[0:1, 0:2 * D], in_=dram["l0_bkv_row"]),
         wait=[st.base_ready], inc=WS, incn=16)
    c1 = S.v(lambda e: e.tensor_copy(bkv, stage[0:1, 0:2 * D]),
             wait=[(WS, WS.count)], inc=wcast)
    S.sy(lambda e: e.dma_start(out=stage[0:1, 0:D], in_=dram["l0_bo_row"]),
         wait=[(wcast, c1)], inc=WS, incn=16)
    c2 = S.v(lambda e: e.tensor_copy(bo_row, stage[0:1, 0:D]),
             wait=[(WS, WS.count)], inc=wcast)
    S.sy(lambda e: e.dma_start(out=stage[0:16, 0:1024], in_=dram["bsel"]),
         wait=[(wcast, c2)], inc=WS, incn=16)
    S.v(lambda e: e.tensor_copy(bsel, stage[0:16, 0:1024]),
        wait=[(WS, WS.count)], inc=wcast)
    WS2 = S.sem("l0a_ws2")
    S.sy(lambda e: e.dma_start(out=bq, in_=dram["l0_bq_cols"]), inc=WS2, incn=16)
    S.v(lambda e: e.tensor_scalar_mul(nbq, bq, -1.0),
        wait=[(WS2, WS2.count)], inc=wcast)

    scr64 = sc.bf16(nt * D)
    st.l0_scr = scr64
    kv_tm = sc.bf16(D)
    vx_tm = sc.bf16(16 * 65)
    S_sh = sc.f32(8 * 130)
    S_acc = S_sh
    rres = S_sh[:, 0:D]
    S_b = sc.bf16(8 * 130)
    phi_a = st.ldbuf[:, 0:512]
    phi_b = st.ldbuf[:, 512:1024]
    ln_sq = st.ldbuf[:, 1024:2048]

    ps_tp = st.tp_bank[:, 0:128]
    ps_p = [B[1], B[2]]
    ps_s = [B[3][:, 0:130], B[4][:, 0:130], B[5][:, 0:130], B[6][:, 0:130]]

    S.v(lambda e: e.memset(vx_tm, 1.0), wait=[st.base_ready], inc=SV)
    S.v(lambda e: e.memset(S_acc, 0.0), inc=SV)
    prep_tok = SV.count
    wready_tok = wcast.count

    tfree = S.sem("l0a_tfree")
    pfree = S.sem("l0a_pfree")
    sfree = S.sem("l0a_sfree")

    for t in range(nt):
        for c in range(8):
            w = []
            if t == 0 and c == 0:
                w = [(SV, prep_tok)]
            if c == 0:
                w = w + [st.tile_tok[t]]
            if t > 0 or c > 0:
                w = w + [(tfree, tfree.count)]
            tk = S.t(lambda e, sl=st.stream[:, t * D + c * 128: t * D + (c + 1) * 128]:
                     e.transpose(ps_tp, sl, st.ident[:]), wait=w, inc=TN)
            S.v(lambda e, d=scr64[:, t * D + c * 128: t * D + (c + 1) * 128]:
                e.tensor_copy(d, ps_tp), wait=[(TN, tk)], inc=tfree)
        xfm = scr64[:, t * D:(t + 1) * D]
        tfree_tok = tfree.count
        if dbg == "tp":
            continue

        for which, (wmat, brow_off) in enumerate([(wk, 0), (wv, D)]):
            for h in range(2):
                psb = ps_p[h]
                for c in range(8):
                    w = []
                    if c == 0:
                        if which == 0 and h == 0:
                            w = [(tfree, tfree_tok), (wcast, wready_tok)]
                        w = w + [(pfree, pfree.count)]
                    S.t(lambda e, wsl=wmat[:, c * D + h * 512: c * D + h * 512 + 512],
                        lhs=xfm[:, c * 128:(c + 1) * 128], p=psb, s0=(c == 0):
                        e.matmul(p[:], lhs, wsl, start=s0, stop=False),
                        wait=w)
                import os as _os
                if _os.environ.get("NOBIAS", ""):
                    tk = S.t(lambda e, wsl=wmat[:, 7 * D + h * 512: 7 * D + h * 512 + 512],
                             lhs=xfm[:, 7 * 128:8 * 128], p=psb:
                             e.matmul(p[:], lhs, wsl, start=False, stop=True), inc=TN)
                else:
                    tk = S.t(lambda e, b=bkv[0:1, brow_off + h * 512: brow_off + h * 512 + 512],
                             p=psb: e.matmul(p[:], st.ones_row[:, 0:128], b,
                                             start=False, stop=True), inc=TN)
                import os as _os2
                if _os2.environ.get("PLAINPHI", "") and which == 0:
                    S.v(lambda e, p=psb, d=kv_tm[:, h * 512:h * 512 + 512]:
                        e.tensor_copy(d, p[:]),
                        wait=[(TN, tk), (TN, TN.count)], inc=pfree)
                elif _os2.environ.get("PLAINPHI", ""):
                    S.v(lambda e, p=psb,
                        d=vx_tm[:, h * 8 * 65:(h + 1) * 8 * 65].rearrange(
                            "p (a b) -> p a b", b=65)[:, :, 0:64]:
                        e.tensor_copy(d, p[:].rearrange("p (a b) -> p a b", b=64)),
                        wait=[(TN, tk), (TN, TN.count)], inc=pfree)
                elif which == 0:
                    va = S.v(lambda e, p=psb: e.tensor_relu(phi_a, p[:]),
                             wait=[(TN, tk), (pfree, pfree.count)], inc=SV)
                    vb = S.v(lambda e, p=psb: e.tensor_scalar_min(phi_b, p[:], 0.0),
                             wait=[(SV, va)], inc=SV)
                    s2 = S.s(lambda e: e.activation(phi_b, phi_b, AF.Exp),
                             wait=[(SV, vb), (SS, SS.count)], inc=SS)
                    S.v(lambda e, d=kv_tm[:, h * 512:h * 512 + 512]:
                        e.tensor_add(d, phi_a, phi_b),
                        wait=[(SS, s2), (SV, va), (TN, TN.count)], inc=pfree)
                else:
                    S.v(lambda e, p=psb,
                        d=vx_tm[:, h * 8 * 65:(h + 1) * 8 * 65].rearrange(
                            "p (a b) -> p a b", b=65)[:, :, 0:64]:
                        e.tensor_copy(d, p[:].rearrange("p (a b) -> p a b", b=64)),
                        wait=[(TN, tk), (TN, TN.count)], inc=pfree)
        vready = pfree.count
        if dbg == "proj":
            continue

        for pr in range(8):
            bank = ps_s[pr % 4]
            w = [(pfree, vready)]
            if t > 0 or pr >= 4:
                w.append((sfree, sfree.count))
            tk = S.t(lambda e, b=bank, lhs=kv_tm[:, pr * 128:(pr + 1) * 128],
                     r=vx_tm[:, pr * 130:(pr + 1) * 130]:
                     e.matmul(b, lhs, r, start=True, stop=True), wait=w, inc=TN)
            S.v(lambda e, b=bank, d=S_acc[:, pr * 130:(pr + 1) * 130]:
                e.tensor_add(d, d, b), wait=[(TN, tk)], inc=sfree)

    if dbg in ("tp", "proj"):
        return
    cast_tok = S.v(lambda e: e.tensor_copy(S_b, S_acc),
                   wait=[(sfree, sfree.count)], inc=SV)
    if dbg == "sweep1":
        return

    sweep1_done = (TN, TN.count)
    load_wt("l0_WqT", wA, extra_wait=[sweep1_done])
    load_wt("l0_WoT", wB, extra_wait=[sweep1_done])
    wq, wo = wA, wB
    wready2 = (wcast, wcast.count)

    zbd = sc.bf16(128)
    zm = S.v(lambda e: e.memset(zbd, 0.0), inc=SV)
    prev = zm
    for h in range(16):
        c = h // 2
        zcol = (h // 2) * 130 + 65 * (h % 2) + 64
        prev = S.v(lambda e, d=zbd[64 * (h % 2):64 * (h % 2) + 64, c * 16 + h:c * 16 + h + 1],
                   s=S_b[64 * (h % 2):64 * (h % 2) + 64, zcol:zcol + 1]:
                   e.tensor_copy(d, s), wait=[(SV, prev), (SV, cast_tok)], inc=SV)
    zbd_tok = SV.count

    phiQ = sc.bf16(8 * 512)
    ctx = phiQ
    rbuf = sc.bf16(512)
    dn = sc.bf16(512, rows=16)
    dn_f = sc.f32(512, rows=16)
    ln_st = sc.f32(8)

    ps_q = [B[1], B[2]]
    ps_dn = B[0][0:16, :]
    ps_bc = [B[3], B[4]]
    ps_n2 = [B[5], B[6]]

    qfree = S.sem("l0a_qfree")
    dnfree = S.sem("l0a_dnfree")
    bcfree = S.sem("l0a_bcfree")
    nmfree = S.sem("l0a_nmfree")
    wofree = S.sem("l0a_wofree")

    for g in range(nt // 4):
        for c in range(8):
            psb = ps_q[c % 2]
            for i in range(8):
                w = []
                if g == 0 and c == 0 and i == 0:
                    w = [(SV, cast_tok), wready2]
                if i == 0 and (g > 0 or c >= 2):
                    w = w + [(qfree, qfree.count)]
                S.t(lambda e, p=psb, wsl=wq[:, i * D + c * 128: i * D + (c + 1) * 128],
                    r=scr64[:].rearrange("p (a b) -> p a b", b=D)
                      [:, 4 * g:4 * g + 4, i * 128:(i + 1) * 128],
                    s0=(i == 0), s1=(i == 7):
                    e.matmul(p[:], wsl, r, start=s0, stop=s1),
                    wait=w, inc=(TN if i == 7 else None))
            tk = TN.count
            s1 = S.s(lambda e, p=psb, b=bq[:, c:c + 1]:
                     e.activation(phi_a, p[:], AF.Relu, bias=b),
                     wait=[(TN, tk), (qfree, qfree.count)], inc=SS)
            sr = S.s(lambda e, p=psb, b=nbq[:, c:c + 1]:
                     e.activation(phi_b, p[:], AF.Relu, bias=b, scale=-1.0),
                     wait=[(SS, s1)], inc=SS)
            s3 = S.s(lambda e: e.activation(phi_b, phi_b, AF.Exp, scale=-1.0),
                     wait=[(SS, sr)], inc=SS)
            S.v(lambda e, d=phiQ[:, c * 512:(c + 1) * 512]:
                e.tensor_add(d, phi_a, phi_b),
                wait=[(SS, s3), (TN, TN.count)], inc=qfree)
        q_tok = qfree.count

        for c in range(8):
            w = []
            if c == 0:
                w = [(SV, zbd_tok), (qfree, q_tok), (dnfree, dnfree.count)]
            S.t(lambda e, lhs=zbd[:, c * 16:(c + 1) * 16],
                r=phiQ[:, c * 512:(c + 1) * 512], s0=(c == 0), s1=(c == 7):
                e.matmul(ps_dn, lhs, r, start=s0, stop=s1),
                wait=w, inc=(TN if c == 7 else None))
        dn_tok = TN.count
        da = S.v(lambda e: e.tensor_scalar_add(dn_f, ps_dn, 1e-6),
                 wait=[(TN, dn_tok), (SV, SV.count)], inc=SV)
        db = S.v(lambda e: e.reciprocal(dn_f, dn_f), wait=[(SV, da)], inc=dnfree)
        S.v(lambda e: e.tensor_copy(dn, dn_f), wait=[(SV, da), (dnfree, db)], inc=SV)
        dn_ready = SV.count

        for pr in range(8):
            bcb = ps_bc[pr % 2]
            nmb = ps_n2[pr % 2]
            w = [(SV, dn_ready)]
            if g > 0 or pr >= 2:
                w.append((bcfree, bcfree.count))
            tb = S.t(lambda e, lhs=bsel[:, pr * 128:(pr + 1) * 128], b=bcb:
                     e.matmul(b[:], lhs, dn, start=True, stop=True),
                     wait=w, inc=TN)
            S.v(lambda e, b=bcb: e.tensor_copy(rbuf, b[:]),
                wait=[(TN, tb), (SV, SV.count), (nmfree, nmfree.count)], inc=bcfree)
            rb_tok = bcfree.count
            for hh in range(2):
                w = []
                if hh == 0 and (g > 0 or pr >= 2):
                    w = [(nmfree, nmfree.count)]
                scol = pr * 130 + 65 * hh
                S.t(lambda e, bsl2=nmb[64 * hh:64 * hh + 64, :],
                    lhs=S_b[64 * hh:64 * hh + 64, scol:scol + 64],
                    r=phiQ[64 * hh:64 * hh + 64, pr * 512:(pr + 1) * 512]:
                    e.matmul(bsl2, lhs, r, start=True, stop=True),
                    wait=w, inc=(TN if hh == 1 else None))
            tn = TN.count
            S.v(lambda e, b=nmb, d=ctx[:, pr * 512:(pr + 1) * 512]:
                e.tensor_mul(d, b[:], rbuf),
                wait=[(TN, tn), (bcfree, rb_tok)], inc=nmfree)
        ctx_tok = nmfree.count

        for stile in range(4):
            t = 4 * g + stile
            for h in range(2):
                psb = ps_bc[h]
                for i in range(8):
                    w = []
                    if i == 0:
                        w = [(nmfree, ctx_tok)]
                        if g > 0 or stile > 0:
                            w.append((wofree, wofree.count))
                    S.t(lambda e, p=psb,
                        lhs=ctx[:, i * 512 + stile * 128: i * 512 + stile * 128 + 128],
                        wsl=wo[:, i * D + h * 512: i * D + h * 512 + 512], s0=(i == 0):
                        e.matmul(p[:], lhs, wsl, start=s0, stop=False),
                        wait=w)
                S.t(lambda e, p=psb, b=bo_row[0:1, h * 512:h * 512 + 512]:
                    e.matmul(p[:], st.ones_row[:, 0:128], b, start=False, stop=True),
                    inc=TN)
            tk = TN.count
            wprev = [st.tile_tok[t - 1]] if (t - 1) in st.tile_tok else []
            S.v(lambda e, p=ps_bc[0], d=rres[:, 0:512],
                s=st.stream[:, t * D: t * D + 512]:
                e.tensor_add(d, p[:], s),
                wait=[(TN, tk), (SS, SS.count), (SV, cast_tok)] + wprev, inc=SV)
            S.v(lambda e, p=ps_bc[1], d=rres[:, 512:1024],
                s=st.stream[:, t * D + 512: t * D + 1024]:
                e.tensor_add(d, p[:], s), inc=wofree)
            res_tok = wofree.count
            st.tile_tok[t] = emit_ln(nc, es, S, st, rres, ln_sq, ln_st,
                                     st.stream[:, t * D:(t + 1) * D],
                                     wait=[(wofree, res_tok)])

    st.l0a_end = {"TN": TN.count, "SV": SV.count, "SS": SS.count}
    sc.close()


def emit_l0_ffn(nc, es, S, st):
    nt = st.nt
    dram = st.dram
    SV, SS, TN = st.SV, st.SS, st.TN
    B = st.banks
    phase_barrier(S, st)
    sc = Scope(st.arena)
    TT = 4
    NP = nt // TT
    TTK = TT * 128

    c_w1 = nc.dram_tensor("l0f_cw1", [32, 128, 8 * 128], BF16, kind="Internal").ap()
    c_w2 = nc.dram_tensor("l0f_cw2", [32, 128, 8 * 128], BF16, kind="Internal").ap()
    c_h1 = nc.dram_tensor("l0f_ch1", [32, 128, 32 * 128], BF16, kind="Internal").ap()
    c_h2 = nc.dram_tensor("l0f_ch2", [32, 128, 32 * 128], BF16, kind="Internal").ap()
    c_w3 = nc.dram_tensor("l0f_cw3", [8, 128, 32 * 128], BF16, kind="Internal").ap()

    x1fm = sc.bf16(TTK * 8)
    yfm = x1fm
    hbufA = sc.bf16(TTK * 32)
    hbufB = sc.bf16(TTK * 32)
    wld = sc.f32(1024)
    w12 = [sc.bf16(1024), sc.bf16(1024)]
    w12b = [sc.bf16(1024), sc.bf16(1024)]
    hld = sc.f32(4096)
    hbf = [sc.bf16(4096), sc.bf16(4096)]
    s1c = sc.f32(32)
    s2c = sc.f32(32)
    h1c = sc.f32(32)
    h2c = sc.f32(32)
    s3c = sc.f32(8)
    rres = sc.f32(1024)
    ln_st = sc.f32(8)
    t1sb2 = [st.ldbuf[:, 0:512], sc.f32(512)[:]]
    t2sb2 = [st.ldbuf[:, 512:1024], sc.f32(512)[:]]
    ln_sq = st.ldbuf[:, 1024:2048]

    SL = S.sem("l0f_sl")
    S.sy(lambda e: e.dma_start(out=s1c, in_=dram["l0_w1_s_cols"]), inc=SL, incn=16)
    S.sy(lambda e: e.dma_start(out=s2c, in_=dram["l0_w2_s_cols"]), inc=SL, incn=16)
    S.sy(lambda e: e.dma_start(out=h1c, in_=dram["l0_h1_s_cols"]), inc=SL, incn=16)
    S.sy(lambda e: e.dma_start(out=h2c, in_=dram["l0_h2_s_cols"]), inc=SL, incn=16)
    S.sy(lambda e: e.dma_start(out=s3c, in_=dram["l0_w3_s_cols"]), inc=SL, incn=16)
    sl_tok = SL.count

    WL = S.sem("l0f_wl")
    HB = [S.sem("l0f_hb0"), S.sem("l0f_hb1")]
    WB = [S.sem("l0f_wb0"), S.sem("l0f_wb1")]
    CS = S.sem("l0f_cs")
    HS = S.sem("l0f_hs")
    tpf = S.sem("l0f_tpf")
    pf = S.sem("l0f_pf")
    hf = S.sem("l0f_hf")
    yf = S.sem("l0f_yf")
    lnf = S.sem("l0f_lnf")

    ps_tp = st.tp_bank[:, 0:128]
    ps_w4 = [[B[1], B[2]], [B[5], B[6]]]
    ps_h = [B[3], B[4]]
    ps_y = [B[5], B[6]]

    def ldcast_pass0(cache, src_blk, dst_bf, wait_w, cast_cols):
        S.sy(lambda e, d=hld[:, 0:cast_cols] if cast_cols > 1024 else wld,
             s=src_blk: e.dma_start(out=d, in_=s),
             wait=wait_w, inc=WL, incn=16)
        ct = S.v(lambda e, d=dst_bf, s=hld[:, 0:cast_cols] if cast_cols > 1024 else wld:
                 e.tensor_copy(d, s), wait=[(WL, WL.count)], inc=CS)
        S.sy(lambda e, d=cache, s=dst_bf: e.dma_start(out=d, in_=s),
             wait=[(CS, ct)], inc=HS, incn=16)
        return ct

    slot_last = {}
    bank_free = {}
    def bw(b):
        e = bank_free.get(id(b))
        return [e] if e else []
    prev_pass_tok = None
    for p in range(NP):
        for t in range(TT):
            gt = p * TT + t
            for c in range(8):
                w = [(tpf, tpf.count)] if (p > 0 or t > 0 or c > 0) else []
                if c == 0:
                    w = w + [st.tile_tok[gt]]
                tk = S.t(lambda e, sl=st.stream[:, gt * D + c * 128: gt * D + (c + 1) * 128]:
                         e.transpose(ps_tp, sl, st.ident[:]), wait=w, inc=TN)
                S.v(lambda e, d=x1fm[:].rearrange("p (a b) -> p a b", b=TTK)
                    [:, c, t * 128:(t + 1) * 128]:
                    e.tensor_copy(d, ps_tp), wait=[(TN, tk)], inc=tpf)
        x1_tok = tpf.count

        mult_hist = []
        for ob in range(32):
            sel = ob % 2
            ps_w = ps_w4[ob % 2]
            wcur = w12 if sel == 0 else w12b
            wwait = []
            if p == 0:
                for which, (cac, dnm) in enumerate([(c_w1, "l0_w1T"), (c_w2, "l0_w2T")]):
                    src = dram[dnm][ob].rearrange("(c p) o -> p c o", p=128)
                    wwait2 = [(CS, CS.count)]
                    S.sy(lambda e, d=wld.rearrange("p (c o) -> p c o", c=8), s=src:
                         e.dma_start(out=d, in_=s),
                         wait=wwait2 + ([(TN, TN.count)] if ob >= 2 else []),
                         inc=WL, incn=16)
                    ct = S.v(lambda e, d=wcur[which], s=wld: e.tensor_copy(d, s),
                             wait=[(WL, WL.count), (HS, HS.count)], inc=CS)
                    S.sy(lambda e, d=cac[ob], s=wcur[which]: e.dma_start(out=d, in_=s),
                         wait=[(CS, ct)], inc=HS, incn=16)
                wrdy = (CS, CS.count)
            else:
                for which, cac in enumerate([c_w1, c_w2]):
                    lw = ([(HS, HS.count)] if ob < 2 and p == 1 else [])
                    e_ = slot_last.get(id(wcur))
                    if e_:
                        lw = lw + [e_]
                    S.sy(lambda e, d=wcur[which], s=cac[ob]: e.dma_start(out=d, in_=s),
                         wait=lw, inc=WB[sel], incn=16)
                wrdy = (WB[sel], WB[sel].count)
            for which in range(2):
                psb = ps_w[which]
                for ic in range(8):
                    w = [wrdy] if ic == 0 else []
                    if ic == 0 and which == 0:
                        w = w + [(tpf, x1_tok), (SL, sl_tok)] + bw(ps_w[0]) + bw(ps_w[1])
                    S.t(lambda e, p2=psb, lhs=x1fm[:].rearrange("p (a b) -> p a b", b=TTK)[:, ic, :],
                        wsl=wcur[which][:, ic * 128:(ic + 1) * 128], s0=(ic == 0), s1=(ic == 7):
                        e.matmul(p2[:], wsl, lhs, start=s0, stop=s1),
                        wait=w, inc=(TN if ic == 7 else None))
            tk = TN.count
            slot_last[id(wcur)] = (TN, tk)
            t1sb = t1sb2[ob % 2]
            t2sb = t2sb2[ob % 2]
            mwar = [mult_hist[ob - 2]] if ob >= 2 else []
            sg = S.s(lambda e, sl=s1c[:, ob:ob + 1], d=t1sb, p2=ps_w[0]:
                     e.activation(d, p2[:], AF.Silu, scale=sl),
                     wait=[(TN, tk), (SS, SS.count)] + mwar, inc=SS)
            vg = S.v(lambda e, sl=s2c[:, ob:ob + 1], d=t2sb, p2=ps_w[1]:
                     e.tensor_scalar_mul(d, p2[:], sl),
                     wait=[(TN, tk), (SV, SV.count)] + mwar, inc=SV)
            bank_free[id(ps_w[0])] = (SS, sg)
            mt = S.v(lambda e, d=hbufA[:].rearrange("p (a b) -> p a b", b=TTK)[:, ob, :],
                     a=t1sb, b2=t2sb:
                     e.tensor_mul(d, a, b2), wait=[(SS, sg), (SV, vg)], inc=pf)
            mult_hist.append((pf, mt))
            bank_free[id(ps_w[1])] = (SV, vg)
        h_tok = pf.count

        for stage, (cac, dnm, hin, hout, scol) in enumerate(
                [(c_h1, "l0_h1T", hbufA, hbufB, h1c), (c_h2, "l0_h2T", hbufB, hbufA, h2c)]):
            for ob in range(32):
                sel = ob % 2
                if p == 0:
                    src = dram[dnm][ob].rearrange("(c p) o -> p c o", p=128)
                    S.sy(lambda e, d=hld.rearrange("p (c o) -> p c o", c=32), s=src:
                         e.dma_start(out=d, in_=s),
                         wait=[(CS, CS.count)], inc=WL, incn=16)
                    ct = S.v(lambda e, d=hbf[sel], s=hld: e.tensor_copy(d, s),
                             wait=[(WL, WL.count), (HS, HS.count)] +
                                  [slot_last[id(hbf[sel])]] if id(hbf[sel]) in slot_last
                                  else [(WL, WL.count), (HS, HS.count)], inc=CS)
                    S.sy(lambda e, d=cac[ob], s=hbf[sel]: e.dma_start(out=d, in_=s),
                         wait=[(CS, ct)], inc=HS, incn=16)
                    hrdy = (CS, ct)
                else:
                    lw = [slot_last[id(hbf[sel])]] if id(hbf[sel]) in slot_last else []
                    if p == 1 and stage == 0 and ob < 2:
                        lw = lw + [(HS, HS.count)]
                    S.sy(lambda e, d=hbf[sel], s=cac[ob]: e.dma_start(out=d, in_=s),
                         wait=lw, inc=HB[sel], incn=16)
                    hrdy = (HB[sel], HB[sel].count)
                psb = ps_h[sel]
                for ic in range(32):
                    w = []
                    if ic == 0:
                        w = [hrdy] + bw(psb)
                        if ob == 0:
                            w.append((pf if stage == 0 else hf, h_tok if stage == 0 else h2_tok))
                    S.t(lambda e, p2=psb,
                        lhs=hbf[sel][:, ic * 128:(ic + 1) * 128],
                        r=hin[:].rearrange("p (a b) -> p a b", b=TTK)[:, ic, :],
                        s0=(ic == 0), s1=(ic == 31):
                        e.matmul(p2[:], lhs, r, start=s0, stop=s1),
                        wait=w, inc=(TN if ic == 31 else None))
                tk = TN.count
                slot_last[id(hbf[sel])] = (TN, tk)
                gt_ = S.s(lambda e, p2=psb, sl=scol[:, ob:ob + 1],
                          d=hout[:].rearrange("p (a b) -> p a b", b=TTK)[:, ob, :]:
                          e.activation(d, p2[:], AF.Gelu, scale=sl),
                          wait=[(TN, tk), (SS, SS.count)], inc=hf)
                bank_free[id(psb)] = (hf, gt_)
            h2_tok = hf.count

        for ob in range(8):
            sel = ob % 2
            if p == 0:
                src = dram["l0_w3T"][ob].rearrange("(c p) o -> p c o", p=128)
                S.sy(lambda e, d=hld.rearrange("p (c o) -> p c o", c=32), s=src:
                     e.dma_start(out=d, in_=s),
                     wait=[(CS, CS.count)], inc=WL, incn=16)
                ct = S.v(lambda e, d=hbf[sel], s=hld: e.tensor_copy(d, s),
                         wait=[(WL, WL.count), (HS, HS.count)] +
                              ([slot_last[id(hbf[sel])]] if id(hbf[sel]) in slot_last else []),
                         inc=CS)
                S.sy(lambda e, d=c_w3[ob], s=hbf[sel]: e.dma_start(out=d, in_=s),
                     wait=[(CS, ct)], inc=HS, incn=16)
                w3rdy = (CS, ct)
            else:
                lw = [slot_last[id(hbf[sel])]] if id(hbf[sel]) in slot_last else []
                S.sy(lambda e, d=hbf[sel], s=c_w3[ob]: e.dma_start(out=d, in_=s),
                     wait=lw, inc=HB[sel], incn=16)
                w3rdy = (HB[sel], HB[sel].count)
            psb = ps_y[sel]
            for ic in range(32):
                w = []
                if ic == 0:
                    w = [w3rdy] + bw(psb)
                    if ob == 0:
                        w.append((hf, h2_tok))
                S.t(lambda e, p2=psb,
                    lhs=hbf[sel][:, ic * 128:(ic + 1) * 128],
                    r=hbufA[:].rearrange("p (a b) -> p a b", b=TTK)[:, ic, :],
                    s0=(ic == 0), s1=(ic == 31):
                    e.matmul(p2[:], lhs, r, start=s0, stop=s1),
                    wait=w, inc=(TN if ic == 31 else None))
            tk = TN.count
            slot_last[id(hbf[sel])] = (TN, tk)
            yv = S.v(lambda e, p2=psb, sl=s3c[:, ob:ob + 1],
                     d=yfm[:].rearrange("p (a b) -> p a b", b=TTK)[:, ob, :]:
                     e.tensor_scalar_mul(d, p2[:], sl),
                     wait=[(TN, tk), (SV, SV.count)], inc=yf)
            bank_free[id(psb)] = (yf, yv)
        y_tok = yf.count

        for t in range(TT):
            gt = p * TT + t
            for c in range(8):
                w = [(tpf, tpf.count)]
                if t == 0 and c == 0:
                    w.append((yf, y_tok))
                tk = S.t(lambda e, sl=yfm[:].rearrange("p (a b) -> p a b", b=TTK)
                         [:, c, t * 128:(t + 1) * 128]:
                         e.transpose(ps_tp, sl, st.ident[:]), wait=w, inc=TN)
                S.v(lambda e, d=rres[:, c * 128:(c + 1) * 128],
                    s=st.stream[:, gt * D + c * 128: gt * D + (c + 1) * 128]:
                    e.tensor_add(d, ps_tp, s),
                    wait=[(TN, tk), (lnf, lnf.count), (SS, SS.count)], inc=tpf)
            res_tok = tpf.count
            st.tile_tok[gt] = emit_ln(nc, es, S, st, rres, ln_sq, ln_st,
                                      st.stream[:, gt * D:(gt + 1) * D],
                                      wait=[(tpf, res_tok)])
            S.v(lambda e: e.engine_nop(), wait=[st.tile_tok[gt]], inc=lnf)

    st.l0f_end = {"TN": TN.count}
    sc.close()


def emit_l1_attn(nc, es, S, st):
    nt = st.nt
    dram = st.dram
    SV, SS, TN = st.SV, st.SS, st.TN
    B = st.banks
    phase_barrier(S, st)
    sc = Scope(st.arena)
    NH = 6
    NG = nt // 4

    WL = S.sem("l1_wl")
    CS = S.sem("l1_cs")
    HS = S.sem("l1_hs")
    cache = {}
    sizes = {"l1_qT": (8, 768), "l1_kT": (8, 768), "l1_vT": (8, 1536),
             "l1_gT": (8, 1536), "l1_oT": (12, 1024)}
    for nm, (blocks, ob) in sizes.items():
        cache[nm] = nc.dram_tensor("c_" + nm, [blocks * 128, ob], BF16,
                                   kind="Internal").ap()
    nw_cols = sc.f32(12)
    NW = S.sem("l1_nw")
    S.sy(lambda e: e.dma_start(out=nw_cols, in_=dram["l1_nw_cols"]), inc=NW, incn=16)
    S.waitonly("vector", [(NW, NW.count)])
    psc = Scope(st.arena)
    wstg = psc.f32(4096)
    wstgb = psc.bf16(4096)
    for nm, (blocks, ob) in sizes.items():
        rows_per = 4096 // ob
        ic = 0
        while ic < blocks:
            n = min(rows_per, blocks - ic)
            S.sy(lambda e, d=wstg[:, 0:n * ob].rearrange("p (c o) -> p c o", c=n),
                 s=dram[nm][ic * 128:(ic + n) * 128, :].rearrange("(c p) o -> p c o", p=128):
                 e.dma_start(out=d, in_=s),
                 wait=[(CS, CS.count)], inc=WL, incn=16)
            if nm == "l1_oT":
                ct = None
                for j in range(n):
                    ct = S.v(lambda e, d=wstgb[:, j * ob:(j + 1) * ob],
                             s=wstg[:, j * ob:(j + 1) * ob],
                             sl=nw_cols[:, ic + j:ic + j + 1]:
                             e.tensor_scalar_mul(d, s, sl),
                             wait=[(WL, WL.count), (HS, HS.count)], inc=CS)
            else:
                ct = S.v(lambda e, d=wstgb[:, 0:n * ob], s=wstg[:, 0:n * ob]:
                         e.tensor_copy(d, s),
                         wait=[(WL, WL.count), (HS, HS.count)], inc=CS)
            S.sy(lambda e, d=cache[nm][ic * 128:(ic + n) * 128, :].rearrange(
                 "(c p) o -> p c o", p=128),
                 s=wstgb[:, 0:n * ob].rearrange("p (c o) -> p c o", c=n):
                 e.dma_start(out=d, in_=s),
                 wait=[(CS, CS.count)], inc=HS, incn=16)
            ic += n
    cache_tok = HS.count
    psc.close()

    SM = S.sem("l1_sm")
    aT32b = sc.bf16(8 * 32)
    bT32b = sc.bf16(8 * 32)
    abstg = st.ldbuf[:, 0:256]
    convq = sc.f32(24)
    convk = sc.f32(24)
    convv = sc.f32(48)
    msl = sc.bf16(128)
    miu = sc.bf16(128)
    msu = sc.bf16(128)
    stg2 = st.ldbuf[:, 0:1024]
    S.sy(lambda e: e.dma_start(out=convq, in_=dram["l1_convq_cols"]), inc=SM, incn=16)
    S.sy(lambda e: e.dma_start(out=convk, in_=dram["l1_convk_cols"]), inc=SM, incn=16)
    S.sy(lambda e: e.dma_start(out=convv, in_=dram["l1_convv_cols"]), inc=SM, incn=16)
    m1 = S.sy(lambda e: e.dma_start(out=abstg.rearrange("p (c o) -> p c o", c=8),
                                    in_=dram["l1_aT32"].rearrange("(c p) o -> p c o", p=128)),
              wait=[st.tile_tok[nt - 1]], inc=SM, incn=16)
    c0a = S.v(lambda e: e.tensor_copy(aT32b, abstg), wait=[(SM, SM.count)], inc=CS)
    S.sy(lambda e: e.dma_start(out=abstg.rearrange("p (c o) -> p c o", c=8),
                               in_=dram["l1_bT32"].rearrange("(c p) o -> p c o", p=128)),
         wait=[(CS, c0a)], inc=SM, incn=16)
    c0 = S.v(lambda e: e.tensor_copy(bT32b, abstg), wait=[(SM, SM.count)], inc=CS)
    S.sy(lambda e: e.dma_start(out=stg2[:, 0:128], in_=dram["mask_strict_low"]),
         wait=[(CS, c0)], inc=SM, incn=16)
    c1 = S.v(lambda e: e.tensor_copy(msl, stg2[:, 0:128]), wait=[(SM, SM.count)], inc=CS)
    S.sy(lambda e: e.dma_start(out=stg2[:, 0:128], in_=dram["tri_incl"]),
         wait=[(CS, c1)], inc=SM, incn=16)
    c2 = S.v(lambda e: e.tensor_copy(miu, stg2[:, 0:128]), wait=[(SM, SM.count)], inc=CS)
    c3 = S.v(lambda e: e.tensor_sub(msu, miu, st.ident[:]), wait=[(CS, c2)], inc=CS)
    alog = sc.f32(1, rows=32)
    dtb = sc.f32(1, rows=32)
    negeA = sc.f32(1, rows=32)
    zz1 = S.v(lambda e: e.memset(alog, 0.0), inc=CS)
    zz2 = S.v(lambda e: e.memset(dtb, 0.0), wait=[(CS, zz1)], inc=CS)
    SM2 = S.sem("l1_sm2")
    S.sy(lambda e: e.dma_start(out=alog[0:6, :], in_=dram["l1_Alog_row"].rearrange("a b -> b a")),
         wait=[(CS, zz2)], inc=SM2, incn=16)
    S.sy(lambda e: e.dma_start(out=dtb[0:6, :], in_=dram["l1_dtb_row"].rearrange("a b -> b a")),
         inc=SM2, incn=16)
    e1 = S.s(lambda e: e.activation(negeA, alog, AF.Exp), wait=[(SM2, SM2.count)], inc=SS)
    S.v(lambda e: e.tensor_scalar_mul(negeA, negeA, -1.0), wait=[(SS, e1)], inc=CS)
    sm_tok = CS.count

    S_f = [sc.f32(256) for _ in range(NH)]
    S_bf = [sc.bf16(256) for _ in range(NH)]
    pv = S.sem("l1_pv")
    for h in range(NH):
        S.v(lambda e, d=S_f[h]: e.memset(d, 0.0), inc=pv)
        S.v(lambda e, d=S_bf[h]: e.memset(d, 0.0), inc=pv)
    onescol = sc.bf16(1)
    onesrow_f = sc.f32(128, rows=1)
    S.v(lambda e: e.memset(onescol, 1.0), inc=pv)
    S.v(lambda e: e.memset(onesrow_f, 1.0), inc=pv)
    qsliv = sc.bf16(6 * 4)
    ksliv = sc.bf16(6 * 4)
    vsliv = sc.bf16(12 * 4)
    for sl_ in (qsliv, ksliv, vsliv):
        S.v(lambda e, d=sl_: e.memset(d, 0.0), inc=pv)
    prep_tok = pv.count

    wbuf1 = sc.bf16(12288)
    wbuf2 = sc.bf16(12288)
    WS1 = S.sem("l1_ws1")
    WS2 = S.sem("l1_ws2")

    x2fm = sc.bf16(8 * 512)
    u_qh = sc.bf16(516)
    u_kh = sc.bf16(516)
    u_vh = sc.bf16(2 * 516)
    cqh = sc.bf16(512)
    ckh = sc.bf16(512)
    cvh = sc.bf16(2 * 512)
    cvt = sc.bf16(512)
    gate_tm = sc.bf16(1536)
    o_tm = sc.bf16(4 * 1536)
    g_fm = sc.f32(512, rows=32)
    b_fm = sc.f32(512, rows=32)
    c_fm = sc.f32(512, rows=32)
    sp_fm = sc.f32(512, rows=32)
    c_tm = sc.f32(4 * 6)
    b_tm = sc.f32(4 * 6)
    nc_tm = sc.f32(4 * 6)
    nb_tm = sc.f32(4 * 6)
    crow = sc.f32(512, rows=1)
    nrow = sc.f32(512, rows=1)
    erow = sc.f32(512, rows=1)
    rnk = sc.f32(512, rows=1)
    rnq = sc.f32(512, rows=1)
    rowsf = sc.f32(512, rows=1)
    rowsb = sc.bf16(5 * 512, rows=1)
    kbrow = sc.bf16(128, rows=1)
    sqb = sc.bf16(512)
    Xl = sc.bf16(128)
    Xu = sc.bf16(128)
    Gt = sc.bf16(128)
    Gn = sc.bf16(128)
    Pn = sc.bf16(128)
    Qn = sc.bf16(128)
    Qt = sc.bf16(128)
    Aqt = sc.bf16(128)
    kpf = sc.bf16(128)
    khf = sc.bf16(128)
    kbf = sc.bf16(128)
    qhf = sc.bf16(128)
    qpf = sc.bf16(128)
    kbt = sc.bf16(128)
    v_t = sc.bf16(256)
    Rt = sc.bf16(256)
    Dl = sc.bf16(256)
    gcol = sc.f32(1)
    ln_st = sc.f32(8)
    rres = sc.f32(1024)
    ogf = sc.bf16(12 * 128)
    ln_sq = st.ldbuf[:, 1024:2048]

    ps_bc = B[0]
    ps_kk = B[1]
    ps_qk = B[2]
    ps_pj = [B[3], B[4]]
    ps_d = [B[3], B[4], B[5]]
    ps_ks = B[6]
    ps_ab = B[6]
    ps_o = B[5]
    ps_tp = st.tp_bank[:, 0:128]

    gf = S.sem("l1_gf")
    tf = S.sem("l1_tf")
    df = S.sem("l1_df")
    RD = S.sem("l1_rd")

    for grp in range(NG):
        t0 = grp * 4
        S.sy(lambda e, d=wbuf1[:, 0:6144].rearrange("p (c o) -> p c o", c=8),
             s=cache["l1_qT"][:].rearrange("(c p) o -> p c o", p=128):
             e.dma_start(out=d, in_=s),
             wait=[(HS, cache_tok), (TN, TN.count)], inc=WS1, incn=16)
        S.sy(lambda e, d=wbuf1[:, 6144:12288].rearrange("p (c o) -> p c o", c=8),
             s=cache["l1_kT"][:].rearrange("(c p) o -> p c o", p=128):
             e.dma_start(out=d, in_=s), inc=WS1, incn=16)
        S.sy(lambda e, d=wbuf2[:].rearrange("p (c o) -> p c o", c=8),
             s=cache["l1_vT"][:].rearrange("(c p) o -> p c o", p=128):
             e.dma_start(out=d, in_=s),
             wait=[(TN, TN.count)], inc=WS2, incn=16)
        qkv_rdy1 = WS1.count
        qkv_rdy2 = WS2.count
        qTv = wbuf1[:, 0:6144]
        kTv = wbuf1[:, 6144:12288]
        vTv = wbuf2

        for tt in range(4):
            gt = t0 + tt
            for c in range(8):
                w = [(tf, tf.count)] if (grp > 0 or tt > 0 or c > 0) else [(pv, prep_tok)]
                if c == 0:
                    w = w + [st.tile_tok[gt]]
                tk = S.t(lambda e, sl=st.stream[:, gt * D + c * 128: gt * D + (c + 1) * 128]:
                         e.transpose(ps_tp, sl, st.ident[:]), wait=w, inc=TN)
                S.v(lambda e, d=x2fm[:].rearrange("p (a b) -> p a b", b=512)
                    [:, c, tt * 128:(tt + 1) * 128]:
                    e.tensor_copy(d, ps_tp), wait=[(TN, tk)], inc=tf)
        x2_tok = tf.count

        for which, wmat2 in ((0, aT32b), (1, bT32b)):
            for ic in range(8):
                w = [(df, df.count), (tf, x2_tok), (CS, sm_tok)] if ic == 0 and which == 0 else []
                S.t(lambda e, p2=ps_ab[32 * which:32 * which + 32, :],
                    wsl=wmat2[:, ic * 32:(ic + 1) * 32],
                    r=x2fm[:].rearrange("p (a b) -> p a b", b=512)[:, ic, :],
                    s0=(ic == 0), s1=(ic == 7):
                    e.matmul(p2, wsl, r, start=s0, stop=s1),
                    wait=w, inc=(TN if ic == 7 else None))
        ab_tok = TN.count
        v1 = S.v(lambda e, bb=dtb.broadcast_to([32, 512]):
                 e.tensor_tensor(g_fm, ps_ab[0:32, :], bb, ALU.add),
                 wait=[(TN, ab_tok), (SV, SV.count), (SS, SS.count)], inc=SV)
        v2 = S.s(lambda e: e.activation(sp_fm, g_fm, AF.Abs),
                 wait=[(SV, v1), (SS, SS.count)], inc=SS)
        s1 = S.s(lambda e: e.activation(sp_fm, sp_fm, AF.Exp, scale=-1.0),
                 wait=[(SS, v2)], inc=SS)
        s2 = S.s(lambda e: e.activation(sp_fm, sp_fm, AF.Ln, bias=1.0),
                 wait=[(SS, s1)], inc=SS)
        v3 = S.v(lambda e: e.tensor_relu(c_fm, g_fm), wait=[(SV, v1), (SS, v2)], inc=SV)
        v4 = S.v(lambda e: e.tensor_add(g_fm, c_fm, sp_fm),
                 wait=[(SV, v3), (SS, s2)], inc=SV)
        v5 = S.v(lambda e, bb=negeA.broadcast_to([32, 512]):
                 e.tensor_tensor(g_fm, g_fm, bb, ALU.mult),
                 wait=[(SV, v4)], inc=SV)
        s3 = S.s(lambda e: e.activation(b_fm, ps_ab[32:64, :], AF.Sigmoid),
                 wait=[(TN, ab_tok), (SS, s2), (SV, v4)], inc=SS)
        cprev = v5
        for cc in range(4):
            cprev = S.v(lambda e, d=c_fm[:, cc * 128:(cc + 1) * 128],
                        s=g_fm[:, cc * 128:(cc + 1) * 128]:
                        e.tensor_tensor_scan(d, s, s, 0.0, ALU.add, ALU.bypass),
                        wait=[(SV, cprev)], inc=SV)
        cums_tok = cprev
        for srcb, dstb, negb, wtok in ((c_fm, c_tm, nc_tm, cums_tok),
                                       (b_fm, b_tm, nb_tm, None)):
            for cc in range(4):
                w = [(SV, SV.count), (SS, SS.count), (TN, TN.count)]
                tk = S.t(lambda e, sl=srcb[0:6, cc * 128:(cc + 1) * 128]:
                         e.transpose(ps_qk[:, 0:6], sl, st.ident_f32[0:6, 0:6]),
                         wait=w, inc=TN)
                vv = S.v(lambda e, d=dstb[:, cc * 6:(cc + 1) * 6]:
                         e.tensor_copy(d, ps_qk[:, 0:6]), wait=[(TN, tk)], inc=SV)
                S.v(lambda e, d=negb[:, cc * 6:(cc + 1) * 6],
                    s=dstb[:, cc * 6:(cc + 1) * 6]:
                    e.tensor_scalar_mul(d, s, -1.0), wait=[(SV, vv)], inc=SV)
        ctb_tok = SV.count

        b0_free = None
        b2e_free = None
        for h in range(NH):
            for wv_, ubuf, sliv, noc, base in ((qTv, u_qh, qsliv, 1, h),
                                               (kTv, u_kh, ksliv, 1, h),
                                               (vTv, u_vh, vsliv, 2, 2 * h)):
                for oc in range(noc):
                    psb = ps_pj[oc % 2]
                    for ic in range(8):
                        w = []
                        if ic == 0:
                            w = [(gf, gf.count), (SV, SV.count)]
                            if h == 0 and wv_ is qTv and oc == 0:
                                w += [(WS1, qkv_rdy1), (WS2, qkv_rdy2), (SV, ctb_tok),
                                      (df, df.count), (SS, SS.count)]
                        stride = 768 if noc == 1 else 1536
                        woff = ic * stride + (base + oc) * 128
                        S.t(lambda e, p2=psb, wsl=wv_[:, woff:woff + 128],
                            r=x2fm[:].rearrange("p (a b) -> p a b", b=512)[:, ic, :],
                            s0=(ic == 0), s1=(ic == 7):
                            e.matmul(p2[:], wsl, r, start=s0, stop=s1),
                            wait=w, inc=(TN if ic == 7 else None))
                    tk = TN.count
                    S.v(lambda e, d=ubuf[:].rearrange("p (a b) -> p a b", b=516)[:, oc, 0:4],
                        s=sliv[:, (base + oc) * 4:(base + oc + 1) * 4]:
                        e.tensor_copy(d, s), wait=[(SV, SV.count)], inc=SV)
                    S.v(lambda e, p2=psb, d=ubuf[:].rearrange("p (a b) -> p a b", b=516)
                        [:, oc, 4:516]:
                        e.tensor_copy(d, p2[:]),
                        wait=[(TN, tk), (SV, SV.count)], inc=gf)
            projh_tok = gf.count
            conv_sets = ((u_qh, cqh, convq, 1, h, qsliv), (u_kh, ckh, convk, 1, h, ksliv),
                         (u_vh, cvh, convv, 2, 2 * h, vsliv))
            for ubuf, cbuf, wcols, noc, base, sliv in conv_sets:
                vv = S.v(lambda e, d=cbuf[:].rearrange("p (a b) -> p a b", b=512)[:, 0:noc, :],
                         s=ubuf[:].rearrange("p (a b) -> p a b", b=516)[:, 0:noc, 1:513],
                         wt=wcols[:].rearrange("p (a b) -> p a b", b=4)
                         [:, base:base + noc, 0:1].broadcast_to([128, noc, 512]):
                         e.tensor_tensor(d, s, wt, ALU.mult),
                         wait=[(gf, projh_tok), (SV, SV.count)], inc=SV)
                for j in range(1, 4):
                    for oc in range(noc):
                        vt_ = S.v(lambda e, d=cvt[:],
                                  s=ubuf[:].rearrange("p (a b) -> p a b", b=516)
                                  [:, oc, 1 + j:513 + j],
                                  wt=wcols[:, (base + oc) * 4 + j:(base + oc) * 4 + j + 1]:
                                  e.tensor_scalar_mul(d, s, wt),
                                  wait=[(SV, SV.count)], inc=SV)
                        vv = S.v(lambda e, d=cbuf[:].rearrange("p (a b) -> p a b", b=512)[:, oc, :],
                                 s=cvt[:]:
                                 e.tensor_tensor(d, d, s, ALU.add),
                                 wait=[(SV, vt_)], inc=SV)
                S.v(lambda e, d=sliv[:, base * 4:(base + noc) * 4].rearrange(
                    "p (a b) -> p a b", b=4)[:, :, 1:4],
                    s=ubuf[:].rearrange("p (a b) -> p a b", b=516)[:, 0:noc, 513:516],
                    sl2=0:
                    e.tensor_copy(d, s), wait=[(SV, vv)], inc=SV)
                S.s(lambda e, d=cbuf[:, 0:noc * 512]:
                    e.activation(d, d, AF.Silu),
                    wait=[(SV, SV.count), (SS, SS.count)], inc=SS)
            conv_tok = SS.count
            import os as _os
            if _os.environ.get("L1A_DBG") == f"conv{grp}_{h}":
                DB = S.sem(f"dbg{grp}{h}")
                for nmb2, off in ((cqh, 0), (ckh, 512), (cvh, 1024)):
                    S.g(lambda e, d=dram["dbg"][:, off:off + (1024 if off == 1024 else 512)],
                        s=nmb2[:, 0:(1024 if off == 1024 else 512)]:
                        e.dma_start(out=d, in_=s),
                        wait=[(SS, conv_tok)], inc=DB, incn=16)
                S.waitonly("vector", [(DB, DB.count)])
                S.waitonly("scalar", [(DB, DB.count)])

            r1 = S.sy(lambda e, d=crow, s=c_fm[h:h + 1, :]: e.dma_start(out=d, in_=s),
                      wait=[(SV, ctb_tok), (TN, TN.count), (SS, SS.count),
                            (SV, SV.count)], inc=RD, incn=16)
            S.sy(lambda e, d=rowsf, s=b_fm[h:h + 1, :]: e.dma_start(out=d, in_=s),
                 wait=[(RD, RD.count)], inc=RD, incn=16)
            rd_tok = RD.count
            nv = S.v(lambda e: e.tensor_scalar_mul(nrow, crow, -1.0),
                     wait=[(RD, rd_tok), (SV, SV.count), (SS, SS.count)], inc=SV)
            bv = S.v(lambda e, d=rowsb[0:1, 0:2560].rearrange("p (a b) -> p a b", b=640)
                     [:, :, 384:512],
                     s=rowsf.rearrange("p (a b) -> p a b", b=128):
                     e.tensor_copy(d, s), wait=[(SV, nv)], inc=SV)
            ev = S.s(lambda e: e.activation(erow, crow, AF.Exp),
                     wait=[(RD, rd_tok), (SS, conv_tok)], inc=SS)
            sqk = S.v(lambda e: e.tensor_tensor(sqb, ckh, ckh, ALU.mult),
                      wait=[(SS, conv_tok), (SV, bv)], inc=SV)
            tk1 = S.t(lambda e: e.matmul(ps_bc[0:1, :], onescol, sqb, start=True, stop=True),
                      wait=[(SV, sqk), (TN, TN.count)], inc=TN)
            sk1 = S.s(lambda e: e.activation(rnk, ps_bc[0:1, 0:512], AF.Ln,
                                             bias=st.c_eps6[0:1, :]),
                      wait=[(TN, tk1), (SS, ev), (SV, SV.count)], inc=SS)
            sk2 = S.s(lambda e: e.activation(rnk, rnk, AF.Exp, scale=-0.5),
                      wait=[(SS, sk1)], inc=SS)
            sqq = S.v(lambda e: e.tensor_tensor(sqb, cqh, cqh, ALU.mult),
                      wait=[(SV, sqk), (TN, tk1)], inc=SV)
            tk2 = S.t(lambda e: e.matmul(ps_bc[0:1, :], onescol, sqb, start=True, stop=True),
                      wait=[(SV, sqq), (SS, sk2)], inc=TN)
            sq1 = S.s(lambda e: e.activation(rnq, ps_bc[0:1, 0:512], AF.Ln,
                                             bias=st.c_eps6[0:1, :]),
                      wait=[(TN, tk2), (SS, sk2)], inc=SS)
            sq2 = S.s(lambda e: e.activation(rnq, rnq, AF.Exp, scale=-0.5,
                                             bias=st.c_qln[0:1, :]),
                      wait=[(SS, sq1)], inc=SS)
            rbv = rowsb[0:1, 0:2560].rearrange("p (a b) -> p a b", b=640)
            w1 = S.v(lambda e, d=rbv[:, :, 0:128],
                     s=rnk.rearrange("p (a b) -> p a b", b=128):
                     e.tensor_copy(d, s), wait=[(SS, sq2), (SV, SV.count)], inc=SV)
            w2 = S.v(lambda e: e.tensor_mul(rowsf, rnk, erow), wait=[(SV, w1)], inc=SV)
            w3 = S.v(lambda e, d=rbv[:, :, 128:256],
                     s=rowsf.rearrange("p (a b) -> p a b", b=128):
                     e.tensor_copy(d, s), wait=[(SV, w2)], inc=SV)
            w4 = S.v(lambda e, d=rbv[:, :, 256:384],
                     s=rnq.rearrange("p (a b) -> p a b", b=128):
                     e.tensor_copy(d, s), wait=[(SV, w3)], inc=SV)
            w5 = S.v(lambda e, d=rbv[:, :, 512:640],
                     s=erow.rearrange("p (a b) -> p a b", b=128):
                     e.tensor_copy(d, s), wait=[(SV, w4)], inc=SV)
            rows_tok = SV.count
            if _os.environ.get("L1A_DBG") == f"rows{grp}_{h}":
                DB2 = S.sem(f"dbg2{grp}{h}")
                for nmb2, off in ((crow, 0), (erow, 512), (rnk, 1024), (rnq, 1536)):
                    S.g(lambda e, d=dram["dbg"][0:1, off:off + 512], s=nmb2:
                        e.dma_start(out=d, in_=s),
                        wait=[(SV, rows_tok)], inc=DB2, incn=16)
                S.waitonly("vector", [(DB2, DB2.count)])
                S.waitonly("scalar", [(DB2, DB2.count)])
                S.waitonly("tensor", [(DB2, DB2.count)])

            for cc in range(4):
                cs = slice(cc * 128, (cc + 1) * 128)
                ccol = c_tm[:, cc * 6 + h:cc * 6 + h + 1]
                nccol = nc_tm[:, cc * 6 + h:cc * 6 + h + 1]
                nbcol = nb_tm[:, cc * 6 + h:cc * 6 + h + 1]
                bcol = b_tm[:, cc * 6 + h:cc * 6 + h + 1]
                for vh in range(2):
                    tkv = S.t(lambda e, sl=cvh[:].rearrange("p (a b) -> p a b", b=512)
                              [:, vh, cs]:
                              e.transpose(ps_tp, sl, st.ident[:]),
                              wait=[(tf, tf.count), (SV, rows_tok)], inc=TN)
                    S.v(lambda e, d=v_t[:, vh * 128:(vh + 1) * 128]:
                        e.tensor_copy(d, ps_tp), wait=[(TN, tkv)], inc=tf)
                vt_tok = tf.count
                w = [(SV, rows_tok)]
                if b0_free is not None:
                    w.append(b0_free)
                tb1 = S.t(lambda e, r=rowsb[0:1, cc * 640:cc * 640 + 512]:
                          e.matmul(ps_bc[:, 0:512], st.ones_row[:, 0:128], r,
                                   start=True, stop=True),
                          wait=w, inc=TN)
                w = []
                if b2e_free is not None:
                    w.append(b2e_free)
                tb3b = S.t(lambda e, r=rowsb[0:1, cc * 640 + 512:cc * 640 + 640]:
                           e.matmul(ps_qk[:, 256:384], st.ones_row[:, 0:128], r,
                                    start=True, stop=True),
                           wait=w, inc=TN)
                vk1 = S.v(lambda e, s=ckh[:, cs]:
                          e.tensor_mul(kpf, s, ps_bc[:, 0:128]),
                          wait=[(TN, tb1), (SV, SV.count)], inc=SV)
                vk2 = S.v(lambda e, s=ckh[:, cs]:
                          e.tensor_mul(khf, s, ps_bc[:, 128:256]),
                          wait=[(SV, vk1)], inc=SV)
                vq0 = S.v(lambda e, s=cqh[:, cs]:
                          e.tensor_mul(qpf, s, ps_bc[:, 256:384]),
                          wait=[(SV, vk2)], inc=SV)
                vq1 = S.v(lambda e:
                          e.tensor_mul(qhf, qpf, ps_qk[:, 256:384]),
                          wait=[(TN, tb3b), (SV, vq0)], inc=SV)
                sd1 = S.s(lambda e, r=crow[0:1, cs],
                          bia=nrow[0:1, cc * 128 + 127:cc * 128 + 128]:
                          e.activation(kbrow, r, AF.Relu, bias=bia),
                          wait=[(SV, vq1), (SS, SS.count)], inc=SS)
                sd2 = S.s(lambda e: e.activation(kbrow, kbrow, AF.Exp, scale=-1.0),
                          wait=[(SS, sd1)], inc=SS)
                vd1 = S.v(lambda e, r=rnk[0:1, cs]:
                          e.tensor_mul(kbrow, kbrow, r),
                          wait=[(SS, sd2), (SV, SV.count)], inc=SV)
                tb4 = S.t(lambda e:
                          e.matmul(ps_qk[:, 256:384], st.ones_row[:, 0:128], kbrow,
                                   start=True, stop=True),
                          wait=[(SV, vd1)], inc=TN)
                vk3 = S.v(lambda e, s=ckh[:, cs]:
                          e.tensor_mul(kbf, s, ps_qk[:, 256:384]),
                          wait=[(TN, tb4)], inc=SV)
                tkb = S.t(lambda e: e.transpose(ps_tp, kbf, st.ident[:]),
                          wait=[(SV, vk3), (tf, vt_tok)], inc=TN)
                vkb = S.v(lambda e: e.tensor_copy(kbt, ps_tp), wait=[(TN, tkb)], inc=tf)
                tr1 = S.t(lambda e, r=crow[0:1, cs]:
                          e.matmul(ps_kk[:, 0:128], onesrow_f, r, start=True, stop=True),
                          wait=[(SV, SV.count), (TN, TN.count), (SS, SS.count)], inc=TN)
                if _os.environ.get("L1A_DBG", "").startswith("core"):
                    S.v(lambda e: e.tensor_copy(cvt[:, 0:128], ps_kk[:, 0:128]),
                        wait=[(TN, tr1), (SV, SV.count)], inc=SV)
                sx1 = S.s(lambda e, bia=nccol: e.activation(Xl, ps_kk[:, 0:128], AF.Relu,
                                                            scale=1.0, bias=bia),
                          wait=[(TN, tr1), (SS, SS.count), (SV, SV.count)], inc=SS)
                sx3 = S.s(lambda e, bia=ccol: e.activation(Xu, ps_kk[:, 0:128], AF.Relu,
                                                           scale=-1.0, bias=bia),
                          wait=[(SS, sx1)], inc=SS)
                sx2 = S.s(lambda e: e.activation(Xl, Xl, AF.Exp, scale=-1.0),
                          wait=[(SS, sx3)], inc=SS)
                sx4 = S.s(lambda e: e.activation(Xu, Xu, AF.Exp, scale=-1.0),
                          wait=[(SS, sx2)], inc=SS)
                sx5 = S.s(lambda e: e.activation(gcol, ps_kk[:, 127:128], AF.Exp),
                          wait=[(SS, sx4)], inc=SS)
                b2e_free = (SV, vk3)
                tkk = S.t(lambda e: e.matmul(ps_qk[:, 0:128], kpf, kpf,
                                             start=True, stop=True),
                          wait=[(SS, sx5), (SV, vk3)], inc=TN)
                tqk = S.t(lambda e: e.matmul(ps_qk[:, 128:256], kpf, qpf,
                                             start=True, stop=True), inc=TN)
                gv1 = S.v(lambda e: e.tensor_mul(Gt, Xl, ps_qk[:, 0:128]),
                          wait=[(TN, tkk), (SV, SV.count)], inc=SV)
                gv2 = S.v(lambda e: e.tensor_mul(Gt, Gt, msl), wait=[(SV, gv1)], inc=SV)
                gv3 = S.v(lambda e, bia=nbcol: e.tensor_scalar_mul(Gt, Gt, bia),
                          wait=[(SV, gv2)], inc=SV)
                gn1 = S.v(lambda e: e.tensor_mul(Gn, Xu, ps_qk[:, 0:128]),
                          wait=[(SV, gv3)], inc=SV)
                gn2 = S.v(lambda e: e.tensor_mul(Gn, Gn, msu), wait=[(SV, gn1)], inc=SV)
                gn3 = S.v(lambda e: e.tensor_mul(Gn, Gn, ps_bc[:, 384:512]),
                          wait=[(SV, gn2)], inc=SV)
                b0_free = (SV, gn3)
                gn4 = S.v(lambda e: e.tensor_scalar_mul(Gn, Gn, -1.0),
                          wait=[(SV, gn3)], inc=SV)
                aq1 = S.v(lambda e: e.tensor_mul(Aqt, Xu, ps_qk[:, 128:256]),
                          wait=[(TN, tqk), (SV, gn4)], inc=SV)
                aq2 = S.v(lambda e: e.tensor_mul(Aqt, Aqt, miu), wait=[(SV, aq1)], inc=SV)
                pv1 = S.v(lambda e: e.tensor_add(Pn, Gn, st.ident[:]),
                          wait=[(SV, gn4)], inc=SV)
                tq1 = S.t(lambda e: e.matmul(ps_d[0][:, 0:128], Gt, Gn,
                                             start=True, stop=True),
                          wait=[(SV, pv1), (SV, aq2)], inc=TN)
                tq2 = S.t(lambda e: e.matmul(ps_d[1][:, 0:128], Gn, Gt,
                                             start=True, stop=True), inc=TN)
                qtokn = S.v(lambda e: e.tensor_copy(Qn, ps_d[0][:, 0:128]),
                            wait=[(TN, tq2)], inc=SV)
                qtokt = S.v(lambda e: e.tensor_copy(Qt, ps_d[1][:, 0:128]),
                            wait=[(TN, tq2)], inc=SV)
                ptok = pv1
                for it in range(6):
                    tqp = S.t(lambda e: e.matmul(ps_d[2][:, 0:128], Qt, Pn,
                                                 start=True, stop=True),
                              wait=[(SV, qtokt), (SV, ptok)], inc=TN)
                    if it < 5:
                        ta = S.t(lambda e: e.matmul(ps_d[0][:, 0:128], Qt, Qn,
                                                    start=True, stop=True),
                                 wait=[(SV, qtokn)], inc=TN)
                        tb_ = S.t(lambda e: e.matmul(ps_d[1][:, 0:128], Qn, Qt,
                                                     start=True, stop=True), inc=TN)
                    ptok = S.v(lambda e: e.tensor_add(Pn, Pn, ps_d[2][:, 0:128]),
                               wait=[(TN, tqp), (SV, SV.count)], inc=SV)
                    if it < 5:
                        qtokn = S.v(lambda e: e.tensor_copy(Qn, ps_d[0][:, 0:128]),
                                    wait=[(TN, tb_)], inc=SV)
                        qtokt = S.v(lambda e: e.tensor_copy(Qt, ps_d[1][:, 0:128]),
                                    wait=[(TN, tb_)], inc=SV)
                tks = S.t(lambda e, sb=S_bf[h]: e.matmul(ps_ks[:, 0:256], khf, sb,
                                                         start=True, stop=True),
                          wait=[(SV, ptok), (tf, tf.count)], inc=TN)
                rv1 = S.v(lambda e: e.tensor_sub(Rt, v_t, ps_ks[:, 0:256]),
                          wait=[(TN, tks), (SV, SV.count)], inc=SV)
                rv2 = S.v(lambda e, bc=bcol: e.tensor_scalar_mul(Rt, Rt, bc),
                          wait=[(SV, rv1)], inc=SV)
                td = S.t(lambda e: e.matmul(ps_d[2][:, 0:256], Pn, Rt,
                                            start=True, stop=True),
                         wait=[(SV, rv2)], inc=TN)
                dv = S.v(lambda e: e.tensor_copy(Dl, ps_d[2][:, 0:256]),
                         wait=[(TN, td)], inc=SV)
                to1 = S.t(lambda e: e.matmul(ps_o[:, 0:256], Aqt, Dl,
                                             start=True, stop=False),
                          wait=[(SV, dv)], inc=TN)
                to2 = S.t(lambda e, sb=S_bf[h]: e.matmul(ps_o[:, 0:256], qhf, sb,
                                                         start=False, stop=True), inc=TN)
                ov = S.v(lambda e, d=o_tm[:].rearrange("p (a b) -> p a b", b=1536)
                         [:, cc, h * 256:(h + 1) * 256]:
                         e.tensor_copy(d, ps_o[:, 0:256]), wait=[(TN, to2)], inc=SV)
                tsu = S.t(lambda e: e.matmul(ps_d[0][:, 0:256], kbt, Dl,
                                             start=True, stop=True),
                          wait=[(SV, ov)], inc=TN)
                su1 = S.v(lambda e, d=S_f[h]: e.tensor_scalar_mul(d, d, gcol),
                          wait=[(SS, sx5), (TN, to2), (SV, ov)], inc=SV)
                su2 = S.v(lambda e, d=S_f[h]: e.tensor_add(d, d, ps_d[0][:, 0:256]),
                          wait=[(TN, tsu), (SV, su1)], inc=SV)
                S.v(lambda e, d=S_bf[h], s=S_f[h]: e.tensor_copy(d, s),
                    wait=[(SV, su2)], inc=df)
                if _os.environ.get("L1A_DBG") == f"core{grp}_{h}_{cc}":
                    DB4 = S.sem("dbg4")
                    for nmb2, off, wd in ((cvt[:, 0:128], 2200, 128), (nc_tm, 2328, 24),
                                          (c_tm, 2048, 24), (b_tm, 2100, 24),
                                          (Xl, 0, 128), (Xu, 128, 128), (Gt, 256, 128),
                                          (Gn, 384, 128), (Pn, 512, 128), (Aqt, 640, 128),
                                          (Rt, 768, 256), (Dl, 1024, 256), (kpf, 1280, 128),
                                          (qpf, 1408, 128), (khf, 1536, 128), (v_t, 1664, 256),
                                          (kbt, 1920, 128)):
                        S.g(lambda e, d=dram["dbg"][:, off:off + wd], s=nmb2:
                            e.dma_start(out=d, in_=s), wait=[(df, df.count)], inc=DB4, incn=16)
                    S.waitonly("vector", [(DB4, DB4.count)])
                    S.waitonly("scalar", [(DB4, DB4.count)])
                    S.waitonly("tensor", [(DB4, DB4.count)])
        delta_tok = df.count
        if _os.environ.get("L1A_DBG") == f"otm{grp}":
            DB3 = S.sem(f"dbg3{grp}")
            S.g(lambda e, d=dram["dbg"][:, 0:6144], s=o_tm[:]:
                e.dma_start(out=d, in_=s), wait=[(df, delta_tok)], inc=DB3, incn=16)
            S.waitonly("vector", [(DB3, DB3.count)])

        S.sy(lambda e, d=wbuf1[:].rearrange("p (c o) -> p c o", c=8),
             s=cache["l1_gT"][:].rearrange("(c p) o -> p c o", p=128):
             e.dma_start(out=d, in_=s),
             wait=[(df, delta_tok), (TN, TN.count)], inc=WS1, incn=16)
        S.sy(lambda e, d=wbuf2[:].rearrange("p (c o) -> p c o", c=12),
             s=cache["l1_oT"][:].rearrange("(c p) o -> p c o", p=128):
             e.dma_start(out=d, in_=s),
             wait=[(df, delta_tok), (TN, TN.count)], inc=WS2, incn=16)
        go_rdy1 = WS1.count
        go_rdy2 = WS2.count

        for cc in range(4):
            gt = t0 + cc
            for ocg in range(3):
                psb = ps_pj[ocg % 2]
                for ic in range(8):
                    w = []
                    if ic == 0:
                        w = [(TN, TN.count), (df, delta_tok), (SV, SV.count),
                             (SS, SS.count)]
                        if cc == 0 and ocg == 0:
                            w += [(WS1, go_rdy1), (WS2, go_rdy2)]
                    S.t(lambda e, p2=psb,
                        lhs=x2fm[:].rearrange("p (a b) -> p a b", b=512)
                        [:, ic, cc * 128:(cc + 1) * 128],
                        wsl=wbuf1[:, ic * 1536 + ocg * 512: ic * 1536 + (ocg + 1) * 512],
                        s0=(ic == 0), s1=(ic == 7):
                        e.matmul(p2[:], lhs, wsl, start=s0, stop=s1),
                        wait=w, inc=(TN if ic == 7 else None))
                tkg = TN.count
                S.s(lambda e, p2=psb, d=gate_tm[:, ocg * 512:(ocg + 1) * 512]:
                    e.activation(d, p2[:], AF.Silu),
                    wait=[(TN, tkg), (SS, SS.count)], inc=SS)
            gate_tok = SS.count
            og = o_tm[:].rearrange("p (a b) -> p a b", b=1536)[:, cc, :]
            m1 = S.v(lambda e, d=og: e.tensor_mul(d, d, gate_tm),
                     wait=[(SS, gate_tok), (SV, SV.count)], inc=SV)
            m2 = S.s(lambda e, d=cvh[:, 0:1024], s=og[:, 0:1024]:
                     e.activation(d, s, AF.Square, accum_out=ln_st[:, 1:2]),
                     wait=[(SV, m1), (SS, SS.count)], inc=SS)
            m2b = S.s(lambda e, d=cvh[:, 0:512], s=og[:, 1024:1536]:
                      e.activation(d, s, AF.Square, accum_out=ln_st[:, 4:5]),
                      wait=[(SS, m2)], inc=SS)
            m3 = S.v(lambda e: e.tensor_scalar(ln_st[:, 2:3], ln_st[:, 1:2],
                                               ln_st[:, 4:5], None, ALU.add),
                     wait=[(SS, m2b), (SV, SV.count)], inc=SV)
            m3b = S.v(lambda e: e.tensor_scalar(ln_st[:, 2:3], ln_st[:, 2:3],
                                                1.0 / 1536, None, ALU.mult),
                      wait=[(SV, m3)], inc=SV)
            m4 = S.s(lambda e: e.activation(ln_st[:, 3:4], ln_st[:, 2:3], AF.Sqrt,
                                            bias=st.c_eps_ln),
                     wait=[(SV, m3b)], inc=SS)
            m5 = S.v(lambda e: e.reciprocal(ln_st[:, 3:4], ln_st[:, 3:4]),
                     wait=[(SS, m4)], inc=SV)
            m6 = S.v(lambda e, d=og: e.tensor_scalar_mul(d, d, ln_st[:, 3:4]),
                     wait=[(SV, m5)], inc=SV)
            for ic in range(12):
                tko = S.t(lambda e, sl=og[:, ic * 128:(ic + 1) * 128]:
                          e.transpose(ps_tp, sl, st.ident[:]),
                          wait=[(SV, m6), (tf, tf.count)], inc=TN)
                S.v(lambda e, d=ogf[:, ic * 128:(ic + 1) * 128]:
                    e.tensor_copy(d, ps_tp), wait=[(TN, tko)], inc=tf)
            og_tok = tf.count
            for oc2 in range(2):
                psb = ps_pj[oc2]
                for ic in range(12):
                    w = [(tf, og_tok)] if ic == 0 else []
                    S.t(lambda e, p2=psb, lhs=ogf[:, ic * 128:(ic + 1) * 128],
                        wsl=wbuf2[:, ic * 1024 + oc2 * 512: ic * 1024 + (oc2 + 1) * 512],
                        s0=(ic == 0), s1=(ic == 11):
                        e.matmul(p2[:], lhs, wsl, start=s0, stop=s1),
                        wait=w, inc=(TN if ic == 11 else None))
            tko2 = TN.count
            wprev = [st.tile_tok[gt - 1]] if gt > 0 else []
            rv1 = S.v(lambda e, p2=ps_pj[0], d=rres[:, 0:512],
                      s=st.stream[:, gt * D: gt * D + 512]:
                      e.tensor_add(d, p2[:], s),
                      wait=[(TN, tko2), (SS, SS.count), (SV, SV.count)] + wprev, inc=SV)
            S.v(lambda e, p2=ps_pj[1], d=rres[:, 512:1024],
                s=st.stream[:, gt * D + 512: gt * D + 1024]:
                e.tensor_add(d, p2[:], s), wait=[(SV, rv1)], inc=SV)
            res_tok = SV.count
            st.tile_tok[gt] = emit_ln(nc, es, S, st, rres, ln_sq, ln_st,
                                      st.stream[:, gt * D:(gt + 1) * D],
                                      wait=[(SV, res_tok)])

    st.l1a_end = {"TN": TN.count}
    sc.close()


def emit_l1_ffn(nc, es, S, st, final_out=False):
    nt = st.nt
    dram = st.dram
    SV, SS, TN = st.SV, st.SS, st.TN
    B = st.banks
    phase_barrier(S, st)
    sc = Scope(st.arena)

    cache = {
        "g": nc.dram_tensor("c_wgq", [1024, 2816], BF16, kind="Internal").ap(),
        "u": nc.dram_tensor("c_wuq", [1024, 2816], BF16, kind="Internal").ap(),
        "d": nc.dram_tensor("c_wdq", [2816, 1024], BF16, kind="Internal").ap(),
    }

    def emit_quant2(xin, xout, nf, qu, qr, bmbuf, tbuf, wait0, dbg_tag=None):
        nb = nf // 16
        a1 = S.v(lambda e: e.reduce_max(
            bmbuf.rearrange("p (a b) -> p a b", b=1),
            xin.rearrange("p (a b) -> p a b", b=16), axis=AX.X,
            apply_absolute_value=True), wait=wait0 + [(SV, SV.count)], inc=SV)
        a2 = S.v(lambda e: e.tensor_scalar(tbuf, bmbuf, 1.0 / 12.0, 5e-13,
                                           ALU.mult, ALU.max), wait=[(SV, a1)], inc=SV)
        a2b = S.v(lambda e: e.tensor_scalar_mul(bmbuf, tbuf, 1.0000005),
                  wait=[(SV, a2)], inc=SV)
        a3 = S.v(lambda e: e.reciprocal(bmbuf, bmbuf), wait=[(SV, a2b)], inc=SV)
        a4 = S.s(lambda e: e.activation(qu, xin, AF.Abs),
                 wait=wait0 + [(SS, SS.count), (SV, a1)], inc=SS)
        a5 = S.v(lambda e, bb=bmbuf.rearrange("p (a b) -> p a b", b=1)
                 .broadcast_to([128, nb, 16]):
                 e.tensor_tensor(qu.rearrange("p (a b) -> p a b", b=16),
                                 qu.rearrange("p (a b) -> p a b", b=16),
                                 bb, ALU.mult),
                 wait=[(SV, a3), (SS, a4)], inc=SV)
        import os as _os
        if dbg_tag and _os.environ.get("L1F_DBG") == dbg_tag:
            DQ1 = S.sem("dbq1")
            S.g(lambda e, d=dram["dbg"][:, 4096:4096 + nf], s=qu:
                e.dma_start(out=d, in_=s), wait=[(SV, a5)], inc=DQ1, incn=16)
            S.g(lambda e, d=dram["dbg"][:, 6656:6656 + min(nf, 1024)], s=xin[:, 0:min(nf, 1024)]:
                e.dma_start(out=d, in_=s), wait=[(SV, a5)], inc=DQ1, incn=16)
            S.g(lambda e, d=dram["dbg"][0:128, 7680:7680 + nf // 16], s=bmbuf:
                e.dma_start(out=d, in_=s), wait=[(SV, a2)], inc=DQ1, incn=16)
            S.waitonly("vector", [(DQ1, DQ1.count)])
            S.waitonly("scalar", [(DQ1, DQ1.count)])
        a7 = S.v(lambda e: e.tensor_scalar(qr.bitcast(I32), qu.bitcast(I32),
                                           0x3F800000, 2097151,
                                           ALU.max, ALU.add),
                 wait=[(SV, a5)], inc=SV)
        a8 = S.v(lambda e: e.tensor_scalar(qr.bitcast(I32), qr.bitcast(I32),
                                           -4194304, None, ALU.bitwise_and),
                 wait=[(SV, a7)], inc=SV)
        a8m = S.v(lambda e: e.scalar_tensor_tensor(qr, qu, 0.5, qr,
                                                   ALU.is_gt, ALU.mult),
                  wait=[(SV, a8)], inc=SV)
        aD1 = S.v(lambda e: e.tensor_scalar(qu, qu, 1.5, 0.5, ALU.is_gt, ALU.subtract),
                  wait=[(SV, a8m)], inc=SV)
        aD2 = S.v(lambda e: e.scalar_tensor_tensor(qu, qr, 1.5, qu,
                                                   ALU.is_equal, ALU.mult),
                  wait=[(SV, aD1)], inc=SV)
        a8 = S.v(lambda e: e.tensor_add(qr, qr, qu), wait=[(SV, aD2)], inc=SV)
        if dbg_tag and _os.environ.get("L1F_DBG") == dbg_tag:
            DQ2 = S.sem("dbq2")
            S.g(lambda e, d=dram["dbg"][:, 5632:5632 + nf], s=qr:
                e.dma_start(out=d, in_=s), wait=[(SV, a8)], inc=DQ2, incn=16)
            S.waitonly("vector", [(DQ2, DQ2.count)])
        aA = S.s(lambda e: e.activation(qu, xin, AF.Sign),
                 wait=[(SV, a8), (SS, SS.count)], inc=SS)
        aB = S.v(lambda e: e.tensor_mul(qr, qr, qu), wait=[(SS, aA)], inc=SV)
        aC = S.v(lambda e, bb=tbuf.rearrange("p (a b) -> p a b", b=1)
                 .broadcast_to([128, nb, 16]):
                 e.tensor_tensor(xout.rearrange("p (a b) -> p a b", b=16),
                                 qr.rearrange("p (a b) -> p a b", b=16),
                                 bb, ALU.mult),
                 wait=[(SV, aB)], inc=SV)
        return aC

    WL = S.sem("f_wl")
    CSQ = S.sem("f_csq")
    HSQ = S.sem("f_hsq")
    TFQ = S.sem("f_tfq")
    psc = Scope(st.arena)
    wstg2 = [psc.f32(2816), psc.f32(2816)]
    wbf2 = [psc.bf16(2816), psc.bf16(2816)]
    tstg2 = [psc.bf16(22 * 128), psc.bf16(22 * 128)]
    qu = sc.f32(2816)
    qr = sc.f32(2816)
    qbm = sc.f32(2816 // 16)
    qt = sc.f32(2816 // 16)
    ps_tp = st.tp_bank[:, 0:128]
    ps_tp2 = st.banks[6][:].bitcast(BF16)[:, 0:128]
    WL2 = [S.sem("f_wl0"), S.sem("f_wl1")]
    HSQ2 = [S.sem("f_hsq0"), S.sem("f_hsq1")]
    hist = [dict(), dict()]
    ri = 0
    for wnm, dnm, rows, cols in (("l1_wgate", "g", 22, 1024),
                                 ("l1_wup", "u", 22, 1024),
                                 ("l1_wdown", "d", 8, 2816)):
        npc = cols // 128
        for rc in range(rows):
            sl_ = ri % 2
            wstg, wbf, tstg = wstg2[sl_], wbf2[sl_], tstg2[sl_]
            h = hist[sl_]
            lw = [st.tile_tok[nt - 1]] if ri == 0 else []
            if "xin" in h:
                lw.append(h["xin"])
            S.sy(lambda e, d=wstg[:, 0:cols],
                 s=dram[wnm][rc * 128:(rc + 1) * 128, :]:
                 e.dma_start(out=d, in_=s), wait=lw, inc=WL2[sl_], incn=16)
            qtok = emit_quant2(wstg[:, 0:cols], wbf[:, 0:cols], cols,
                               qu[:, 0:cols], qr[:, 0:cols],
                               qbm[:, 0:cols // 16],
                               qt[:, 0:cols // 16],
                               wait0=[(WL2[sl_], WL2[sl_].count)] +
                                     ([h["tp"]] if "tp" in h else []))
            h["xin"] = (SS, SS.count)
            cp_toks = []
            for pc in range(npc):
                tb = ps_tp if pc % 2 == 0 else ps_tp2
                w = [(SV, qtok)]
                if pc == 0 and "store" in h:
                    w.append(h["store"])
                if pc in (0, 1):
                    w.append((TFQ, TFQ.count))
                else:
                    w.append((TFQ, cp_toks[pc - 2]))
                tk = S.t(lambda e, sl=wbf[:, pc * 128:(pc + 1) * 128], d=tb:
                         e.transpose(d, sl, st.ident[:]),
                         wait=w, inc=TN)
                ck = S.v(lambda e, d=tstg[:, pc * 128:(pc + 1) * 128], s=tb:
                         e.tensor_copy(d, s), wait=[(TN, tk)], inc=TFQ)
                cp_toks.append(ck)
            st_tok = TFQ.count
            h["tp"] = (TN, TN.count)
            S.sy(lambda e, d=cache[dnm].rearrange("(c p) o -> p c o", p=128)
                 [:, :, rc * 128:(rc + 1) * 128],
                 s=tstg[:, 0:npc * 128].rearrange("p (c o) -> p c o", c=npc):
                 e.dma_start(out=d, in_=s),
                 wait=[(TFQ, st_tok)], inc=HSQ2[sl_], incn=16)
            h["store"] = (HSQ2[sl_], HSQ2[sl_].count)
            ri += 1
    cache_tok = None
    cache_rdy = [(HSQ2[0], HSQ2[0].count), (HSQ2[1], HSQ2[1].count)]
    S.waitonly("sync", cache_rdy)
    import os as _os
    if _os.environ.get("L1F_DBG") == "wg":
        DW = S.sem("dbgw")
        S.g(lambda e, d=dram["dbg"][:, 0:2816],
            s=cache["g"][0:128, :]: e.dma_start(out=d, in_=s),
            wait=cache_rdy, inc=DW, incn=16)
        S.g(lambda e, d=dram["dbg"][:, 2816:5632],
            s=cache["g"].rearrange("(c p) o -> p c o", p=128)[:, 1, :]:
            e.dma_start(out=d, in_=s), inc=DW, incn=16)
        S.waitonly("sync", [(DW, DW.count)])
    psc.close()

    xq = sc.bf16(1024)
    xqfm = sc.bf16(4 * 1024)
    hbuf = sc.f32(4 * 2816)
    hqfm4 = sc.bf16(4 * 2816)
    rres = sc.f32(1024)
    wg1 = sc.bf16(8 * 512)
    wu1 = sc.bf16(8 * 512)
    wd1 = sc.bf16(22 * 256)
    t1sb = st.ldbuf[:, 0:512]
    ln_sq = st.ldbuf[:, 1024:2048]
    ln_st = sc.f32(8)
    outf = rres if final_out else None

    WG = [S.sem("f_wg0"), S.sem("f_wg1")]
    WD = [S.sem("f_wd0"), S.sem("f_wd1")]
    qf = S.sem("f_qf")
    tf = S.sem("f_tf")
    hf = S.sem("f_hf")
    OF = S.sem("f_of")

    ps_g = B[1]
    ps_u = B[2]
    ps_d2 = [B[3], B[4]]

    NSUP = nt // 4
    ocg_sizes = [512, 512, 512, 512, 512, 256]
    wd_last = None
    hq_war = {tt: (TN, 0) for tt in range(4)}
    ev_hist = []
    xq_war = (TN, 0)
    for sp in range(NSUP):
        t0 = sp * 4
        for tt in range(4):
            gt = t0 + tt
            qtok = emit_quant2(st.stream[:, gt * D:(gt + 1) * D], xq[:],
                               D, qu[:, 0:D], qr[:, 0:D],
                               qbm[:, 0:64], qt[:, 0:64],
                               wait0=[st.tile_tok[gt], xq_war] + (cache_rdy if sp == 0 else []),
                               dbg_tag=f"xq{sp}_{tt}")
            import os as _os
            if _os.environ.get("L1F_DBG") == f"xq{sp}_{tt}":
                DBX = S.sem(f"dbgx{sp}{tt}")
                S.g(lambda e, d=dram["dbg"][:, 0:1024], s=xq[:]:
                    e.dma_start(out=d, in_=s), wait=[(SV, qtok)], inc=DBX, incn=16)
                S.g(lambda e, d=dram["dbg"][:, 1024:2048], s=qu[:, 0:1024]:
                    e.dma_start(out=d, in_=s), wait=[(SV, qtok)], inc=DBX, incn=16)
                S.g(lambda e, d=dram["dbg"][:, 2048:3072], s=qr[:, 0:1024]:
                    e.dma_start(out=d, in_=s), wait=[(SV, qtok)], inc=DBX, incn=16)
                S.g(lambda e, d=dram["dbg"][:, 3072:3136], s=qbm[:, 0:64]:
                    e.dma_start(out=d, in_=s), wait=[(SV, qtok)], inc=DBX, incn=16)
                S.g(lambda e, d=dram["dbg"][:, 3136:3200], s=qt[:, 0:64]:
                    e.dma_start(out=d, in_=s), wait=[(SV, qtok)], inc=DBX, incn=16)
                S.waitonly("vector", [(DBX, DBX.count)])
                S.waitonly("tensor", [(DBX, DBX.count)])
            cpx = []
            for c in range(8):
                tb = ps_tp if c % 2 == 0 else ps_tp2
                w = [(SV, qtok)]
                if c in (0, 1):
                    w.append((tf, tf.count))
                else:
                    w.append((tf, cpx[c - 2]))
                tk = S.t(lambda e, sl=xq[:, c * 128:(c + 1) * 128], d=tb:
                         e.transpose(d, sl, st.ident[:]), wait=w, inc=TN)
                ck = S.v(lambda e, d=xqfm[:, tt * D + c * 128: tt * D + (c + 1) * 128], s=tb:
                         e.tensor_copy(d, s), wait=[(TN, tk)], inc=tf)
                cpx.append(ck)
            xq_war = (TN, TN.count)
        xq_tok = tf.count

        off = 0
        for ocg, osz in enumerate(ocg_sizes):
            sel = 0
            wgb = wg1
            wub = wu1
            S.sy(lambda e, d=wgb[:, 0:8 * osz].rearrange("p (c o) -> p c o", c=8),
                 s=cache["g"].rearrange("(c p) o -> p c o", p=128)[:, :, off:off + osz]:
                 e.dma_start(out=d, in_=s),
                 wait=(cache_rdy if sp == 0 and ocg == 0 else []) + [(TN, TN.count)], inc=WG[0], incn=16)
            S.sy(lambda e, d=wub[:, 0:8 * osz].rearrange("p (c o) -> p c o", c=8),
                 s=cache["u"].rearrange("(c p) o -> p c o", p=128)[:, :, off:off + osz]:
                 e.dma_start(out=d, in_=s),
                 wait=[(TN, TN.count)], inc=WG[0], incn=16)
            wrdy = WG[0].count
            for tt in range(4):
                for which, wb in ((0, wgb), (1, wub)):
                    psb = ps_g if which == 0 else ps_u
                    for ic in range(8):
                        w = []
                        if ic == 0:
                            w = [(hf, hf.count)]
                            if tt == 0 and which == 0:
                                w += [(WG[0], wrdy), (tf, xq_tok)]
                        S.t(lambda e, p2=psb[:, 0:osz],
                            lhs=xqfm[:, tt * D + ic * 128: tt * D + (ic + 1) * 128],
                            wsl=wb[:, ic * osz:(ic + 1) * osz],
                            s0=(ic == 0), s1=(ic == 7):
                            e.matmul(p2, lhs, wsl, start=s0, stop=s1),
                            wait=w, inc=(TN if ic == 7 else None))
                tkgu = TN.count
                sg = S.s(lambda e, p2=ps_g[:, 0:osz], d=t1sb[:, 0:osz]:
                         e.activation(d, p2, AF.Silu),
                         wait=[(TN, tkgu), (SS, SS.count)], inc=SS)
                S.v(lambda e, p2=ps_u[:, 0:osz],
                    d=hbuf[:, tt * 2816 + off: tt * 2816 + off + osz], s=t1sb[:, 0:osz]:
                    e.tensor_mul(d, s, p2),
                    wait=[(SS, sg), (SV, SV.count)], inc=hf)
            off += osz
        h_tok = hf.count

        hq_rdy = []
        for tt in range(4):
            if _os.environ.get("L1F_DBG") == f"h{sp}_{tt}":
                DBH2 = S.sem(f"dbgh2{sp}{tt}")
                S.g(lambda e, d=dram["dbg"][:, 0:2816], s=hbuf[:, tt * 2816:(tt + 1) * 2816]:
                    e.dma_start(out=d, in_=s), wait=[(hf, h_tok)], inc=DBH2, incn=16)
                S.waitonly("vector", [(DBH2, DBH2.count)])
                S.waitonly("tensor", [(DBH2, DBH2.count)])
            hqv = hbuf[:, tt * 2816:(tt + 1) * 2816].bitcast(BF16)[:, 0:2816]
            qtok = emit_quant2(hbuf[:, tt * 2816:(tt + 1) * 2816],
                               hqv, 2816,
                               qu[:, 0:2816], qr[:, 0:2816],
                               qbm[:, 0:176], qt[:, 0:176],
                               wait0=[(hf, h_tok)])
            if _os.environ.get("L1F_DBG") == f"hq{sp}_{tt}":
                DBH = S.sem(f"dbgh{sp}{tt}")
                S.g(lambda e, d=dram["dbg"][:, 0:2816], s=hqv:
                    e.dma_start(out=d, in_=s), wait=[(SV, qtok)], inc=DBH, incn=16)
                S.waitonly("vector", [(DBH, DBH.count)])
                S.waitonly("tensor", [(DBH, DBH.count)])
            cp2 = []
            for c in range(22):
                tb = ps_tp if c % 2 == 0 else ps_tp2
                w = [(SV, qtok)]
                if c in (0, 1):
                    w.append((tf, tf.count))
                else:
                    w.append((tf, cp2[c - 2]))
                if c == 0:
                    w.append(hq_war[tt])
                tk = S.t(lambda e, sl=hqv[:, c * 128:(c + 1) * 128], d=tb:
                         e.transpose(d, sl, st.ident[:]), wait=w, inc=TN)
                ck = S.v(lambda e, d=hqfm4[:, tt * 2816 + c * 128:
                                          tt * 2816 + (c + 1) * 128], s=tb:
                         e.tensor_copy(d, s), wait=[(TN, tk)], inc=tf)
                cp2.append(ck)
            hq_rdy.append((tf, tf.count))

        for oc2 in range(4):
            lw = list(cache_rdy) if sp == 0 and oc2 == 0 else []
            if wd_last is not None:
                lw.append(wd_last)
            S.sy(lambda e, d=wd1[:].rearrange("p (c o) -> p c o", c=22),
                 s=cache["d"].rearrange("(c p) o -> p c o", p=128)
                 [:, :, oc2 * 256:(oc2 + 1) * 256]:
                 e.dma_start(out=d, in_=s), wait=lw, inc=WD[0], incn=16)
            wd_tok = WD[0].count
            for tt in range(4):
                gi = oc2 * 4 + tt
                psb = ps_d2[gi % 2]
                for ic in range(22):
                    w = []
                    if ic == 0:
                        w = [(WD[0], wd_tok), hq_rdy[tt]]
                        if len(ev_hist) >= 2:
                            w.append(ev_hist[-2])
                        if oc2 == 0 and tt == 0:
                            w.append((OF, OF.count))
                    S.t(lambda e, p2=psb[:, 0:256],
                        lhs=hqfm4[:, tt * 2816 + ic * 128: tt * 2816 + (ic + 1) * 128],
                        wsl=wd1[:, ic * 256:(ic + 1) * 256],
                        s0=(ic == 0), s1=(ic == 21):
                        e.matmul(p2, lhs, wsl, start=s0, stop=s1),
                        wait=w, inc=(TN if ic == 21 else None))
                tkd = TN.count
                if tt == 3:
                    wd_last = (TN, tkd)
                ev = S.v(lambda e, p2=psb[:, 0:256],
                         d=hbuf[:, tt * 2816 + 1408 + oc2 * 256: tt * 2816 + 1408 + (oc2 + 1) * 256]:
                         e.tensor_copy(d, p2), wait=[(TN, tkd), (SV, SV.count)], inc=OF)
                ev_hist.append((OF, ev))
        for tt in range(4):
            hq_war[tt] = wd_last
        dn_ev = OF.count

        for tt in range(4):
            gt = t0 + tt
            wprev = [st.tile_tok[gt - 1]] if gt > 0 else []
            rv1 = S.v(lambda e, d=rres[:, 0:1024],
                      p2=hbuf[:, tt * 2816 + 1408: tt * 2816 + 2432],
                      s=st.stream[:, gt * D: (gt + 1) * D]:
                      e.tensor_add(d, p2, s),
                      wait=[(OF, dn_ev), (SV, SV.count)] + wprev, inc=OF)
            res_tok = OF.count
            if final_out:
                st.tile_tok[gt] = emit_ln(nc, es, S, st, rres, ln_sq, ln_st,
                                          outf[:], wait=[(OF, res_tok)])
                S.sy(lambda e, d=dram["out"][gt * 128:(gt + 1) * 128, :], s=outf[:]:
                     e.dma_start(out=d, in_=s),
                     wait=[st.tile_tok[gt]], inc=st.ST, incn=16)
                S.waitonly("vector", [(st.ST, st.ST.count)])
            else:
                st.tile_tok[gt] = emit_ln(nc, es, S, st, rres, ln_sq, ln_st,
                                          st.stream[:, gt * D:(gt + 1) * D],
                                          wait=[(OF, res_tok)])
    sc.close()



def kernel(**inputs):
    x = np.ascontiguousarray(np.asarray(inputs["L0_attn_query"], np.float32))
    arrays = host_prep(inputs)
    nc, need = build_program(arrays, ("l0a", "l0f", "l1a", "l1f"), nt=32,
                             stream_in="x", final_out=True)
    in_maps = []
    for b in range(NCORES):
        m = {"x": np.ascontiguousarray(x[b])}
        for nm in need:
            m[nm] = arrays[nm]
        in_maps.append(m)
    res = run_bass_kernel_spmd(nc, in_maps, core_ids=list(range(NCORES)))
    return np.stack([r["out"] for r in res.results], axis=0)
